# version 05
# baseline (speedup 1.0000x reference)
import contextlib
import numpy as np
import ml_dtypes
import concourse.bass as bass
import concourse.mybir as mybir
from concourse.bass_utils import run_bass_kernel_spmd

F32 = mybir.dt.float32
BF16 = mybir.dt.bfloat16
AF = mybir.ActivationFunctionType
ALU = mybir.AluOpType
AX = mybir.AxisListType

D = 2048
T = 2048
NDC = D // 128
NH = 16
HD = 128
EPS = 1e-5
SCL = float(1.0 / np.sqrt(128.0))
SHIFT = 25.0
IE = 1408
NIC = IE // 128
ISH = 5632
TG = 1024
TG_PASSES = 2
E_MOE = 16
TOPK = 4
NGT = T // 512


class Sem:
    __slots__ = ("h", "v", "name")

    def __init__(self, h, name):
        self.h = h
        self.v = 0
        self.name = name


class Ctx:
    def __init__(self, nc, sem_es):
        self.nc = nc
        self.sem_es = sem_es
        self.steps = []
        self.sems = []
        self._semmap = {}
        self._owner = {}

    def sem(self, name):
        if name in self._semmap:
            return self._semmap[name]
        h = self.sem_es.enter_context(self.nc.semaphore(name))
        s = Sem(h, name)
        self.sems.append(s)
        self._semmap[name] = s
        return s

    def add(self, eng, fn, waits=(), incs=()):
        for (s, n, va) in incs:
            owner = self._owner.setdefault(s, eng)
            if owner != eng:
                self._owner[s] = None
        self.steps.append((eng, fn, list(waits), list(incs)))

    def emit(self):
        steps = self.steps
        with self.nc.Block() as blk:
            for eng_name, reg in (("sync", blk.sync), ("gpsimd", blk.gpsimd),
                                  ("tensor", blk.tensor), ("vector", blk.vector),
                                  ("scalar", blk.scalar)):
                def mk(nm):
                    def f(eng):
                        chain = None
                        for (e, fn, waits, incs) in steps:
                            if e != nm:
                                continue
                            assert len(incs) <= 1, (nm, incs)
                            for (s, val) in waits:
                                if val > 0:
                                    eng.wait_ge(s.h, val)
                            if nm in ("vector", "scalar") and chain is not None:
                                eng.wait_ge(chain[0].h, chain[1])
                            inst = fn(eng)
                            for (s, n, va) in incs:
                                inst.then_inc(s.h, n)
                                if self._owner.get(s) == nm:
                                    chain = (s, va)
                    return f
                reg(mk(eng_name))


def dstep(ctx, eng, fn, waits, sem):
    sem.v += 16
    ctx.add(eng, fn, waits, [(sem, 16, sem.v)])


def cstep(ctx, eng, fn, waits, *sems):
    merged = {}
    order = []
    for s in sems:
        s.v += 1
        if s not in merged:
            merged[s] = 0
            order.append(s)
        merged[s] += 1
    ctx.add(eng, fn, waits, [(s, merged[s], s.v) for s in order])


def sbuf(es, nc, name, shape, dtype):
    return es.enter_context(nc.sbuf_tensor(name, shape, dtype)).ap()


def psum(es, nc, name, shape, dtype=F32):
    return es.enter_context(nc.psum_tensor(name, shape, dtype)).ap()


_uid = [0]


def uname(base):
    _uid[0] += 1
    return "%s_%d" % (base, _uid[0])


def emit_ln(ctx, C, x_src, xb, t0, tn, quantize=False, pre=(), ext=None):
    nc = ctx.nc
    dL = ctx.sem("ln_dL")
    sQ = ctx.sem("ln_sQ")
    tS = ctx.sem("ln_tS")
    vS = ctx.sem("ln_vS")
    tB = ctx.sem("ln_tB")
    sB = ctx.sem("ln_sB")
    vN = ctx.sem("ln_vN")
    dL0, sQ0, tS0, vS0, tB0, sB0, vN0 = dL.v, sQ.v, tS.v, vS.v, tB.v, sB.v, vN.v

    NG = tn // 512
    pre = list(pre)
    with contextlib.ExitStack() as les:
        sqws = sbuf(les, nc, uname("ln_sq"), [128, tn], BF16)
        rows = sbuf(les, nc, uname("ln_rows"), [128, 2 * tn], F32)
        seg_m = rows[0:1, 0:tn]
        seg_r = rows[0:1, tn:2 * tn]
        rstd_b = sbuf(les, nc, uname("ln_rb"), [128, tn], BF16)
        mrstd_b = sbuf(les, nc, uname("ln_mb"), [128, tn], BF16)
        if quantize:
            nws = sbuf(les, nc, uname("ln_nws"), [128, tn], F32)
            qws = sbuf(les, nc, uname("ln_qws"), [128, 2 * tn], F32)

        stats_scope = contextlib.ExitStack()
        if ext is None:
            psM = psum(stats_scope, nc, uname("ln_psM"), [1, tn])
            psS = psum(stats_scope, nc, uname("ln_psS"), [1, tn])
        else:
            psM, psS = ext["psM"], ext["psS"]

        for dc in range(NDC):
            cw = pre if dc == 0 else []
            dstep(ctx, "gpsimd",
                  (lambda dc: lambda g: g.dma_start(
                      xb[:, dc * tn:(dc + 1) * tn],
                      x_src[dc * 128:(dc + 1) * 128, t0:t0 + tn]))(dc),
                  cw, dL)
        dL_all = dL0 + 16 * NDC
        for dc in range(NDC):
            w = [(dL, dL_all)]
            if dc == 0:
                w += pre
            if dc >= 1:
                w.append((tS, tS0 + 2 * dc))
            cstep(ctx, "scalar",
                  (lambda dc: lambda s: s.activation(
                      sqws[:, :], xb[:, dc * tn:(dc + 1) * tn], AF.Square))(dc),
                  w, sQ)

            def mmm(dc):
                def f(t):
                    last = None
                    for g in range(NG):
                        last = t.matmul(psM[0:1, g * 512:(g + 1) * 512],
                                        C["ones_bcol"][:, 0:1],
                                        xb[:, dc * tn + g * 512: dc * tn + (g + 1) * 512],
                                        start=(dc == 0), stop=(dc == NDC - 1))
                    return last
                return f

            def smm(dc):
                def f(t):
                    last = None
                    for g in range(NG):
                        last = t.matmul(psS[0:1, g * 512:(g + 1) * 512],
                                        C["ones_bcol"][:, 0:1],
                                        sqws[:, g * 512:(g + 1) * 512],
                                        start=(dc == 0), stop=(dc == NDC - 1))
                    return last
                return f
            w = [(dL, dL_all)]
            if dc == 0:
                w += pre
            cstep(ctx, "tensor", mmm(dc), w, tS)
            cstep(ctx, "tensor", smm(dc), [(sQ, sQ0 + dc + 1)], tS)
        p1 = psM[0:1, 0:tn]
        w = [(tS, tS.v)] + pre
        cstep(ctx, "vector", lambda v: v.tensor_scalar(
            seg_m, psM[0:1, :], 1.0 / D, None, ALU.mult), w, vS)
        cstep(ctx, "vector", lambda v: v.tensor_scalar(
            seg_r, psS[0:1, :], 1.0 / D, None, ALU.mult), [], vS)
        cstep(ctx, "vector", lambda v: v.tensor_tensor(
            p1, seg_m, seg_m, ALU.mult), [], vS)
        cstep(ctx, "vector", lambda v: v.tensor_tensor(
            seg_r, seg_r, p1, ALU.subtract), [], vS)
        cstep(ctx, "vector", lambda v: v.tensor_scalar(
            seg_r, seg_r, float(EPS), None, ALU.add), [], vS)
        cstep(ctx, "scalar", lambda s: s.activation(
            p1, seg_r, AF.Sqrt), [(vS, vS.v)], sQ)
        cstep(ctx, "vector", lambda v: v.reciprocal(seg_r, p1),
              [(sQ, sQ.v)], vS)
        cstep(ctx, "vector", lambda v: v.tensor_tensor(
            seg_m, seg_m, seg_r, ALU.mult), [], vS)
        stats_done = vS.v
        stats_scope.close()

        bc_scope = contextlib.ExitStack()
        if ext is None:
            psBc = psum(bc_scope, nc, uname("ln_psBc"), [128, tn])
        else:
            psBc = ext["psBc"]

        def bc(row):
            def f(t):
                last = None
                for g in range(NG):
                    last = t.matmul(psBc[:, g * 512:(g + 1) * 512],
                                    C["ones_f32"][0:1, 0:128],
                                    row[0:1, g * 512:(g + 1) * 512],
                                    start=True, stop=True)
                return last
            return f
        cstep(ctx, "tensor", bc(seg_r), [(vS, stats_done)], tB)
        cstep(ctx, "scalar", lambda s: s.copy(rstd_b[:, :], psBc[:, :]),
              [(tB, tB0 + 1)], sB)
        cstep(ctx, "tensor", bc(seg_m), [(sB, sB0 + 1)], tB)
        cstep(ctx, "scalar", lambda s: s.copy(mrstd_b[:, :], psBc[:, :]),
              [(tB, tB0 + 2)], sB)
        bc_scope.close()
        for dc in range(NDC):
            w = [(sB, sB0 + 2)] if dc == 0 else []
            if not quantize:
                cstep(ctx, "vector",
                      (lambda dc: lambda v: v.tensor_tensor(
                          xb[:, dc * tn:(dc + 1) * tn],
                          xb[:, dc * tn:(dc + 1) * tn], rstd_b[:, :],
                          ALU.mult))(dc),
                      w, vN)
                cstep(ctx, "vector",
                      (lambda dc: lambda v: v.tensor_tensor(
                          xb[:, dc * tn:(dc + 1) * tn],
                          xb[:, dc * tn:(dc + 1) * tn], mrstd_b[:, :],
                          ALU.subtract))(dc),
                      [], vN)
            else:
                cstep(ctx, "vector",
                      (lambda dc: lambda v: v.tensor_tensor(
                          nws[:, :], xb[:, dc * tn:(dc + 1) * tn], rstd_b[:, :],
                          ALU.mult))(dc),
                      w, vN)
                cstep(ctx, "vector",
                      (lambda dc: lambda v: v.tensor_tensor(
                          nws[:, :], nws[:, :], mrstd_b[:, :], ALU.subtract))(dc),
                      [], vN)
                cstep(ctx, "vector",
                      (lambda dc: lambda v: v.tensor_scalar(
                          qws[:, 0:tn], nws[:, :], 0.33, 0.5, ALU.is_gt, ALU.mult))(dc),
                      [], vN)
                cstep(ctx, "vector",
                      (lambda dc: lambda v: v.tensor_scalar(
                          qws[:, tn:2 * tn], nws[:, :], -0.33, 0.5, ALU.is_lt,
                          ALU.mult))(dc),
                      [], vN)
                cstep(ctx, "vector",
                      (lambda dc: lambda v: v.tensor_tensor(
                          xb[:, dc * tn:(dc + 1) * tn], qws[:, 0:tn],
                          qws[:, tn:2 * tn], ALU.subtract))(dc),
                      [], vN)
    return (vN, vN.v)


def emit_residual_out(ctx, es, groups, mk_matmuls, extra_first_waits=()):
    nc = ctx.nc
    pe = ctx.sem("o_pe")
    ev = ctx.sem("o_ev")
    dx = [ctx.sem("o_dx%d" % i) for i in range(4)]
    ds = [ctx.sem("o_ds%d" % i) for i in range(4)]
    pe0, ev0 = pe.v, ev.v
    dx0 = [s.v for s in dx]
    ds0 = [s.v for s in ds]
    xt = sbuf(es, nc, uname("ro_xt"), [128, 4 * 512], F32)
    ot = sbuf(es, nc, uname("ro_ot"), [128, 4 * 512], F32)
    for g, grp in enumerate(groups):
        xi = g % 4
        w = [(ev, ev0 + g - 3)]
        if g < 4:
            w += list(extra_first_waits)
        dstep(ctx, "sync",
              (lambda grp, xi: lambda s: s.dma_start(
                  xt[:, xi * 512:(xi + 1) * 512], grp["x_src_ap"]))(grp, xi),
              w, dx[xi])
        w = [(ev, ev0 + g - 1)]
        if g < 2:
            w += list(extra_first_waits)
        cstep(ctx, "tensor", (lambda g: lambda t: mk_matmuls(g, t))(g), w, pe)
        w = [(pe, pe0 + g + 1), (dx[xi], dx0[xi] + 16 * (g // 4 + 1)),
             (ds[xi], ds0[xi] + 16 * (g // 4))]
        cstep(ctx, "vector",
              (lambda grp, xi: lambda v: v.tensor_tensor(
                  ot[:, xi * 512:(xi + 1) * 512], grp["ps"],
                  xt[:, xi * 512:(xi + 1) * 512], ALU.add))(grp, xi),
              w, ev)
        dstep(ctx, "sync",
              (lambda grp, xi: lambda s: s.dma_start(
                  grp["x_dst_ap"], ot[:, xi * 512:(xi + 1) * 512]))(grp, xi),
              [(ev, ev0 + g + 1)], ds[xi])
    return pe


def phase_join(ctx):
    join = ctx.sem("join")
    for eng in ("sync", "gpsimd", "tensor", "vector", "scalar"):
        waits = []
        if eng == "sync":
            for s in ctx.sems:
                if s is not join and s.v > 0:
                    waits.append((s, s.v))
        cstep(ctx, eng, lambda e: e.nop(), waits, join)
    target = join.v
    for eng in ("sync", "gpsimd", "tensor", "vector", "scalar"):
        ctx.add(eng, lambda e: e.nop(), [(join, target)], [])


def emit_attn(ctx, C, layer, x_src, x_dst, W):
    nc = ctx.nc
    causal = (layer == 0)
    with contextlib.ExitStack() as pes:
        xb = sbuf(pes, nc, uname("at_xn"), [128, NDC * T], BF16)
        zall = sbuf(pes, nc, uname("at_z"), [128, NH * T], BF16)
        q_sb = sbuf(pes, nc, uname("at_q"), [128, T], BF16)
        k_sb = sbuf(pes, nc, uname("at_k"), [128, T], BF16)
        vf_sb = sbuf(pes, nc, uname("at_vf"), [128, T], BF16)
        vt_sb = sbuf(pes, nc, uname("at_vt"), [128, T], BF16)
        wst = sbuf(pes, nc, uname("at_wst"), [128, 2 * 3 * T], BF16)
        dIc = ctx.sem("at_dIc")
        if layer == 0:
            mk = sbuf(pes, nc, uname("at_mask"), [128, 4 * 512], BF16)
            maskT = [mk[:, v * 512:(v + 1) * 512] for v in range(4)]
            for v in range(4):
                dstep(ctx, "sync",
                      (lambda v: lambda s: s.dma_start(
                          mk[:, v * 512:(v + 1) * 512], C["maskT_dram"][v]))(v),
                      [], dIc)
        else:
            cs = sbuf(pes, nc, uname("at_rope"), [64, 2 * T], BF16)
            ropeA = cs[:, 0:T]
            ropeB = cs[:, T:2 * T]
            dstep(ctx, "sync", lambda s: s.dma_start(cs[:, 0:T], C["rope_dram"][0]),
                  [], dIc)
            dstep(ctx, "sync", lambda s: s.dma_start(cs[:, T:2 * T], C["rope_dram"][1]),
                  [], dIc)
        dIc_done = dIc.v

        ln_done = emit_ln(ctx, C, x_src, xb, 0, T, quantize=False)

        dWp = [ctx.sem("at_dW0"), ctx.sem("at_dW1")]
        dB = ctx.sem("at_dB")
        tP = ctx.sem("at_tP")
        sE = ctx.sem("at_sE")
        vR = ctx.sem("at_vR")
        tT = ctx.sem("at_tT")
        tQ = ctx.sem("at_tQ")
        sX = ctx.sem("at_sX")
        vM = ctx.sem("at_vM")
        tA = ctx.sem("at_tA")
        tZ = ctx.sem("at_tZ")
        vL = ctx.sem("at_vL")
        vZ = ctx.sem("at_vZ")
        dWp0 = [s.v for s in dWp]
        dB0, tP0 = dB.v, tP.v
        evac_done_at = []

        psP = psum(pes, nc, uname("at_psP"), [128, 2048])
        psB2 = psum(pes, nc, uname("at_psB2"), [128, 1024])
        psT = psB2[:, 0:512].bitcast(BF16)
        psS = psum(pes, nc, uname("at_psS"), [128, 1024])

        def proj_bank(i):
            j = i % 6
            if j < 4:
                return psP[:, j * 512:(j + 1) * 512]
            return psB2[:, (j - 4) * 512:(j - 3) * 512]

        def z_ps(g):
            return psP[:, (g % 2) * 512:((g % 2) + 1) * 512]

        def l_ps(g):
            return psP[0:1, 1024 + (g % 2) * 512: 1024 + ((g % 2) + 1) * 512]

        def rl_ps(g):
            return psB2[:, (g % 2) * 512:((g % 2) + 1) * 512]

        with contextlib.ExitStack() as hes:
            att = sbuf(hes, nc, uname("at_att"), [128, 2 * 512], BF16)
            if layer == 0:
                bias_sb = sbuf(hes, nc, uname("at_bias"), [1, 3 * 128], F32)
            lrow = sbuf(hes, nc, uname("at_lr"), [1, 512], F32)
            rl_b = sbuf(hes, nc, uname("at_rlb"), [128, 512], F32)
            if layer == 1:
                rws = sbuf(hes, nc, uname("at_rws"), [128, T], F32)
                rw2 = sbuf(hes, nc, uname("at_rw2"), [32, 2048], F32)

            nb = 0
            for h in range(NH):
                wbuf = h % 2
                woff = wbuf * 3 * T
                w = [] if h < 2 else [(tP, tP0 + 12 * (h - 1))]
                for j, name in enumerate(("wq", "wk", "wv")):
                    src = W[name + "_slab"][h]
                    dstep(ctx, "gpsimd",
                          (lambda src, woff, j: lambda g: g.dma_start(
                              wst[:, woff + j * T: woff + (j + 1) * T]
                              .rearrange("p (dc e) -> p dc e", dc=NDC), src))(src, woff, j),
                          w if j == 0 else [], dWp[h % 2])
                if layer == 0:
                    wb = [(tP, tP0 + 12 * h)]
                    if h == 0:
                        wb.append(ln_done)
                    dstep(ctx, "sync",
                          (lambda h: lambda s: s.dma_start(
                              bias_sb[0:1, :], W["bqkv"][h]))(h),
                          wb, dB)

                for j, dst in enumerate((q_sb, k_sb, vf_sb)):
                    for g in range(4):
                        ps = proj_bank(nb)
                        wts = [(dWp[h % 2], dWp0[h % 2] + 48 * (h // 2 + 1))]
                        if nb >= 6:
                            wts.append((sE, evac_done_at[nb - 6]))
                        if j == 0 and g == 0 and h > 0:
                            wts.append((vZ, vZ.v))
                            wts.append((vL, vL.v))
                        if h == 0 and j == 0 and g == 0:
                            wts.append(ln_done)
                        if layer == 0:
                            wts.append((dB, dB0 + 16 * (h + 1)))

                        def mkp(ps, woff, j, g):
                            def f(t):
                                last = None
                                for dc in range(NDC):
                                    last = t.matmul(
                                        ps,
                                        wst[:, woff + j * T + dc * 128:
                                            woff + j * T + (dc + 1) * 128],
                                        xb[:, dc * T + g * 512: dc * T + (g + 1) * 512],
                                        start=(dc == 0),
                                        stop=(dc == NDC - 1 and layer == 1))
                                if layer == 0:
                                    last = t.matmul(
                                        ps, bias_sb[0:1, j * 128:(j + 1) * 128],
                                        C["ones_f32"][0:1, 0:512],
                                        start=False, stop=True)
                                return last
                            return f
                        cstep(ctx, "tensor", mkp(ps, woff, j, g), wts, tP)
                        if layer == 1 and j < 2:
                            w2_ = [(tP, tP.v)]
                            if g == 0:
                                w2_.append((vR, vR.v))
                            cstep(ctx, "scalar",
                                  (lambda ps, g: lambda s: s.copy(
                                      rws[0:64, g * 512:(g + 1) * 512],
                                      ps[0:64, :]))(ps, g),
                                  w2_, sE)
                            cstep(ctx, "scalar",
                                  (lambda ps, dst, g: lambda s: s.copy(
                                      dst[64:128, g * 512:(g + 1) * 512],
                                      ps[64:128, :]))(ps, dst, g),
                                  [], sE)
                        else:
                            cstep(ctx, "scalar",
                                  (lambda ps, dst, g: lambda s: s.copy(
                                      dst[:, g * 512:(g + 1) * 512], ps[:, :]))(ps, dst, g),
                                  [(tP, tP.v)], sE)
                        evac_done_at.append(sE.v)
                        nb += 1
                    if layer == 1 and j < 2:
                        s0a = rw2[0:32, 0:1024]
                        s0b = rw2[0:32, 1024:2048]
                        for half in range(2):
                            sl = slice(half * 1024, (half + 1) * 1024)
                            w2_ = [(sE, sE.v), (dIc, dIc_done)] if half == 0 else []
                            cstep(ctx, "vector",
                                  (lambda sl: lambda v: v.tensor_tensor(
                                      s0a, rws[0:32, sl], ropeA[0:32, sl],
                                      ALU.mult))(sl),
                                  w2_, vR)
                            cstep(ctx, "vector",
                                  (lambda sl: lambda v: v.tensor_tensor(
                                      s0b, rws[32:64, sl], ropeA[32:64, sl],
                                      ALU.mult))(sl),
                                  [], vR)
                            cstep(ctx, "vector",
                                  (lambda sl, dst: lambda v: v.tensor_tensor(
                                      dst[0:32, sl], s0a, s0b, ALU.subtract))(sl, dst),
                                  [], vR)
                            cstep(ctx, "vector",
                                  (lambda sl: lambda v: v.tensor_tensor(
                                      s0a, rws[0:32, sl], ropeB[0:32, sl],
                                      ALU.mult))(sl),
                                  [], vR)
                            cstep(ctx, "vector",
                                  (lambda sl: lambda v: v.tensor_tensor(
                                      s0b, rws[32:64, sl], ropeB[32:64, sl],
                                      ALU.mult))(sl),
                                  [], vR)
                            cstep(ctx, "vector",
                                  (lambda sl, dst: lambda v: v.tensor_tensor(
                                      dst[32:64, sl], s0b, s0a, ALU.add))(sl, dst),
                                  [], vR)

                for r in range(2):
                    def tr(r):
                        def f(t):
                            last = None
                            for i in range(8):
                                tc = r * 8 + i
                                last = t.transpose(
                                    psT[:, i * 128:(i + 1) * 128],
                                    vf_sb[:, tc * 128:(tc + 1) * 128],
                                    C["ident_bf"][:, :])
                            return last
                        return f
                    w2_ = [(sE, sE.v)] if r == 0 else [(vR, vR.v)]
                    cstep(ctx, "tensor", tr(r), w2_, tT)
                    cstep(ctx, "vector",
                          (lambda r: lambda v: v.tensor_copy(
                              vt_sb[:, r * 1024:(r + 1) * 1024], psT[:, :]))(r),
                          [(tT, tT.v)], vR)

                qk_ready = [(sE, sE.v), (vR, vR.v)]
                ti = 0
                for g in range(4):
                    tc_hi = min(NDC - 1, 4 * g + 3) if causal else NDC - 1
                    tcs = list(range(tc_hi + 1))
                    for ii, tc in enumerate(tcs):
                        sps = psS[:, (ti % 2) * 512:((ti % 2) + 1) * 512]
                        w2_ = list(qk_ready) if ti == 0 else []
                        w2_.append((sX, sX.v - 1))
                        cstep(ctx, "tensor",
                              (lambda sps, tc, g: lambda t: t.matmul(
                                  sps, k_sb[:, tc * 128:(tc + 1) * 128],
                                  q_sb[:, g * 512:(g + 1) * 512],
                                  start=True, stop=True))(sps, tc, g),
                              w2_, tQ)
                        w2_ = [(tQ, tQ.v), (tA, tA.v - 1)]
                        cstep(ctx, "scalar",
                              (lambda sps, ti: lambda s: s.activation(
                                  att[:, (ti % 2) * 512:((ti % 2) + 1) * 512], sps,
                                  AF.Exp, bias=np.float32(-SHIFT),
                                  scale=np.float32(SCL)))(sps, ti),
                              w2_, sX)
                        dep = sX
                        if causal and tc >= 4 * g:
                            vvar = tc - 4 * g
                            cstep(ctx, "vector",
                                  (lambda ti, vvar: lambda v: v.tensor_tensor(
                                      att[:, (ti % 2) * 512:((ti % 2) + 1) * 512],
                                      att[:, (ti % 2) * 512:((ti % 2) + 1) * 512],
                                      maskT[vvar], ALU.mult))(ti, vvar),
                                  [(sX, sX.v), (dIc, dIc_done)], vM)
                            dep = vM
                        first = (ii == 0)
                        last = (ii == len(tcs) - 1)

                        def av(ti, tc, g, first, last):
                            def f(t):
                                t.matmul(l_ps(g), C["ones_bcol"][:, 0:1],
                                         att[:, (ti % 2) * 512:((ti % 2) + 1) * 512],
                                         start=first, stop=last)
                                return t.matmul(
                                    z_ps(g), vt_sb[:, tc * 128:(tc + 1) * 128],
                                    att[:, (ti % 2) * 512:((ti % 2) + 1) * 512],
                                    start=first, stop=last)
                            return f
                        w2_ = [(dep, dep.v)]
                        if first:
                            w2_.append((vZ, vZ.v - 1))
                            w2_.append((vL, vL.v - 3))
                        cstep(ctx, "tensor", av(ti, tc, g, first, last), w2_, tA)
                        ti += 1
                    cstep(ctx, "vector",
                          (lambda g: lambda v: v.tensor_copy(lrow[0:1, :], l_ps(g)))(g),
                          [(tA, tA.v)], vL)
                    cstep(ctx, "vector",
                          lambda v: v.reciprocal(lrow[0:1, :], lrow[0:1, :]),
                          [], vL)
                    cstep(ctx, "tensor",
                          (lambda g: lambda t: t.matmul(
                              rl_ps(g), C["ones_f32"][0:1, 0:128], lrow[0:1, 0:512],
                              start=True, stop=True))(g),
                          [(vL, vL.v)], tZ)
                    cstep(ctx, "scalar",
                          (lambda g: lambda s: s.copy(rl_b[:, :], rl_ps(g)))(g),
                          [(tZ, tZ.v), (vZ, vZ.v)], sE)
                    cstep(ctx, "vector",
                          (lambda g, h: lambda v: v.tensor_tensor(
                              zall[:, h * T + g * 512: h * T + (g + 1) * 512],
                              z_ps(g), rl_b[:, :], ALU.mult))(g, h),
                          [(sE, sE.v)], vZ)
            att_done = [(vZ, vZ.v)]

        with contextlib.ExitStack() as oes:
            dWo = [ctx.sem("at_dWo0"), ctx.sem("at_dWo1")]
            dWo0 = [s.v for s in dWo]
            oPE = ctx.sem("o_pe")
            oPE0 = oPE.v
            wos = sbuf(oes, nc, uname("at_wos"), [128, 2 * T], BF16)
            if layer == 0:
                bo_sb = sbuf(oes, nc, uname("at_bo"), [1, T], F32)
                dB2 = ctx.sem("at_dBo")
                dstep(ctx, "sync", lambda s: s.dma_start(bo_sb[0:1, :], W["bo"]),
                      list(att_done), dB2)
                bo_target = dB2.v

            groups = []
            for dc in range(NDC):
                w = [(oPE, oPE0 + 4 * (dc - 1))]
                if dc < 2:
                    w += list(att_done)
                dstep(ctx, "gpsimd",
                      (lambda dc: lambda g: g.dma_start(
                          wos[:, (dc % 2) * T:((dc % 2) + 1) * T]
                          .rearrange("p (h j) -> p h j", h=NH), W["wo_slab"][dc]))(dc),
                      w, dWo[dc % 2])
                for g in range(4):
                    groups.append(dict(dc=dc, g=g))
            psO = psS
            for gi, grp in enumerate(groups):
                dc, g = grp["dc"], grp["g"]
                grp["ps"] = psO[:, (gi % 2) * 512:((gi % 2) + 1) * 512]
                grp["x_src_ap"] = x_src[dc * 128:(dc + 1) * 128, g * 512:(g + 1) * 512]
                grp["x_dst_ap"] = x_dst[dc * 128:(dc + 1) * 128, g * 512:(g + 1) * 512]

            def mk_out(gi, t):
                dc = groups[gi]["dc"]
                g = groups[gi]["g"]
                ps = groups[gi]["ps"]
                t.wait_ge(dWo[dc % 2].h, dWo0[dc % 2] + 16 * (dc // 2 + 1))
                if layer == 0 and gi == 0:
                    t.wait_ge(dB2.h, bo_target)
                last = None
                for h in range(NH):
                    last = t.matmul(
                        ps, wos[:, (dc % 2) * T + h * 128:(dc % 2) * T + (h + 1) * 128],
                        zall[:, h * T + g * 512: h * T + (g + 1) * 512],
                        start=(h == 0), stop=(h == NH - 1 and layer == 1))
                if layer == 0:
                    last = t.matmul(ps, bo_sb[0:1, dc * 128:(dc + 1) * 128],
                                    C["ones_f32"][0:1, 0:512], start=False, stop=True)
                return last

            emit_residual_out(ctx, oes, groups, mk_out, extra_first_waits=att_done)


def emit_l0ffn(ctx, C, x_src, x_dst, W):
    nc = ctx.nc
    with contextlib.ExitStack() as pes:
        xb = sbuf(pes, nc, uname("f0_xq"), [128, NDC * T], BF16)
        wraw = sbuf(pes, nc, uname("f0_wr"), [128, 2 * 2 * T], F32)
        wq = sbuf(pes, nc, uname("f0_wq"), [128, 2 * T], BF16)
        qt1 = sbuf(pes, nc, uname("f0_t1"), [128, T], F32)
        qt2 = sbuf(pes, nc, uname("f0_t2"), [128, T], F32)

        ln_done = emit_ln(ctx, C, x_src, xb, 0, T, quantize=True)

        dWf = [ctx.sem("f0_dW0"), ctx.sem("f0_dW1")]
        vQ = ctx.sem("f0_vQ")
        pe = ctx.sem("o_pe")
        ev = ctx.sem("o_ev")
        dx = [ctx.sem("o_dx%d" % i) for i in range(4)]
        ds = [ctx.sem("o_ds%d" % i) for i in range(4)]
        dWf0 = [s.v for s in dWf]
        vQ0, pe0, ev0 = vQ.v, pe.v, ev.v
        dx0 = [s.v for s in dx]
        ds0 = [s.v for s in ds]
        psO = psum(pes, nc, uname("f0_psO"), [128, 1024])
        xt = sbuf(pes, nc, uname("ro_xt"), [128, 4 * 512], F32)
        ot = sbuf(pes, nc, uname("ro_ot"), [128, 4 * 512], F32)

        for oc in range(NDC):
            w = [(vQ, vQ0 + 4 * (oc - 1))]
            dstep(ctx, "gpsimd",
                  (lambda oc: lambda g: g.dma_start(
                      wraw[:, (oc % 2) * 2 * T:(oc % 2) * 2 * T + T]
                      .rearrange("p (dc o) -> p dc o", dc=NDC),
                      W["wT_slab"][oc]))(oc),
                  w, dWf[oc % 2])
            dstep(ctx, "gpsimd",
                  (lambda oc: lambda g: g.dma_start(
                      wraw[:, (oc % 2) * 2 * T + T:(oc % 2) * 2 * T + 2 * T]
                      .rearrange("p (dc o) -> p dc o", dc=NDC),
                      W["rT_slab"][oc]))(oc),
                  [], dWf[oc % 2])
            wsl = wraw[:, (oc % 2) * 2 * T:(oc % 2) * 2 * T + T]
            rsl = wraw[:, (oc % 2) * 2 * T + T:(oc % 2) * 2 * T + 2 * T]
            qsl = wq[:, (oc % 2) * T:((oc % 2) + 1) * T]
            w = [(dWf[oc % 2], dWf0[oc % 2] + 32 * (oc // 2 + 1)),
                 (pe, pe0 + 4 * (oc - 1))]
            cstep(ctx, "vector",
                  (lambda wsl: lambda v: v.tensor_scalar(
                      qt1[:, :], wsl, 0.2, 0.6, ALU.is_gt, ALU.mult))(wsl), w, vQ)
            cstep(ctx, "vector",
                  (lambda wsl: lambda v: v.tensor_scalar(
                      qt2[:, :], wsl, -0.2, 0.6, ALU.is_lt, ALU.mult))(wsl), [], vQ)
            cstep(ctx, "vector",
                  lambda v: v.tensor_tensor(qt1[:, :], qt1[:, :], qt2[:, :],
                                            ALU.subtract), [], vQ)
            cstep(ctx, "vector",
                  (lambda rsl, qsl: lambda v: v.tensor_tensor(
                      qsl, qt1[:, :], rsl, ALU.add))(rsl, qsl), [], vQ)
            vq_ready = vQ.v
            for g in range(4):
                gi = 4 * oc + g
                xi = gi % 4
                ps = psO[:, (gi % 2) * 512:((gi % 2) + 1) * 512]
                x_src_ap = x_src[oc * 128:(oc + 1) * 128, g * 512:(g + 1) * 512]
                x_dst_ap = x_dst[oc * 128:(oc + 1) * 128, g * 512:(g + 1) * 512]
                w = [(ev, ev0 + gi - 3)]
                if gi < 4:
                    w.append(ln_done)
                dstep(ctx, "sync",
                      (lambda x_src_ap, xi: lambda s: s.dma_start(
                          xt[:, xi * 512:(xi + 1) * 512], x_src_ap))(x_src_ap, xi),
                      w, dx[xi])
                w = [(ev, ev0 + gi - 1), (vQ, vq_ready)]
                if gi == 0:
                    w.append(ln_done)

                def mk(ps, oc, g):
                    def f(t):
                        last = None
                        for dc in range(NDC):
                            last = t.matmul(
                                ps, wq[:, (oc % 2) * T + dc * 128:
                                       (oc % 2) * T + (dc + 1) * 128],
                                xb[:, dc * T + g * 512: dc * T + (g + 1) * 512],
                                start=(dc == 0), stop=(dc == NDC - 1))
                        return last
                    return f
                cstep(ctx, "tensor", mk(ps, oc, g), w, pe)
                w = [(pe, pe0 + gi + 1), (dx[xi], dx0[xi] + 16 * (gi // 4 + 1)),
                     (ds[xi], ds0[xi] + 16 * (gi // 4))]
                cstep(ctx, "vector",
                      (lambda ps, xi: lambda v: v.tensor_tensor(
                          ot[:, xi * 512:(xi + 1) * 512], ps,
                          xt[:, xi * 512:(xi + 1) * 512], ALU.add))(ps, xi),
                      w, ev)
                dstep(ctx, "sync",
                      (lambda x_dst_ap, xi: lambda s: s.dma_start(
                          x_dst_ap, ot[:, xi * 512:(xi + 1) * 512]))(x_dst_ap, xi),
                      [(ev, ev0 + gi + 1)], ds[xi])


def emit_l1ffn(ctx, C, x_src, x_dst, W):
    nc = ctx.nc
    with contextlib.ExitStack() as pes:
        acc = sbuf(pes, nc, uname("f1_acc"), [128, NDC * TG], F32)
        xb = sbuf(pes, nc, uname("f1_xn"), [128, NDC * TG], BF16)
        lg_tm = sbuf(pes, nc, uname("f1_lg"), [128, 8 * 16], F32)
        rr = sbuf(pes, nc, uname("f1_rr"), [128, 16], F32)
        cf = sbuf(pes, nc, uname("f1_cf"), [128, 16], F32)
        mrow = sbuf(pes, nc, uname("f1_mr"), [128, 2], F32)
        ctile = sbuf(pes, nc, uname("f1_ct"), [16, 8 * 128], F32)
        srow = sbuf(pes, nc, uname("f1_sr"), [1, TG], F32)
        gws = sbuf(pes, nc, uname("f1_gws"), [128, NDC * 16], BF16)
        sgs = sbuf(pes, nc, uname("f1_sgs"), [128, NDC], BF16)

        coef_dram = nc.dram_tensor(uname("coef_dram"), [17, TG], F32,
                                   kind="Internal").ap()

        dG = ctx.sem("f1_dG")
        dWsp = [ctx.sem("f1_dWs0"), ctx.sem("f1_dWs1")]
        dW2 = ctx.sem("f1_dW2")
        dCB = ctx.sem("f1_dCB")
        dCS = ctx.sem("f1_dCS")
        tR = ctx.sem("f1_tR")
        vRt = ctx.sem("f1_vRt")
        sR = ctx.sem("f1_sR")
        tU = ctx.sem("f1_tU")
        sU = ctx.sem("f1_sU")
        vU = ctx.sem("f1_vU")
        tD = ctx.sem("f1_tD")
        vD = ctx.sem("f1_vD")
        oEV = ctx.sem("o_ev")
        oDXs = [ctx.sem("o_dx%d" % i) for i in range(4)]
        oDSs = [ctx.sem("o_ds%d" % i) for i in range(4)]

        psU = psum(pes, nc, uname("f1_psU"), [128, 2 * TG])
        psD = psum(pes, nc, uname("f1_psD"), [128, 2 * TG])

        dstep(ctx, "gpsimd", lambda g: g.dma_start(
            gws.rearrange("p (dc e) -> p dc e", dc=NDC),
            W["gwT"].rearrange("(dc p) e -> p dc e", p=128)), [], dG)
        dstep(ctx, "gpsimd", lambda g: g.dma_start(sgs[:, :], W["sgT"][:, :]),
              [], dG)
        dG_done = dG.v

        units = []
        for e in range(E_MOE):
            units.append(dict(kind="exp", e=e, gate_row=e))
        for s in range(4):
            units.append(dict(kind="sh", e=s, gate_row=16))

        for tgi in range(TG_PASSES):
            t0 = tgi * TG
            pre = [(tU, tU.v), (tD, tD.v), (vD, vD.v), (vU, vU.v), (sU, sU.v),
                   (oEV, oEV.v), (dCB, dCB.v)] + [(s, s.v) for s in oDSs]
            ln_done = emit_ln(ctx, C, x_src, xb, t0, TG, quantize=False, pre=pre,
                              ext=dict(psM=psU[0:1, 0:TG], psS=psU[0:1, TG:2 * TG],
                                       psBc=psD[:, 0:TG]))

            def rmm(t):
                last = None
                for g in range(TG // 512):
                    for dc in range(NDC):
                        last = t.matmul(
                            psU[0:16, g * 512:(g + 1) * 512],
                            gws[:, dc * 16:(dc + 1) * 16],
                            xb[:, dc * TG + g * 512: dc * TG + (g + 1) * 512],
                            start=(dc == 0), stop=(dc == NDC - 1))
                return last
            cstep(ctx, "tensor", rmm, [ln_done, (dG, dG_done)], tR)

            def smm(t):
                last = None
                for g in range(TG // 512):
                    for dc in range(NDC):
                        last = t.matmul(
                            psU[0:1, TG + g * 512: TG + (g + 1) * 512],
                            sgs[:, dc:dc + 1],
                            xb[:, dc * TG + g * 512: dc * TG + (g + 1) * 512],
                            start=(dc == 0), stop=(dc == NDC - 1))
                return last
            cstep(ctx, "tensor", smm, [], tR)
            cstep(ctx, "scalar", lambda s: s.activation(
                srow[0:1, :], psU[0:1, TG:2 * TG], AF.Sigmoid), [(tR, tR.v)], sR)
            dstep(ctx, "sync",
                  lambda s: s.dma_start(coef_dram[16:17, :], srow[0:1, :]),
                  [(sR, sR.v), (dCB, dCB.v)] + pre, dCS)
            cstep(ctx, "vector", lambda v: v.tensor_copy(
                ctile[0:16, :], psU[0:16, 0:TG]), [(sR, sR.v)], vRt)
            for tc in range(8):
                cstep(ctx, "tensor",
                      (lambda tc: lambda t: t.transpose(
                          psD[0:128, 0:16], ctile[0:16, tc * 128:(tc + 1) * 128],
                          C["ident_f32"][0:16, 0:16]))(tc),
                      [(vRt, vRt.v)], tR)
                cstep(ctx, "vector",
                      (lambda tc: lambda v: v.tensor_copy(
                          lg_tm[:, tc * 16:(tc + 1) * 16], psD[0:128, 0:16]))(tc),
                      [(tR, tR.v)], vRt)
            for tc in range(8):
                sl = slice(tc * 16, (tc + 1) * 16)
                cstep(ctx, "vector",
                      (lambda sl: lambda v: v.reduce_max(
                          mrow[:, 0:1], lg_tm[:, sl], axis=AX.X, negate=True))(sl),
                      [], vRt)
                cstep(ctx, "scalar",
                      (lambda sl: lambda s: s.activation(
                          rr[:, :], lg_tm[:, sl], AF.Exp, bias=mrow[:, 0:1],
                          accum_out=mrow[:, 1:2]))(sl),
                      [(vRt, vRt.v)], sR)
                cstep(ctx, "vector",
                      lambda v: v.reciprocal(mrow[:, 1:2], mrow[:, 1:2]),
                      [(sR, sR.v)], vRt)
                cstep(ctx, "vector", lambda v: v.tensor_single_scalar(
                    rr[:, :], rr[:, :], mrow[:, 1:2], ALU.mult), [], vRt)
                cstep(ctx, "vector", lambda v: v.memset(cf[:, :], 0.0), [], vRt)
                for r in range(TOPK):
                    cstep(ctx, "vector", lambda v: v.reduce_max(
                        mrow[:, 0:1], rr[:, :], axis=AX.X), [], vRt)
                    cstep(ctx, "vector", lambda v: v.tensor_single_scalar(
                        lg_tm[:, 0:16], rr[:, :], mrow[:, 0:1], ALU.is_ge), [], vRt)
                    cstep(ctx, "vector", lambda v: v.tensor_tensor(
                        lg_tm[:, 0:16], lg_tm[:, 0:16], rr[:, :], ALU.mult), [], vRt)
                    cstep(ctx, "vector", lambda v: v.tensor_tensor(
                        cf[:, :], cf[:, :], lg_tm[:, 0:16], ALU.add), [], vRt)
                    cstep(ctx, "vector", lambda v: v.tensor_tensor(
                        rr[:, :], rr[:, :], lg_tm[:, 0:16], ALU.subtract), [], vRt)
                cstep(ctx, "tensor",
                      (lambda tc: lambda t: t.transpose(
                          psD[0:16, 0:128], cf[:, 0:16], C["ident_f32"][:, :]))(tc),
                      [(vRt, vRt.v)], tR)
                cstep(ctx, "vector",
                      (lambda tc: lambda v: v.tensor_copy(
                          ctile[0:16, tc * 128:(tc + 1) * 128], psD[0:16, 0:128]))(tc),
                      [(tR, tR.v)], vRt)
            dstep(ctx, "sync",
                  lambda s: s.dma_start(coef_dram[0:16, :], ctile[0:16, :]),
                  [(vRt, vRt.v)], dCS)
            for i in range(NDC):
                w = [(oEV, oEV.v)] if i == 0 else []
                cstep(ctx, "vector",
                      (lambda i: lambda v: v.memset(acc[:, i * TG:(i + 1) * TG],
                                                    0.0))(i),
                      w, vRt)
            acc_zeroed = vRt.v
            coef_stored = dCS.v

            with contextlib.ExitStack() as ues:
                he = sbuf(ues, nc, uname("f1_h"), [128, NIC * TG], BF16)
                w2s = sbuf(ues, nc, uname("f1_w2"), [128, NIC * 2048], BF16)
                w13s = sbuf(ues, nc, uname("f1_w13"), [128, 4 * NDC * 128], BF16)
                coefb = sbuf(ues, nc, uname("f1_cb"), [128, TG], F32)
                sg = sbuf(ues, nc, uname("f1_sg"), [128, TG], F32)

                tU0u, tD0u, vD0u, vU0u = tU.v, tD.v, vD.v, vU.v
                pairno = [0]
                for ui, u in enumerate(units):
                    dstep(ctx, "sync",
                          (lambda u: lambda s: s.dma_start(
                              coefb[:, :],
                              coef_dram[u["gate_row"]:u["gate_row"] + 1, :]
                              .partition_broadcast(128)))(u),
                          [(dCS, coef_stored), (vU, vU.v)], dCB)
                    for ic in range(NIC):
                        w = []
                        if ic == 0:
                            w = [(tD, tD0u + 32 * ui)]
                            if ui == 0:
                                w.append(ln_done)
                        src = (W["w2T_sl"][u["e"]][ic] if u["kind"] == "exp"
                               else W["sdwT_sl"][u["e"]][ic])
                        dstep(ctx, "gpsimd",
                              (lambda src, ic: lambda g: g.dma_start(
                                  w2s[:, ic * 2048:(ic + 1) * 2048], src))(src, ic),
                              w, dW2)
                    w2_done = dW2.v
                    for p in range(NIC):
                        pn = pairno[0]
                        pb = pn % 2
                        w = [(tU, tU0u + pn - 1)]
                        if pn == 0:
                            w.append(ln_done)
                        for j in range(2):
                            if u["kind"] == "exp":
                                src = (W["w13g_sl"] if j == 0 else W["w13u_sl"])[u["e"]][p]
                            else:
                                src = (W["sgug_sl"] if j == 0 else W["sguu_sl"])[u["e"]][p]
                            dstep(ctx, "gpsimd",
                                  (lambda src, pb, j: lambda g: g.dma_start(
                                      w13s[:, (2 * pb + j) * NDC * 128:
                                           (2 * pb + j + 1) * NDC * 128]
                                      .rearrange("p (dc e) -> p dc e", dc=NDC),
                                      src))(src, pb, j),
                                  w if j == 0 else [], dWsp[pb])

                        def upmm(pb):
                            def f(t):
                                last = None
                                for j in range(2):
                                    ps = psU[:, j * TG:(j + 1) * TG]
                                    for dc in range(NDC):
                                        for g5 in range(TG // 512):
                                            last = t.matmul(
                                                ps[:, g5 * 512:(g5 + 1) * 512],
                                                w13s[:, (2 * pb + j) * NDC * 128 + dc * 128:
                                                     (2 * pb + j) * NDC * 128 + (dc + 1) * 128],
                                                xb[:, dc * TG + g5 * 512:
                                                   dc * TG + (g5 + 1) * 512],
                                                start=(dc == 0), stop=(dc == NDC - 1))
                                return last
                            return f
                        w = [(dWsp[pb], dWsp[pb].v), (sU, sU.v), (vU, vU.v)]
                        if ui == 0 and p == 0:
                            w.append(ln_done)
                        cstep(ctx, "tensor", upmm(pb), w, tU)
                        cstep(ctx, "scalar",
                              lambda s: s.activation(sg[:, :], psU[:, 0:TG], AF.Silu),
                              [(tU, tU.v)], sU)
                        w = [(sU, sU.v), (dCB, dCB.v)]
                        if p == 0:
                            w.append((tD, tD0u + 32 * ui))
                        cstep(ctx, "vector",
                              (lambda p: lambda v: v.tensor_tensor(
                                  he[:, p * TG:(p + 1) * TG], sg[:, :],
                                  psU[:, TG:2 * TG], ALU.mult))(p),
                              w, vU)
                        cstep(ctx, "vector",
                              (lambda p: lambda v: v.tensor_tensor(
                                  he[:, p * TG:(p + 1) * TG],
                                  he[:, p * TG:(p + 1) * TG],
                                  coefb[:, :], ALU.mult))(p),
                              [], vU)
                        pairno[0] += 1
                    for oc in range(NDC):
                        ri = oc % 2

                        def dnmm(ri, oc):
                            def f(t):
                                last = None
                                for g5 in range(TG // 512):
                                    for ic in range(NIC):
                                        last = t.matmul(
                                            psD[:, ri * TG + g5 * 512:
                                                ri * TG + (g5 + 1) * 512],
                                            w2s[:, ic * 2048 + oc * 128:
                                                ic * 2048 + (oc + 1) * 128],
                                            he[:, ic * TG + g5 * 512:
                                               ic * TG + (g5 + 1) * 512],
                                            start=(ic == 0), stop=(ic == NIC - 1))
                                return last
                            return f
                        w = [(dW2, w2_done), (vD, vD.v - 1)]
                        if oc == 0:
                            w.append((vU, vU.v))
                        cstep(ctx, "tensor", dnmm(ri, oc), w, tD, tD)
                        w = [(tD, tD.v)]
                        if ui == 0 and oc == 0:
                            w.append((vRt, acc_zeroed))
                        cstep(ctx, "vector",
                              (lambda ri, oc: lambda v: v.tensor_tensor(
                                  acc[:, oc * TG:(oc + 1) * TG],
                                  acc[:, oc * TG:(oc + 1) * TG],
                                  psD[:, ri * TG:(ri + 1) * TG], ALU.add))(ri, oc),
                              w, vD)
                units_done = (tD.v, vD.v)

            with contextlib.ExitStack() as fes:
                xt = sbuf(fes, nc, uname("f1_xt"), [128, 4 * 512], F32)
                ot = sbuf(fes, nc, uname("f1_ot"), [128, 4 * 512], F32)
                oEVb = oEV.v
                oDXb = [s.v for s in oDXs]
                oDSb = [s.v for s in oDSs]
                groups = []
                for oc in range(NDC):
                    for g in range(TG // 512):
                        groups.append((oc, g))
                for gi, (oc, g) in enumerate(groups):
                    xi = gi % 4
                    w = [(oEV, oEVb + gi - 3)]
                    if gi < 4:
                        w.append((tD, units_done[0]))
                    dstep(ctx, "sync",
                          (lambda oc, g, xi, t0: lambda s: s.dma_start(
                              xt[:, xi * 512:(xi + 1) * 512],
                              x_src[oc * 128:(oc + 1) * 128,
                                    t0 + g * 512: t0 + (g + 1) * 512]))(oc, g, xi, t0),
                          w, oDXs[xi])
                    w = [(oDXs[xi], oDXb[xi] + 16 * (gi // 4 + 1)), (vD, units_done[1]),
                         (oDSs[xi], oDSb[xi] + 16 * (gi // 4))]
                    cstep(ctx, "vector",
                          (lambda oc, g, xi: lambda v: v.tensor_tensor(
                              ot[:, xi * 512:(xi + 1) * 512],
                              acc[:, oc * TG + g * 512: oc * TG + (g + 1) * 512],
                              xt[:, xi * 512:(xi + 1) * 512], ALU.add))(oc, g, xi),
                          w, oEV)
                    dstep(ctx, "sync",
                          (lambda oc, g, xi, t0: lambda s: s.dma_start(
                              x_dst[oc * 128:(oc + 1) * 128,
                                    t0 + g * 512: t0 + (g + 1) * 512],
                              ot[:, xi * 512:(xi + 1) * 512]))(oc, g, xi, t0),
                          [(oEV, oEVb + gi + 1)], oDSs[xi])


def declare_consts(ctx, es, layer_set):
    nc = ctx.nc
    C = {}
    C["ones_bcol"] = sbuf(es, nc, "c_ones_b", [128, 1], BF16)
    C["ones_f32"] = sbuf(es, nc, "c_ones_f", [1, 512], F32)
    C["ident_bf"] = sbuf(es, nc, "c_id_bf", [128, 128], BF16)
    C["ident_f32"] = sbuf(es, nc, "c_id_f", [128, 128], F32)
    ident_bf = nc.dram_tensor("ident_bf", [128, 128], BF16, kind="ExternalInput").ap()
    ident_f = nc.dram_tensor("ident_f", [128, 128], F32, kind="ExternalInput").ap()
    dI = ctx.sem("c_dI")
    cM = ctx.sem("c_cM")
    dstep(ctx, "sync", lambda s: s.dma_start(C["ident_bf"][:, :], ident_bf[:, :]),
          [], dI)
    dstep(ctx, "sync", lambda s: s.dma_start(C["ident_f32"][:, :], ident_f[:, :]),
          [], dI)
    if "l0attn" in layer_set:
        C["maskT_dram"] = nc.dram_tensor("maskT", [4, 128, 512], BF16,
                                         kind="ExternalInput").ap()
    if "l1attn" in layer_set:
        C["rope_dram"] = nc.dram_tensor("rope", [2, 64, T], BF16,
                                        kind="ExternalInput").ap()
    cstep(ctx, "vector", lambda v: v.memset(C["ones_bcol"][:, :], 1.0),
          [(dI, dI.v)], cM)
    cstep(ctx, "vector", lambda v: v.memset(C["ones_f32"][:, :], 1.0), [], cM)
    ctx.add("tensor", lambda t: t.nop(), [(cM, cM.v), (dI, dI.v)], [])
    ctx.add("scalar", lambda t: t.nop(), [(cM, cM.v), (dI, dI.v)], [])
    ctx.add("vector", lambda t: t.nop(), [(dI, dI.v)], [])
    ctx.add("gpsimd", lambda t: t.nop(), [(cM, cM.v), (dI, dI.v)], [])
    return C


def declare_weight_aps(nc, phases):
    W = {}
    if "l0attn" in phases:
        wq = nc.dram_tensor("a0_wq", [NH, D, HD], F32, kind="ExternalInput").ap()
        wk = nc.dram_tensor("a0_wk", [NH, D, HD], F32, kind="ExternalInput").ap()
        wv = nc.dram_tensor("a0_wv", [NH, D, HD], F32, kind="ExternalInput").ap()
        wo = nc.dram_tensor("a0_wo", [NH, HD, D], F32, kind="ExternalInput").ap()
        bqkv = nc.dram_tensor("a0_b", [NH, 1, 3 * 128], F32, kind="ExternalInput").ap()
        bo = nc.dram_tensor("a0_bo", [1, D], F32, kind="ExternalInput").ap()
        W["l0attn"] = dict(
            wq_slab=[wq[h].rearrange("(dc p) e -> p dc e", p=128) for h in range(NH)],
            wk_slab=[wk[h].rearrange("(dc p) e -> p dc e", p=128) for h in range(NH)],
            wv_slab=[wv[h].rearrange("(dc p) e -> p dc e", p=128) for h in range(NH)],
            wo_slab=[wo[:, :, dc * 128:(dc + 1) * 128].rearrange("h e j -> e h j")
                     for dc in range(NDC)],
            bqkv=[bqkv[h] for h in range(NH)],
            bo=bo[0:1, :],
        )
    if "l0ffn" in phases:
        wT = nc.dram_tensor("f0_wT", [D, D], F32, kind="ExternalInput").ap()
        rT = nc.dram_tensor("f0_rT", [D, D], F32, kind="ExternalInput").ap()
        W["l0ffn"] = dict(
            wT_slab=[wT[:, oc * 128:(oc + 1) * 128]
                     .rearrange("(dc p) o -> p dc o", p=128) for oc in range(NDC)],
            rT_slab=[rT[:, oc * 128:(oc + 1) * 128]
                     .rearrange("(dc p) o -> p dc o", p=128) for oc in range(NDC)],
        )
    if "l1attn" in phases:
        wqT = nc.dram_tensor("a1_wqT", [D, D], F32, kind="ExternalInput").ap()
        wkT = nc.dram_tensor("a1_wkT", [D, D], F32, kind="ExternalInput").ap()
        wvT = nc.dram_tensor("a1_wvT", [D, D], F32, kind="ExternalInput").ap()
        woT = nc.dram_tensor("a1_woT", [D, D], F32, kind="ExternalInput").ap()
        W["l1attn"] = dict(
            wq_slab=[wqT[:, h * 128:(h + 1) * 128]
                     .rearrange("(dc p) e -> p dc e", p=128) for h in range(NH)],
            wk_slab=[wkT[:, h * 128:(h + 1) * 128]
                     .rearrange("(dc p) e -> p dc e", p=128) for h in range(NH)],
            wv_slab=[wvT[:, h * 128:(h + 1) * 128]
                     .rearrange("(dc p) e -> p dc e", p=128) for h in range(NH)],
            wo_slab=[woT[:, dc * 128:(dc + 1) * 128]
                     .rearrange("(h p) j -> p h j", p=128) for dc in range(NDC)],
        )
    if "l1ffn" in phases:
        gwT = nc.dram_tensor("f1_gwT", [D, 16], F32, kind="ExternalInput").ap()
        sgT = nc.dram_tensor("f1_sgT", [128, NDC], F32, kind="ExternalInput").ap()
        w13T = nc.dram_tensor("f1_w13T", [E_MOE, D, 2 * IE], F32,
                              kind="ExternalInput").ap()
        w2T = nc.dram_tensor("f1_w2T", [E_MOE, IE, D], F32, kind="ExternalInput").ap()
        sguT = nc.dram_tensor("f1_sguT", [D, 2 * ISH], F32, kind="ExternalInput").ap()
        sdwT = nc.dram_tensor("f1_sdwT", [ISH, D], F32, kind="ExternalInput").ap()

        def colslab(ap2d, c0):
            return ap2d[:, c0:c0 + 128].rearrange("(dc p) e -> p dc e", p=128)

        W["l1ffn"] = dict(
            gwT=gwT[:, :],
            sgT=sgT[:, :],
            w13g_sl=[[colslab(w13T[e], p * 128) for p in range(NIC)]
                     for e in range(E_MOE)],
            w13u_sl=[[colslab(w13T[e], IE + p * 128) for p in range(NIC)]
                     for e in range(E_MOE)],
            w2T_sl=[[w2T[e][ic * 128:(ic + 1) * 128, :] for ic in range(NIC)]
                    for e in range(E_MOE)],
            sgug_sl=[[colslab(sguT, s * IE + p * 128) for p in range(NIC)]
                     for s in range(4)],
            sguu_sl=[[colslab(sguT, ISH + s * IE + p * 128) for p in range(NIC)]
                     for s in range(4)],
            sdwT_sl=[[sdwT[s * IE + ic * 128: s * IE + (ic + 1) * 128, :]
                      for ic in range(NIC)] for s in range(4)],
        )
    return W


PHASE_FNS = dict(
    l0attn=lambda ctx, C, xs, xd, W: emit_attn(ctx, C, 0, xs, xd, W),
    l0ffn=emit_l0ffn,
    l1attn=lambda ctx, C, xs, xd, W: emit_attn(ctx, C, 1, xs, xd, W),
    l1ffn=emit_l1ffn,
)


DEBUG_MIDS = True


def build_program(phases):
    nc = bass.Bass("TRN2", target_bir_lowering=False)
    x_in = nc.dram_tensor("x_in", [D, T], F32, kind="ExternalInput").ap()
    x_out = nc.dram_tensor("x_out", [D, T], F32, kind="ExternalOutput").ap()
    xr = [x_in]
    for i in range(len(phases) - 1):
        xr.append(nc.dram_tensor("x_mid%d" % i, [D, T], F32,
                                 kind="ExternalOutput" if DEBUG_MIDS else "Internal").ap())
    xr.append(x_out)
    W = declare_weight_aps(nc, phases)
    with contextlib.ExitStack() as es:
        ctx = Ctx(nc, es)
        C = declare_consts(ctx, es, set(phases))
        for i, ph in enumerate(phases):
            PHASE_FNS[ph](ctx, C, xr[i], xr[i + 1], W[ph])
            phase_join(ctx)
        ctx.emit()
    return nc


def host_consts():
    ident = np.eye(128)
    maskT = np.zeros((4, 128, 512), dtype=ml_dtypes.bfloat16)
    tkl = np.arange(128)[:, None]
    tql = np.arange(512)[None, :]
    for v in range(4):
        maskT[v] = (tkl <= tql - 128 * v).astype(ml_dtypes.bfloat16)
    inv_freq = 1.0 / (10000.0 ** (np.arange(0, 128, 2, dtype=np.float64) / 128.0))
    t = np.arange(T, dtype=np.float64)
    ang = np.outer(inv_freq, t)
    sin, cos = np.sin(ang), np.cos(ang)
    ropeA = np.concatenate([cos[0:32], sin[0:32]], axis=0)
    ropeB = np.concatenate([sin[32:64], cos[32:64]], axis=0)
    rope = np.stack([ropeA, ropeB]).astype(ml_dtypes.bfloat16)
    return dict(ident_bf=ident.astype(ml_dtypes.bfloat16),
                ident_f=ident.astype(np.float32),
                maskT=maskT, rope=rope)


def host_weight_maps(inputs, phases):
    m = {}
    if "l0attn" in phases:
        m["a0_wq"] = np.ascontiguousarray(inputs["L0_attn_W_Q"])
        m["a0_wk"] = np.ascontiguousarray(inputs["L0_attn_W_K"])
        m["a0_wv"] = np.ascontiguousarray(inputs["L0_attn_W_V"])
        m["a0_wo"] = np.ascontiguousarray(inputs["L0_attn_W_O"])
        m["a0_b"] = np.ascontiguousarray(
            np.stack([inputs["L0_attn_b_Q"], inputs["L0_attn_b_K"],
                      inputs["L0_attn_b_V"]], axis=1).reshape(NH, 1, 3 * 128))
        m["a0_bo"] = np.ascontiguousarray(inputs["L0_attn_b_O"].reshape(1, D))
    if "l0ffn" in phases:
        m["f0_wT"] = np.ascontiguousarray(inputs["L0_ffn_weight"].T)
        m["f0_rT"] = np.ascontiguousarray(inputs["L0_ffn_rram_variation"].T)
    if "l1attn" in phases:
        m["a1_wqT"] = np.ascontiguousarray(inputs["L1_attn_wq"].T)
        m["a1_wkT"] = np.ascontiguousarray(inputs["L1_attn_wk"].T)
        m["a1_wvT"] = np.ascontiguousarray(inputs["L1_attn_wv"].T)
        m["a1_woT"] = np.ascontiguousarray(inputs["L1_attn_wo"].T)
    if "l1ffn" in phases:
        m["f1_gwT"] = np.ascontiguousarray(inputs["L1_ffn_gate_w"].T)
        m["f1_sgT"] = np.ascontiguousarray(
            inputs["L1_ffn_sgate_w"].reshape(NDC, 128).T)
        m["f1_w13T"] = np.ascontiguousarray(inputs["L1_ffn_w13"].transpose(0, 2, 1))
        m["f1_w2T"] = np.ascontiguousarray(inputs["L1_ffn_w2"].transpose(0, 2, 1))
        m["f1_sguT"] = np.ascontiguousarray(inputs["L1_ffn_shared_gate_up_w"].T)
        m["f1_sdwT"] = np.ascontiguousarray(inputs["L1_ffn_shared_down_w"].T)
    return m


def const_maps(phases):
    hc = host_consts()
    m = dict(ident_bf=hc["ident_bf"], ident_f=hc["ident_f"])
    if "l0attn" in phases:
        m["maskT"] = hc["maskT"]
    if "l1attn" in phases:
        m["rope"] = hc["rope"]
    return m


def run_phases(x_percore, inputs, phases):
    nc = build_program(phases)
    wm = host_weight_maps(inputs, phases)
    cm = const_maps(phases)
    in_maps = []
    for c in range(8):
        d = dict(x_in=x_percore[c])
        d.update(wm)
        d.update(cm)
        in_maps.append(d)
    res = run_bass_kernel_spmd(nc, in_maps, core_ids=list(range(8)))
    return [res.results[c]["x_out"] for c in range(8)]


PHASE_ORDER = ["l0attn", "l0ffn", "l1attn", "l1ffn"]
FUSED = True


def kernel(**inputs):
    x = inputs["L0_attn_x"]
    x_percore = [np.ascontiguousarray(x[c].T) for c in range(8)]
    if FUSED:
        x_percore = run_phases(x_percore, inputs, PHASE_ORDER)
    else:
        for ph in PHASE_ORDER:
            x_percore = run_phases(x_percore, inputs, [ph])
    out = np.stack([np.ascontiguousarray(xc.T) for xc in x_percore])
    return np.ascontiguousarray(out.astype(np.float32))


# version 06
# speedup vs baseline: 1.0054x; 1.0054x over previous
import contextlib
import numpy as np
import ml_dtypes
import concourse.bass as bass
import concourse.mybir as mybir
from concourse.bass_utils import run_bass_kernel_spmd

F32 = mybir.dt.float32
BF16 = mybir.dt.bfloat16
AF = mybir.ActivationFunctionType
ALU = mybir.AluOpType
AX = mybir.AxisListType

D = 2048
T = 2048
NDC = D // 128
NH = 16
HD = 128
EPS = 1e-5
SCL = float(1.0 / np.sqrt(128.0))
SHIFT = 25.0
IE = 1408
NIC = IE // 128
ISH = 5632
TG = 1024
TG_PASSES = 2
E_MOE = 16
TOPK = 4
NGT = T // 512


class Sem:
    __slots__ = ("h", "v", "name")

    def __init__(self, h, name):
        self.h = h
        self.v = 0
        self.name = name


class Ctx:
    def __init__(self, nc, sem_es):
        self.nc = nc
        self.sem_es = sem_es
        self.steps = []
        self.sems = []
        self._semmap = {}
        self._owner = {}

    def sem(self, name):
        if name in self._semmap:
            return self._semmap[name]
        h = self.sem_es.enter_context(self.nc.semaphore(name))
        s = Sem(h, name)
        self.sems.append(s)
        self._semmap[name] = s
        return s

    def add(self, eng, fn, waits=(), incs=()):
        for (s, n, va) in incs:
            owner = self._owner.setdefault(s, eng)
            if owner != eng:
                self._owner[s] = None
        self.steps.append((eng, fn, list(waits), list(incs)))

    def emit(self):
        steps = self.steps
        with self.nc.Block() as blk:
            for eng_name, reg in (("sync", blk.sync), ("gpsimd", blk.gpsimd),
                                  ("tensor", blk.tensor), ("vector", blk.vector),
                                  ("scalar", blk.scalar)):
                def mk(nm):
                    def f(eng):
                        chain = None
                        for (e, fn, waits, incs) in steps:
                            if e != nm:
                                continue
                            assert len(incs) <= 1, (nm, incs)
                            for (s, val) in waits:
                                if val > 0:
                                    eng.wait_ge(s.h, val)
                            if nm in ("vector", "scalar") and chain is not None:
                                eng.wait_ge(chain[0].h, chain[1])
                            inst = fn(eng)
                            for (s, n, va) in incs:
                                inst.then_inc(s.h, n)
                                if self._owner.get(s) == nm:
                                    chain = (s, va)
                    return f
                reg(mk(eng_name))


def dstep(ctx, eng, fn, waits, sem):
    sem.v += 16
    ctx.add(eng, fn, waits, [(sem, 16, sem.v)])


def cstep(ctx, eng, fn, waits, *sems):
    merged = {}
    order = []
    for s in sems:
        s.v += 1
        if s not in merged:
            merged[s] = 0
            order.append(s)
        merged[s] += 1
    ctx.add(eng, fn, waits, [(s, merged[s], s.v) for s in order])


def sbuf(es, nc, name, shape, dtype):
    return es.enter_context(nc.sbuf_tensor(name, shape, dtype)).ap()


def psum(es, nc, name, shape, dtype=F32):
    return es.enter_context(nc.psum_tensor(name, shape, dtype)).ap()


_uid = [0]


def uname(base):
    _uid[0] += 1
    return "%s_%d" % (base, _uid[0])


def emit_ln(ctx, C, x_src, xb, t0, tn, quantize=False, pre=(), ext=None):
    nc = ctx.nc
    dL = ctx.sem("ln_dL")
    sQ = ctx.sem("ln_sQ")
    tS = ctx.sem("ln_tS")
    vS = ctx.sem("ln_vS")
    tB = ctx.sem("ln_tB")
    sB = ctx.sem("ln_sB")
    vN = ctx.sem("ln_vN")
    dL0, sQ0, tS0, vS0, tB0, sB0, vN0 = dL.v, sQ.v, tS.v, vS.v, tB.v, sB.v, vN.v

    NG = tn // 512
    pre = list(pre)
    with contextlib.ExitStack() as les:
        sqws = sbuf(les, nc, uname("ln_sq"), [128, tn], BF16)
        rows = sbuf(les, nc, uname("ln_rows"), [128, 2 * tn], F32)
        seg_m = rows[0:1, 0:tn]
        seg_r = rows[0:1, tn:2 * tn]
        rstd_b = sbuf(les, nc, uname("ln_rb"), [128, tn], BF16)
        mrstd_b = sbuf(les, nc, uname("ln_mb"), [128, tn], BF16)
        if quantize:
            nws = sbuf(les, nc, uname("ln_nws"), [128, tn], F32)
            qws = sbuf(les, nc, uname("ln_qws"), [128, 2 * tn], F32)

        stats_scope = contextlib.ExitStack()
        if ext is None:
            psM = psum(stats_scope, nc, uname("ln_psM"), [1, tn])
            psS = psum(stats_scope, nc, uname("ln_psS"), [1, tn])
        else:
            psM, psS = ext["psM"], ext["psS"]

        for dc in range(NDC):
            cw = pre if dc == 0 else []
            dstep(ctx, "gpsimd",
                  (lambda dc: lambda g: g.dma_start(
                      xb[:, dc * tn:(dc + 1) * tn],
                      x_src[dc * 128:(dc + 1) * 128, t0:t0 + tn]))(dc),
                  cw, dL)
        dL_all = dL0 + 16 * NDC
        for dc in range(NDC):
            w = [(dL, dL_all)]
            if dc == 0:
                w += pre
            if dc >= 1:
                w.append((tS, tS0 + 2 * dc))
            cstep(ctx, "scalar",
                  (lambda dc: lambda s: s.activation(
                      sqws[:, :], xb[:, dc * tn:(dc + 1) * tn], AF.Square))(dc),
                  w, sQ)

            def mmm(dc):
                def f(t):
                    last = None
                    for g in range(NG):
                        last = t.matmul(psM[0:1, g * 512:(g + 1) * 512],
                                        C["ones_bcol"][:, 0:1],
                                        xb[:, dc * tn + g * 512: dc * tn + (g + 1) * 512],
                                        start=(dc == 0), stop=(dc == NDC - 1))
                    return last
                return f

            def smm(dc):
                def f(t):
                    last = None
                    for g in range(NG):
                        last = t.matmul(psS[0:1, g * 512:(g + 1) * 512],
                                        C["ones_bcol"][:, 0:1],
                                        sqws[:, g * 512:(g + 1) * 512],
                                        start=(dc == 0), stop=(dc == NDC - 1))
                    return last
                return f
            w = [(dL, dL_all)]
            if dc == 0:
                w += pre
            cstep(ctx, "tensor", mmm(dc), w, tS)
            cstep(ctx, "tensor", smm(dc), [(sQ, sQ0 + dc + 1)], tS)
        p1 = psM[0:1, 0:tn]
        w = [(tS, tS.v)] + pre
        cstep(ctx, "vector", lambda v: v.tensor_scalar(
            seg_m, psM[0:1, :], 1.0 / D, None, ALU.mult), w, vS)
        cstep(ctx, "vector", lambda v: v.tensor_scalar(
            seg_r, psS[0:1, :], 1.0 / D, None, ALU.mult), [], vS)
        cstep(ctx, "vector", lambda v: v.tensor_tensor(
            p1, seg_m, seg_m, ALU.mult), [], vS)
        cstep(ctx, "vector", lambda v: v.tensor_tensor(
            seg_r, seg_r, p1, ALU.subtract), [], vS)
        cstep(ctx, "vector", lambda v: v.tensor_scalar(
            seg_r, seg_r, float(EPS), None, ALU.add), [], vS)
        cstep(ctx, "scalar", lambda s: s.activation(
            p1, seg_r, AF.Sqrt), [(vS, vS.v)], sQ)
        cstep(ctx, "vector", lambda v: v.reciprocal(seg_r, p1),
              [(sQ, sQ.v)], vS)
        cstep(ctx, "vector", lambda v: v.tensor_tensor(
            seg_m, seg_m, seg_r, ALU.mult), [], vS)
        stats_done = vS.v
        stats_scope.close()

        bc_scope = contextlib.ExitStack()
        if ext is None:
            psBc = psum(bc_scope, nc, uname("ln_psBc"), [128, tn])
        else:
            psBc = ext["psBc"]

        def bc(row):
            def f(t):
                last = None
                for g in range(NG):
                    last = t.matmul(psBc[:, g * 512:(g + 1) * 512],
                                    C["ones_f32"][0:1, 0:128],
                                    row[0:1, g * 512:(g + 1) * 512],
                                    start=True, stop=True)
                return last
            return f
        cstep(ctx, "tensor", bc(seg_r), [(vS, stats_done)], tB)
        cstep(ctx, "scalar", lambda s: s.copy(rstd_b[:, :], psBc[:, :]),
              [(tB, tB0 + 1)], sB)
        cstep(ctx, "tensor", bc(seg_m), [(sB, sB0 + 1)], tB)
        cstep(ctx, "scalar", lambda s: s.copy(mrstd_b[:, :], psBc[:, :]),
              [(tB, tB0 + 2)], sB)
        bc_scope.close()
        for dc in range(NDC):
            w = [(sB, sB0 + 2)] if dc == 0 else []
            if not quantize:
                cstep(ctx, "vector",
                      (lambda dc: lambda v: v.tensor_tensor(
                          xb[:, dc * tn:(dc + 1) * tn],
                          xb[:, dc * tn:(dc + 1) * tn], rstd_b[:, :],
                          ALU.mult))(dc),
                      w, vN)
                cstep(ctx, "vector",
                      (lambda dc: lambda v: v.tensor_tensor(
                          xb[:, dc * tn:(dc + 1) * tn],
                          xb[:, dc * tn:(dc + 1) * tn], mrstd_b[:, :],
                          ALU.subtract))(dc),
                      [], vN)
            else:
                cstep(ctx, "vector",
                      (lambda dc: lambda v: v.tensor_tensor(
                          nws[:, :], xb[:, dc * tn:(dc + 1) * tn], rstd_b[:, :],
                          ALU.mult))(dc),
                      w, vN)
                cstep(ctx, "vector",
                      (lambda dc: lambda v: v.tensor_tensor(
                          nws[:, :], nws[:, :], mrstd_b[:, :], ALU.subtract))(dc),
                      [], vN)
                cstep(ctx, "vector",
                      (lambda dc: lambda v: v.tensor_scalar(
                          qws[:, 0:tn], nws[:, :], 0.33, 0.5, ALU.is_gt, ALU.mult))(dc),
                      [], vN)
                cstep(ctx, "vector",
                      (lambda dc: lambda v: v.tensor_scalar(
                          qws[:, tn:2 * tn], nws[:, :], -0.33, 0.5, ALU.is_lt,
                          ALU.mult))(dc),
                      [], vN)
                cstep(ctx, "vector",
                      (lambda dc: lambda v: v.tensor_tensor(
                          xb[:, dc * tn:(dc + 1) * tn], qws[:, 0:tn],
                          qws[:, tn:2 * tn], ALU.subtract))(dc),
                      [], vN)
    return (vN, vN.v)


def emit_residual_out(ctx, es, groups, mk_matmuls, extra_first_waits=()):
    nc = ctx.nc
    pe = ctx.sem("o_pe")
    ev = ctx.sem("o_ev")
    dx = [ctx.sem("o_dx%d" % i) for i in range(4)]
    ds = [ctx.sem("o_ds%d" % i) for i in range(4)]
    pe0, ev0 = pe.v, ev.v
    dx0 = [s.v for s in dx]
    ds0 = [s.v for s in ds]
    xt = sbuf(es, nc, uname("ro_xt"), [128, 4 * 512], F32)
    ot = sbuf(es, nc, uname("ro_ot"), [128, 4 * 512], F32)
    for g, grp in enumerate(groups):
        xi = g % 4
        w = [(ev, ev0 + g - 3)]
        if g < 4:
            w += list(extra_first_waits)
        dstep(ctx, "sync",
              (lambda grp, xi: lambda s: s.dma_start(
                  xt[:, xi * 512:(xi + 1) * 512], grp["x_src_ap"]))(grp, xi),
              w, dx[xi])
        w = [(ev, ev0 + g - 1)]
        if g < 2:
            w += list(extra_first_waits)
        cstep(ctx, "tensor", (lambda g: lambda t: mk_matmuls(g, t))(g), w, pe)
        w = [(pe, pe0 + g + 1), (dx[xi], dx0[xi] + 16 * (g // 4 + 1)),
             (ds[xi], ds0[xi] + 16 * (g // 4))]
        cstep(ctx, "vector",
              (lambda grp, xi: lambda v: v.tensor_tensor(
                  ot[:, xi * 512:(xi + 1) * 512], grp["ps"],
                  xt[:, xi * 512:(xi + 1) * 512], ALU.add))(grp, xi),
              w, ev)
        dstep(ctx, "sync",
              (lambda grp, xi: lambda s: s.dma_start(
                  grp["x_dst_ap"], ot[:, xi * 512:(xi + 1) * 512]))(grp, xi),
              [(ev, ev0 + g + 1)], ds[xi])
    return pe


def phase_join(ctx):
    join = ctx.sem("join")
    for eng in ("sync", "gpsimd", "tensor", "vector", "scalar"):
        waits = []
        if eng == "sync":
            for s in ctx.sems:
                if s is not join and s.v > 0:
                    waits.append((s, s.v))
        cstep(ctx, eng, lambda e: e.nop(), waits, join)
    target = join.v
    for eng in ("sync", "gpsimd", "tensor", "vector", "scalar"):
        ctx.add(eng, lambda e: e.nop(), [(join, target)], [])


def emit_attn(ctx, C, layer, x_src, x_dst, W):
    nc = ctx.nc
    causal = (layer == 0)
    with contextlib.ExitStack() as pes:
        xb = sbuf(pes, nc, uname("at_xn"), [128, NDC * T], BF16)
        zall = sbuf(pes, nc, uname("at_z"), [128, NH * T], BF16)
        q_sb = sbuf(pes, nc, uname("at_q"), [128, T], BF16)
        k_sb = sbuf(pes, nc, uname("at_k"), [128, T], BF16)
        vf_sb = sbuf(pes, nc, uname("at_vf"), [128, T], BF16)
        vt_sb = sbuf(pes, nc, uname("at_vt"), [128, T], BF16)
        wst = sbuf(pes, nc, uname("at_wst"), [128, 2 * 3 * T], BF16)
        dIc = ctx.sem("at_dIc")
        if layer == 0:
            mk = sbuf(pes, nc, uname("at_mask"), [128, 4 * 512], BF16)
            maskT = [mk[:, v * 512:(v + 1) * 512] for v in range(4)]
            for v in range(4):
                dstep(ctx, "sync",
                      (lambda v: lambda s: s.dma_start(
                          mk[:, v * 512:(v + 1) * 512], C["maskT_dram"][v]))(v),
                      [], dIc)
        else:
            cs = sbuf(pes, nc, uname("at_rope"), [64, 2 * T], BF16)
            ropeA = cs[:, 0:T]
            ropeB = cs[:, T:2 * T]
            dstep(ctx, "sync", lambda s: s.dma_start(cs[:, 0:T], C["rope_dram"][0]),
                  [], dIc)
            dstep(ctx, "sync", lambda s: s.dma_start(cs[:, T:2 * T], C["rope_dram"][1]),
                  [], dIc)
        dIc_done = dIc.v

        ln_done = emit_ln(ctx, C, x_src, xb, 0, T, quantize=False)

        dWp = [ctx.sem("at_dW0"), ctx.sem("at_dW1")]
        dB = ctx.sem("at_dB")
        tP = ctx.sem("at_tP")
        sE = ctx.sem("at_sE")
        vR = ctx.sem("at_vR")
        tT = ctx.sem("at_tT")
        tQ = ctx.sem("at_tQ")
        sX = ctx.sem("at_sX")
        vM = ctx.sem("at_vM")
        tA = ctx.sem("at_tA")
        tZ = ctx.sem("at_tZ")
        vL = ctx.sem("at_vL")
        vZ = ctx.sem("at_vZ")
        dWp0 = [s.v for s in dWp]
        dB0, tP0 = dB.v, tP.v
        evac_done_at = []

        psP = psum(pes, nc, uname("at_psP"), [128, 2048])
        psB2 = psum(pes, nc, uname("at_psB2"), [128, 1024])
        psT = psB2[:, 0:512].bitcast(BF16)
        psS = psum(pes, nc, uname("at_psS"), [128, 1024])

        def proj_bank(i):
            j = i % 6
            if j < 4:
                return psP[:, j * 512:(j + 1) * 512]
            return psB2[:, (j - 4) * 512:(j - 3) * 512]

        def z_ps(g):
            return psP[:, (g % 2) * 512:((g % 2) + 1) * 512]

        def l_ps(g):
            return psP[0:1, 1024 + (g % 2) * 512: 1024 + ((g % 2) + 1) * 512]

        def rl_ps(g):
            return psB2[:, (g % 2) * 512:((g % 2) + 1) * 512]

        with contextlib.ExitStack() as hes:
            att = sbuf(hes, nc, uname("at_att"), [128, 2 * 512], BF16)
            if layer == 0:
                bias_sb = sbuf(hes, nc, uname("at_bias"), [1, 3 * 128], F32)
            lrow = sbuf(hes, nc, uname("at_lr"), [1, 512], F32)
            rl_b = sbuf(hes, nc, uname("at_rlb"), [128, 512], F32)
            if layer == 1:
                rws = sbuf(hes, nc, uname("at_rws"), [128, T], F32)
                rw2 = sbuf(hes, nc, uname("at_rw2"), [32, 2048], F32)

            nb = 0
            for h in range(NH):
                wbuf = h % 2
                woff = wbuf * 3 * T
                w = [] if h < 2 else [(tP, tP0 + 12 * (h - 1))]
                for j, name in enumerate(("wq", "wk", "wv")):
                    src = W[name + "_slab"][h]
                    dstep(ctx, "gpsimd",
                          (lambda src, woff, j: lambda g: g.dma_start(
                              wst[:, woff + j * T: woff + (j + 1) * T], src))(src, woff, j),
                          w if j == 0 else [], dWp[h % 2])
                if layer == 0:
                    wb = [(tP, tP0 + 12 * h)]
                    if h == 0:
                        wb.append(ln_done)
                    dstep(ctx, "sync",
                          (lambda h: lambda s: s.dma_start(
                              bias_sb[0:1, :], W["bqkv"][h]))(h),
                          wb, dB)

                for j, dst in enumerate((q_sb, k_sb, vf_sb)):
                    for g in range(4):
                        ps = proj_bank(nb)
                        wts = [(dWp[h % 2], dWp0[h % 2] + 48 * (h // 2 + 1))]
                        if nb >= 6:
                            wts.append((sE, evac_done_at[nb - 6]))
                        if j == 0 and g == 0 and h > 0:
                            wts.append((vZ, vZ.v))
                            wts.append((vL, vL.v))
                        if h == 0 and j == 0 and g == 0:
                            wts.append(ln_done)
                        if layer == 0:
                            wts.append((dB, dB0 + 16 * (h + 1)))

                        def mkp(ps, woff, j, g):
                            def f(t):
                                last = None
                                for dc in range(NDC):
                                    last = t.matmul(
                                        ps,
                                        wst[:, woff + j * T + dc * 128:
                                            woff + j * T + (dc + 1) * 128],
                                        xb[:, dc * T + g * 512: dc * T + (g + 1) * 512],
                                        start=(dc == 0),
                                        stop=(dc == NDC - 1 and layer == 1))
                                if layer == 0:
                                    last = t.matmul(
                                        ps, bias_sb[0:1, j * 128:(j + 1) * 128],
                                        C["ones_f32"][0:1, 0:512],
                                        start=False, stop=True)
                                return last
                            return f
                        cstep(ctx, "tensor", mkp(ps, woff, j, g), wts, tP)
                        if layer == 1 and j < 2:
                            w2_ = [(tP, tP.v)]
                            if g == 0:
                                w2_.append((vR, vR.v))
                            cstep(ctx, "scalar",
                                  (lambda ps, g: lambda s: s.copy(
                                      rws[0:64, g * 512:(g + 1) * 512],
                                      ps[0:64, :]))(ps, g),
                                  w2_, sE)
                            cstep(ctx, "scalar",
                                  (lambda ps, dst, g: lambda s: s.copy(
                                      dst[64:128, g * 512:(g + 1) * 512],
                                      ps[64:128, :]))(ps, dst, g),
                                  [], sE)
                        else:
                            cstep(ctx, "scalar",
                                  (lambda ps, dst, g: lambda s: s.copy(
                                      dst[:, g * 512:(g + 1) * 512], ps[:, :]))(ps, dst, g),
                                  [(tP, tP.v)], sE)
                        evac_done_at.append(sE.v)
                        nb += 1
                    if layer == 1 and j < 2:
                        s0a = rw2[0:32, 0:1024]
                        s0b = rw2[0:32, 1024:2048]
                        for half in range(2):
                            sl = slice(half * 1024, (half + 1) * 1024)
                            w2_ = [(sE, sE.v), (dIc, dIc_done)] if half == 0 else []
                            cstep(ctx, "vector",
                                  (lambda sl: lambda v: v.tensor_tensor(
                                      s0a, rws[0:32, sl], ropeA[0:32, sl],
                                      ALU.mult))(sl),
                                  w2_, vR)
                            cstep(ctx, "vector",
                                  (lambda sl: lambda v: v.tensor_tensor(
                                      s0b, rws[32:64, sl], ropeA[32:64, sl],
                                      ALU.mult))(sl),
                                  [], vR)
                            cstep(ctx, "vector",
                                  (lambda sl, dst: lambda v: v.tensor_tensor(
                                      dst[0:32, sl], s0a, s0b, ALU.subtract))(sl, dst),
                                  [], vR)
                            cstep(ctx, "vector",
                                  (lambda sl: lambda v: v.tensor_tensor(
                                      s0a, rws[0:32, sl], ropeB[0:32, sl],
                                      ALU.mult))(sl),
                                  [], vR)
                            cstep(ctx, "vector",
                                  (lambda sl: lambda v: v.tensor_tensor(
                                      s0b, rws[32:64, sl], ropeB[32:64, sl],
                                      ALU.mult))(sl),
                                  [], vR)
                            cstep(ctx, "vector",
                                  (lambda sl, dst: lambda v: v.tensor_tensor(
                                      dst[32:64, sl], s0b, s0a, ALU.add))(sl, dst),
                                  [], vR)

                for r in range(2):
                    def tr(r):
                        def f(t):
                            last = None
                            for i in range(8):
                                tc = r * 8 + i
                                last = t.transpose(
                                    psT[:, i * 128:(i + 1) * 128],
                                    vf_sb[:, tc * 128:(tc + 1) * 128],
                                    C["ident_bf"][:, :])
                            return last
                        return f
                    w2_ = [(sE, sE.v)] if r == 0 else [(vR, vR.v)]
                    cstep(ctx, "tensor", tr(r), w2_, tT)
                    cstep(ctx, "vector",
                          (lambda r: lambda v: v.tensor_copy(
                              vt_sb[:, r * 1024:(r + 1) * 1024], psT[:, :]))(r),
                          [(tT, tT.v)], vR)

                qk_ready = [(sE, sE.v), (vR, vR.v)]
                ti = 0
                for g in range(4):
                    tc_hi = min(NDC - 1, 4 * g + 3) if causal else NDC - 1
                    tcs = list(range(tc_hi + 1))
                    for ii, tc in enumerate(tcs):
                        sps = psS[:, (ti % 2) * 512:((ti % 2) + 1) * 512]
                        w2_ = list(qk_ready) if ti == 0 else []
                        w2_.append((sX, sX.v - 1))
                        cstep(ctx, "tensor",
                              (lambda sps, tc, g: lambda t: t.matmul(
                                  sps, k_sb[:, tc * 128:(tc + 1) * 128],
                                  q_sb[:, g * 512:(g + 1) * 512],
                                  start=True, stop=True))(sps, tc, g),
                              w2_, tQ)
                        w2_ = [(tQ, tQ.v), (tA, tA.v - 1)]
                        cstep(ctx, "scalar",
                              (lambda sps, ti: lambda s: s.activation(
                                  att[:, (ti % 2) * 512:((ti % 2) + 1) * 512], sps,
                                  AF.Exp, bias=np.float32(-SHIFT),
                                  scale=np.float32(SCL)))(sps, ti),
                              w2_, sX)
                        dep = sX
                        if causal and tc >= 4 * g:
                            vvar = tc - 4 * g
                            cstep(ctx, "vector",
                                  (lambda ti, vvar: lambda v: v.tensor_tensor(
                                      att[:, (ti % 2) * 512:((ti % 2) + 1) * 512],
                                      att[:, (ti % 2) * 512:((ti % 2) + 1) * 512],
                                      maskT[vvar], ALU.mult))(ti, vvar),
                                  [(sX, sX.v), (dIc, dIc_done)], vM)
                            dep = vM
                        first = (ii == 0)
                        last = (ii == len(tcs) - 1)

                        def av(ti, tc, g, first, last):
                            def f(t):
                                t.matmul(l_ps(g), C["ones_bcol"][:, 0:1],
                                         att[:, (ti % 2) * 512:((ti % 2) + 1) * 512],
                                         start=first, stop=last)
                                return t.matmul(
                                    z_ps(g), vt_sb[:, tc * 128:(tc + 1) * 128],
                                    att[:, (ti % 2) * 512:((ti % 2) + 1) * 512],
                                    start=first, stop=last)
                            return f
                        w2_ = [(dep, dep.v)]
                        if first:
                            w2_.append((vZ, vZ.v - 1))
                            w2_.append((vL, vL.v - 3))
                        cstep(ctx, "tensor", av(ti, tc, g, first, last), w2_, tA)
                        ti += 1
                    cstep(ctx, "vector",
                          (lambda g: lambda v: v.tensor_copy(lrow[0:1, :], l_ps(g)))(g),
                          [(tA, tA.v)], vL)
                    cstep(ctx, "vector",
                          lambda v: v.reciprocal(lrow[0:1, :], lrow[0:1, :]),
                          [], vL)
                    cstep(ctx, "tensor",
                          (lambda g: lambda t: t.matmul(
                              rl_ps(g), C["ones_f32"][0:1, 0:128], lrow[0:1, 0:512],
                              start=True, stop=True))(g),
                          [(vL, vL.v)], tZ)
                    cstep(ctx, "scalar",
                          (lambda g: lambda s: s.copy(rl_b[:, :], rl_ps(g)))(g),
                          [(tZ, tZ.v), (vZ, vZ.v)], sE)
                    cstep(ctx, "vector",
                          (lambda g, h: lambda v: v.tensor_tensor(
                              zall[:, h * T + g * 512: h * T + (g + 1) * 512],
                              z_ps(g), rl_b[:, :], ALU.mult))(g, h),
                          [(sE, sE.v)], vZ)
            att_done = [(vZ, vZ.v)]

        with contextlib.ExitStack() as oes:
            dWo = [ctx.sem("at_dWo0"), ctx.sem("at_dWo1")]
            dWo0 = [s.v for s in dWo]
            oPE = ctx.sem("o_pe")
            oPE0 = oPE.v
            wos = sbuf(oes, nc, uname("at_wos"), [128, 2 * T], BF16)
            if layer == 0:
                bo_sb = sbuf(oes, nc, uname("at_bo"), [1, T], F32)
                dB2 = ctx.sem("at_dBo")
                dstep(ctx, "sync", lambda s: s.dma_start(bo_sb[0:1, :], W["bo"]),
                      list(att_done), dB2)
                bo_target = dB2.v

            groups = []
            for dc in range(NDC):
                w = [(oPE, oPE0 + 4 * (dc - 1))]
                if dc < 2:
                    w += list(att_done)
                dstep(ctx, "gpsimd",
                      (lambda dc: lambda g: g.dma_start(
                          wos[:, (dc % 2) * T:((dc % 2) + 1) * T], W["wo_slab"][dc]))(dc),
                      w, dWo[dc % 2])
                for g in range(4):
                    groups.append(dict(dc=dc, g=g))
            psO = psS
            for gi, grp in enumerate(groups):
                dc, g = grp["dc"], grp["g"]
                grp["ps"] = psO[:, (gi % 2) * 512:((gi % 2) + 1) * 512]
                grp["x_src_ap"] = x_src[dc * 128:(dc + 1) * 128, g * 512:(g + 1) * 512]
                grp["x_dst_ap"] = x_dst[dc * 128:(dc + 1) * 128, g * 512:(g + 1) * 512]

            def mk_out(gi, t):
                dc = groups[gi]["dc"]
                g = groups[gi]["g"]
                ps = groups[gi]["ps"]
                t.wait_ge(dWo[dc % 2].h, dWo0[dc % 2] + 16 * (dc // 2 + 1))
                if layer == 0 and gi == 0:
                    t.wait_ge(dB2.h, bo_target)
                last = None
                for h in range(NH):
                    last = t.matmul(
                        ps, wos[:, (dc % 2) * T + h * 128:(dc % 2) * T + (h + 1) * 128],
                        zall[:, h * T + g * 512: h * T + (g + 1) * 512],
                        start=(h == 0), stop=(h == NH - 1 and layer == 1))
                if layer == 0:
                    last = t.matmul(ps, bo_sb[0:1, dc * 128:(dc + 1) * 128],
                                    C["ones_f32"][0:1, 0:512], start=False, stop=True)
                return last

            emit_residual_out(ctx, oes, groups, mk_out, extra_first_waits=att_done)


def emit_l0ffn(ctx, C, x_src, x_dst, W):
    nc = ctx.nc
    with contextlib.ExitStack() as pes:
        xb = sbuf(pes, nc, uname("f0_xq"), [128, NDC * T], BF16)
        wraw = sbuf(pes, nc, uname("f0_wr"), [128, 2 * 2 * T], F32)
        wq = sbuf(pes, nc, uname("f0_wq"), [128, 2 * T], BF16)
        qt1 = sbuf(pes, nc, uname("f0_t1"), [128, T], F32)
        qt2 = sbuf(pes, nc, uname("f0_t2"), [128, T], F32)

        ln_done = emit_ln(ctx, C, x_src, xb, 0, T, quantize=True)

        dWf = [ctx.sem("f0_dW0"), ctx.sem("f0_dW1")]
        vQ = ctx.sem("f0_vQ")
        pe = ctx.sem("o_pe")
        ev = ctx.sem("o_ev")
        dx = [ctx.sem("o_dx%d" % i) for i in range(4)]
        ds = [ctx.sem("o_ds%d" % i) for i in range(4)]
        dWf0 = [s.v for s in dWf]
        vQ0, pe0, ev0 = vQ.v, pe.v, ev.v
        dx0 = [s.v for s in dx]
        ds0 = [s.v for s in ds]
        psO = psum(pes, nc, uname("f0_psO"), [128, 1024])
        xt = sbuf(pes, nc, uname("ro_xt"), [128, 4 * 512], F32)
        ot = sbuf(pes, nc, uname("ro_ot"), [128, 4 * 512], F32)

        for oc in range(NDC):
            w = [(vQ, vQ0 + 4 * (oc - 1))]
            dstep(ctx, "gpsimd",
                  (lambda oc: lambda g: g.dma_start(
                      wraw[:, (oc % 2) * 2 * T:(oc % 2) * 2 * T + T],
                      W["wT_slab"][oc]))(oc),
                  w, dWf[oc % 2])
            dstep(ctx, "gpsimd",
                  (lambda oc: lambda g: g.dma_start(
                      wraw[:, (oc % 2) * 2 * T + T:(oc % 2) * 2 * T + 2 * T],
                      W["rT_slab"][oc]))(oc),
                  [], dWf[oc % 2])
            wsl = wraw[:, (oc % 2) * 2 * T:(oc % 2) * 2 * T + T]
            rsl = wraw[:, (oc % 2) * 2 * T + T:(oc % 2) * 2 * T + 2 * T]
            qsl = wq[:, (oc % 2) * T:((oc % 2) + 1) * T]
            w = [(dWf[oc % 2], dWf0[oc % 2] + 32 * (oc // 2 + 1)),
                 (pe, pe0 + 4 * (oc - 1))]
            cstep(ctx, "vector",
                  (lambda wsl: lambda v: v.tensor_scalar(
                      qt1[:, :], wsl, 0.2, 0.6, ALU.is_gt, ALU.mult))(wsl), w, vQ)
            cstep(ctx, "vector",
                  (lambda wsl: lambda v: v.tensor_scalar(
                      qt2[:, :], wsl, -0.2, 0.6, ALU.is_lt, ALU.mult))(wsl), [], vQ)
            cstep(ctx, "vector",
                  lambda v: v.tensor_tensor(qt1[:, :], qt1[:, :], qt2[:, :],
                                            ALU.subtract), [], vQ)
            cstep(ctx, "vector",
                  (lambda rsl, qsl: lambda v: v.tensor_tensor(
                      qsl, qt1[:, :], rsl, ALU.add))(rsl, qsl), [], vQ)
            vq_ready = vQ.v
            for g in range(4):
                gi = 4 * oc + g
                xi = gi % 4
                ps = psO[:, (gi % 2) * 512:((gi % 2) + 1) * 512]
                x_src_ap = x_src[oc * 128:(oc + 1) * 128, g * 512:(g + 1) * 512]
                x_dst_ap = x_dst[oc * 128:(oc + 1) * 128, g * 512:(g + 1) * 512]
                w = [(ev, ev0 + gi - 3)]
                if gi < 4:
                    w.append(ln_done)
                dstep(ctx, "sync",
                      (lambda x_src_ap, xi: lambda s: s.dma_start(
                          xt[:, xi * 512:(xi + 1) * 512], x_src_ap))(x_src_ap, xi),
                      w, dx[xi])
                w = [(ev, ev0 + gi - 1), (vQ, vq_ready)]
                if gi == 0:
                    w.append(ln_done)

                def mk(ps, oc, g):
                    def f(t):
                        last = None
                        for dc in range(NDC):
                            last = t.matmul(
                                ps, wq[:, (oc % 2) * T + dc * 128:
                                       (oc % 2) * T + (dc + 1) * 128],
                                xb[:, dc * T + g * 512: dc * T + (g + 1) * 512],
                                start=(dc == 0), stop=(dc == NDC - 1))
                        return last
                    return f
                cstep(ctx, "tensor", mk(ps, oc, g), w, pe)
                w = [(pe, pe0 + gi + 1), (dx[xi], dx0[xi] + 16 * (gi // 4 + 1)),
                     (ds[xi], ds0[xi] + 16 * (gi // 4))]
                cstep(ctx, "vector",
                      (lambda ps, xi: lambda v: v.tensor_tensor(
                          ot[:, xi * 512:(xi + 1) * 512], ps,
                          xt[:, xi * 512:(xi + 1) * 512], ALU.add))(ps, xi),
                      w, ev)
                dstep(ctx, "sync",
                      (lambda x_dst_ap, xi: lambda s: s.dma_start(
                          x_dst_ap, ot[:, xi * 512:(xi + 1) * 512]))(x_dst_ap, xi),
                      [(ev, ev0 + gi + 1)], ds[xi])


def emit_l1ffn(ctx, C, x_src, x_dst, W):
    nc = ctx.nc
    with contextlib.ExitStack() as pes:
        acc = sbuf(pes, nc, uname("f1_acc"), [128, NDC * TG], F32)
        xb = sbuf(pes, nc, uname("f1_xn"), [128, NDC * TG], BF16)
        lg_tm = sbuf(pes, nc, uname("f1_lg"), [128, 8 * 16], F32)
        rr = sbuf(pes, nc, uname("f1_rr"), [128, 16], F32)
        cf = sbuf(pes, nc, uname("f1_cf"), [128, 16], F32)
        mrow = sbuf(pes, nc, uname("f1_mr"), [128, 2], F32)
        ctile = sbuf(pes, nc, uname("f1_ct"), [16, 8 * 128], F32)
        srow = sbuf(pes, nc, uname("f1_sr"), [1, TG], F32)
        gws = sbuf(pes, nc, uname("f1_gws"), [128, NDC * 16], BF16)
        sgs = sbuf(pes, nc, uname("f1_sgs"), [128, NDC], BF16)

        coef_dram = nc.dram_tensor(uname("coef_dram"), [17, TG], F32,
                                   kind="Internal").ap()

        dG = ctx.sem("f1_dG")
        dWsp = [ctx.sem("f1_dWs0"), ctx.sem("f1_dWs1")]
        dW2 = ctx.sem("f1_dW2")
        dCB = ctx.sem("f1_dCB")
        dCS = ctx.sem("f1_dCS")
        tR = ctx.sem("f1_tR")
        vRt = ctx.sem("f1_vRt")
        sR = ctx.sem("f1_sR")
        tU = ctx.sem("f1_tU")
        sU = ctx.sem("f1_sU")
        vU = ctx.sem("f1_vU")
        tD = ctx.sem("f1_tD")
        vD = ctx.sem("f1_vD")
        oEV = ctx.sem("o_ev")
        oDXs = [ctx.sem("o_dx%d" % i) for i in range(4)]
        oDSs = [ctx.sem("o_ds%d" % i) for i in range(4)]

        psU = psum(pes, nc, uname("f1_psU"), [128, 2 * TG])
        psD = psum(pes, nc, uname("f1_psD"), [128, 2 * TG])

        dstep(ctx, "gpsimd", lambda g: g.dma_start(gws[:, :], W["gwT"]), [], dG)
        dstep(ctx, "gpsimd", lambda g: g.dma_start(sgs[:, :], W["sgT"][:, :]),
              [], dG)
        dG_done = dG.v

        units = []
        for e in range(E_MOE):
            units.append(dict(kind="exp", e=e, gate_row=e))
        for s in range(4):
            units.append(dict(kind="sh", e=s, gate_row=16))

        for tgi in range(TG_PASSES):
            t0 = tgi * TG
            pre = [(tU, tU.v), (tD, tD.v), (vD, vD.v), (vU, vU.v), (sU, sU.v),
                   (oEV, oEV.v), (dCB, dCB.v)] + [(s, s.v) for s in oDSs]
            ln_done = emit_ln(ctx, C, x_src, xb, t0, TG, quantize=False, pre=pre,
                              ext=dict(psM=psU[0:1, 0:TG], psS=psU[0:1, TG:2 * TG],
                                       psBc=psD[:, 0:TG]))

            def rmm(t):
                last = None
                for g in range(TG // 512):
                    for dc in range(NDC):
                        last = t.matmul(
                            psU[0:16, g * 512:(g + 1) * 512],
                            gws[:, dc * 16:(dc + 1) * 16],
                            xb[:, dc * TG + g * 512: dc * TG + (g + 1) * 512],
                            start=(dc == 0), stop=(dc == NDC - 1))
                return last
            cstep(ctx, "tensor", rmm, [ln_done, (dG, dG_done)], tR)

            def smm(t):
                last = None
                for g in range(TG // 512):
                    for dc in range(NDC):
                        last = t.matmul(
                            psU[0:1, TG + g * 512: TG + (g + 1) * 512],
                            sgs[:, dc:dc + 1],
                            xb[:, dc * TG + g * 512: dc * TG + (g + 1) * 512],
                            start=(dc == 0), stop=(dc == NDC - 1))
                return last
            cstep(ctx, "tensor", smm, [], tR)
            cstep(ctx, "scalar", lambda s: s.activation(
                srow[0:1, :], psU[0:1, TG:2 * TG], AF.Sigmoid), [(tR, tR.v)], sR)
            dstep(ctx, "sync",
                  lambda s: s.dma_start(coef_dram[16:17, :], srow[0:1, :]),
                  [(sR, sR.v), (dCB, dCB.v)] + pre, dCS)
            cstep(ctx, "vector", lambda v: v.tensor_copy(
                ctile[0:16, :], psU[0:16, 0:TG]), [(sR, sR.v)], vRt)
            for tc in range(8):
                cstep(ctx, "tensor",
                      (lambda tc: lambda t: t.transpose(
                          psD[0:128, 0:16], ctile[0:16, tc * 128:(tc + 1) * 128],
                          C["ident_f32"][0:16, 0:16]))(tc),
                      [(vRt, vRt.v)], tR)
                cstep(ctx, "vector",
                      (lambda tc: lambda v: v.tensor_copy(
                          lg_tm[:, tc * 16:(tc + 1) * 16], psD[0:128, 0:16]))(tc),
                      [(tR, tR.v)], vRt)
            for tc in range(8):
                sl = slice(tc * 16, (tc + 1) * 16)
                cstep(ctx, "vector",
                      (lambda sl: lambda v: v.reduce_max(
                          mrow[:, 0:1], lg_tm[:, sl], axis=AX.X, negate=True))(sl),
                      [], vRt)
                cstep(ctx, "scalar",
                      (lambda sl: lambda s: s.activation(
                          rr[:, :], lg_tm[:, sl], AF.Exp, bias=mrow[:, 0:1],
                          accum_out=mrow[:, 1:2]))(sl),
                      [(vRt, vRt.v)], sR)
                cstep(ctx, "vector",
                      lambda v: v.reciprocal(mrow[:, 1:2], mrow[:, 1:2]),
                      [(sR, sR.v)], vRt)
                cstep(ctx, "vector", lambda v: v.tensor_single_scalar(
                    rr[:, :], rr[:, :], mrow[:, 1:2], ALU.mult), [], vRt)
                cstep(ctx, "vector", lambda v: v.memset(cf[:, :], 0.0), [], vRt)
                for r in range(TOPK):
                    cstep(ctx, "vector", lambda v: v.reduce_max(
                        mrow[:, 0:1], rr[:, :], axis=AX.X), [], vRt)
                    cstep(ctx, "vector", lambda v: v.tensor_single_scalar(
                        lg_tm[:, 0:16], rr[:, :], mrow[:, 0:1], ALU.is_ge), [], vRt)
                    cstep(ctx, "vector", lambda v: v.tensor_tensor(
                        lg_tm[:, 0:16], lg_tm[:, 0:16], rr[:, :], ALU.mult), [], vRt)
                    cstep(ctx, "vector", lambda v: v.tensor_tensor(
                        cf[:, :], cf[:, :], lg_tm[:, 0:16], ALU.add), [], vRt)
                    cstep(ctx, "vector", lambda v: v.tensor_tensor(
                        rr[:, :], rr[:, :], lg_tm[:, 0:16], ALU.subtract), [], vRt)
                cstep(ctx, "tensor",
                      (lambda tc: lambda t: t.transpose(
                          psD[0:16, 0:128], cf[:, 0:16], C["ident_f32"][:, :]))(tc),
                      [(vRt, vRt.v)], tR)
                cstep(ctx, "vector",
                      (lambda tc: lambda v: v.tensor_copy(
                          ctile[0:16, tc * 128:(tc + 1) * 128], psD[0:16, 0:128]))(tc),
                      [(tR, tR.v)], vRt)
            dstep(ctx, "sync",
                  lambda s: s.dma_start(coef_dram[0:16, :], ctile[0:16, :]),
                  [(vRt, vRt.v)], dCS)
            for i in range(NDC):
                w = [(oEV, oEV.v)] if i == 0 else []
                cstep(ctx, "vector",
                      (lambda i: lambda v: v.memset(acc[:, i * TG:(i + 1) * TG],
                                                    0.0))(i),
                      w, vRt)
            acc_zeroed = vRt.v
            coef_stored = dCS.v

            with contextlib.ExitStack() as ues:
                he = sbuf(ues, nc, uname("f1_h"), [128, NIC * TG], BF16)
                w2s = sbuf(ues, nc, uname("f1_w2"), [128, NIC * 2048], BF16)
                w13s = sbuf(ues, nc, uname("f1_w13"), [128, 4 * NDC * 128], BF16)
                coefb = sbuf(ues, nc, uname("f1_cb"), [128, TG], F32)
                sg = sbuf(ues, nc, uname("f1_sg"), [128, TG], F32)

                tU0u, tD0u, vD0u, vU0u = tU.v, tD.v, vD.v, vU.v
                pairno = [0]
                for ui, u in enumerate(units):
                    dstep(ctx, "sync",
                          (lambda u: lambda s: s.dma_start(
                              coefb[:, :],
                              coef_dram[u["gate_row"]:u["gate_row"] + 1, :]
                              .partition_broadcast(128)))(u),
                          [(dCS, coef_stored), (vU, vU.v)], dCB)
                    for ic in range(NIC):
                        w = []
                        if ic == 0:
                            w = [(tD, tD0u + 32 * ui)]
                            if ui == 0:
                                w.append(ln_done)
                        src = (W["w2T_sl"][u["e"]][ic] if u["kind"] == "exp"
                               else W["sdwT_sl"][u["e"]][ic])
                        dstep(ctx, "gpsimd",
                              (lambda src, ic: lambda g: g.dma_start(
                                  w2s[:, ic * 2048:(ic + 1) * 2048], src))(src, ic),
                              w, dW2)
                    w2_done = dW2.v
                    for p in range(NIC):
                        pn = pairno[0]
                        pb = pn % 2
                        w = [(tU, tU0u + pn - 1)]
                        if pn == 0:
                            w.append(ln_done)
                        for j in range(2):
                            if u["kind"] == "exp":
                                src = (W["w13g_sl"] if j == 0 else W["w13u_sl"])[u["e"]][p]
                            else:
                                src = (W["sgug_sl"] if j == 0 else W["sguu_sl"])[u["e"]][p]
                            dstep(ctx, "gpsimd",
                                  (lambda src, pb, j: lambda g: g.dma_start(
                                      w13s[:, (2 * pb + j) * NDC * 128:
                                           (2 * pb + j + 1) * NDC * 128],
                                      src))(src, pb, j),
                                  w if j == 0 else [], dWsp[pb])

                        def upmm(pb):
                            def f(t):
                                last = None
                                for j in range(2):
                                    ps = psU[:, j * TG:(j + 1) * TG]
                                    for dc in range(NDC):
                                        for g5 in range(TG // 512):
                                            last = t.matmul(
                                                ps[:, g5 * 512:(g5 + 1) * 512],
                                                w13s[:, (2 * pb + j) * NDC * 128 + dc * 128:
                                                     (2 * pb + j) * NDC * 128 + (dc + 1) * 128],
                                                xb[:, dc * TG + g5 * 512:
                                                   dc * TG + (g5 + 1) * 512],
                                                start=(dc == 0), stop=(dc == NDC - 1))
                                return last
                            return f
                        w = [(dWsp[pb], dWsp[pb].v), (sU, sU.v), (vU, vU.v)]
                        if ui == 0 and p == 0:
                            w.append(ln_done)
                        cstep(ctx, "tensor", upmm(pb), w, tU)
                        cstep(ctx, "scalar",
                              lambda s: s.activation(sg[:, :], psU[:, 0:TG], AF.Silu),
                              [(tU, tU.v)], sU)
                        w = [(sU, sU.v), (dCB, dCB.v)]
                        if p == 0:
                            w.append((tD, tD0u + 32 * ui))
                        cstep(ctx, "vector",
                              (lambda p: lambda v: v.tensor_tensor(
                                  he[:, p * TG:(p + 1) * TG], sg[:, :],
                                  psU[:, TG:2 * TG], ALU.mult))(p),
                              w, vU)
                        cstep(ctx, "vector",
                              (lambda p: lambda v: v.tensor_tensor(
                                  he[:, p * TG:(p + 1) * TG],
                                  he[:, p * TG:(p + 1) * TG],
                                  coefb[:, :], ALU.mult))(p),
                              [], vU)
                        pairno[0] += 1
                    for oc in range(NDC):
                        ri = oc % 2

                        def dnmm(ri, oc):
                            def f(t):
                                last = None
                                for g5 in range(TG // 512):
                                    for ic in range(NIC):
                                        last = t.matmul(
                                            psD[:, ri * TG + g5 * 512:
                                                ri * TG + (g5 + 1) * 512],
                                            w2s[:, ic * 2048 + oc * 128:
                                                ic * 2048 + (oc + 1) * 128],
                                            he[:, ic * TG + g5 * 512:
                                               ic * TG + (g5 + 1) * 512],
                                            start=(ic == 0), stop=(ic == NIC - 1))
                                return last
                            return f
                        w = [(dW2, w2_done), (vD, vD.v - 1)]
                        if oc == 0:
                            w.append((vU, vU.v))
                        cstep(ctx, "tensor", dnmm(ri, oc), w, tD, tD)
                        w = [(tD, tD.v)]
                        if ui == 0 and oc == 0:
                            w.append((vRt, acc_zeroed))
                        cstep(ctx, "vector",
                              (lambda ri, oc: lambda v: v.tensor_tensor(
                                  acc[:, oc * TG:(oc + 1) * TG],
                                  acc[:, oc * TG:(oc + 1) * TG],
                                  psD[:, ri * TG:(ri + 1) * TG], ALU.add))(ri, oc),
                              w, vD)
                units_done = (tD.v, vD.v)

            with contextlib.ExitStack() as fes:
                xt = sbuf(fes, nc, uname("f1_xt"), [128, 4 * 512], F32)
                ot = sbuf(fes, nc, uname("f1_ot"), [128, 4 * 512], F32)
                oEVb = oEV.v
                oDXb = [s.v for s in oDXs]
                oDSb = [s.v for s in oDSs]
                groups = []
                for oc in range(NDC):
                    for g in range(TG // 512):
                        groups.append((oc, g))
                for gi, (oc, g) in enumerate(groups):
                    xi = gi % 4
                    w = [(oEV, oEVb + gi - 3)]
                    if gi < 4:
                        w.append((tD, units_done[0]))
                    dstep(ctx, "sync",
                          (lambda oc, g, xi, t0: lambda s: s.dma_start(
                              xt[:, xi * 512:(xi + 1) * 512],
                              x_src[oc * 128:(oc + 1) * 128,
                                    t0 + g * 512: t0 + (g + 1) * 512]))(oc, g, xi, t0),
                          w, oDXs[xi])
                    w = [(oDXs[xi], oDXb[xi] + 16 * (gi // 4 + 1)), (vD, units_done[1]),
                         (oDSs[xi], oDSb[xi] + 16 * (gi // 4))]
                    cstep(ctx, "vector",
                          (lambda oc, g, xi: lambda v: v.tensor_tensor(
                              ot[:, xi * 512:(xi + 1) * 512],
                              acc[:, oc * TG + g * 512: oc * TG + (g + 1) * 512],
                              xt[:, xi * 512:(xi + 1) * 512], ALU.add))(oc, g, xi),
                          w, oEV)
                    dstep(ctx, "sync",
                          (lambda oc, g, xi, t0: lambda s: s.dma_start(
                              x_dst[oc * 128:(oc + 1) * 128,
                                    t0 + g * 512: t0 + (g + 1) * 512],
                              ot[:, xi * 512:(xi + 1) * 512]))(oc, g, xi, t0),
                          [(oEV, oEVb + gi + 1)], oDSs[xi])


def declare_consts(ctx, es, layer_set):
    nc = ctx.nc
    C = {}
    C["ones_bcol"] = sbuf(es, nc, "c_ones_b", [128, 1], BF16)
    C["ones_f32"] = sbuf(es, nc, "c_ones_f", [1, 512], F32)
    C["ident_bf"] = sbuf(es, nc, "c_id_bf", [128, 128], BF16)
    C["ident_f32"] = sbuf(es, nc, "c_id_f", [128, 128], F32)
    ident_bf = nc.dram_tensor("ident_bf", [128, 128], BF16, kind="ExternalInput").ap()
    ident_f = nc.dram_tensor("ident_f", [128, 128], F32, kind="ExternalInput").ap()
    dI = ctx.sem("c_dI")
    cM = ctx.sem("c_cM")
    dstep(ctx, "sync", lambda s: s.dma_start(C["ident_bf"][:, :], ident_bf[:, :]),
          [], dI)
    dstep(ctx, "sync", lambda s: s.dma_start(C["ident_f32"][:, :], ident_f[:, :]),
          [], dI)
    if "l0attn" in layer_set:
        C["maskT_dram"] = nc.dram_tensor("maskT", [4, 128, 512], BF16,
                                         kind="ExternalInput").ap()
    if "l1attn" in layer_set:
        C["rope_dram"] = nc.dram_tensor("rope", [2, 64, T], BF16,
                                        kind="ExternalInput").ap()
    cstep(ctx, "vector", lambda v: v.memset(C["ones_bcol"][:, :], 1.0),
          [(dI, dI.v)], cM)
    cstep(ctx, "vector", lambda v: v.memset(C["ones_f32"][:, :], 1.0), [], cM)
    ctx.add("tensor", lambda t: t.nop(), [(cM, cM.v), (dI, dI.v)], [])
    ctx.add("scalar", lambda t: t.nop(), [(cM, cM.v), (dI, dI.v)], [])
    ctx.add("vector", lambda t: t.nop(), [(dI, dI.v)], [])
    ctx.add("gpsimd", lambda t: t.nop(), [(cM, cM.v), (dI, dI.v)], [])
    return C


def declare_weight_aps(nc, phases):
    W = {}
    if "l0attn" in phases:
        wq = nc.dram_tensor("a0_wq", [NH, 128, D], F32, kind="ExternalInput").ap()
        wk = nc.dram_tensor("a0_wk", [NH, 128, D], F32, kind="ExternalInput").ap()
        wv = nc.dram_tensor("a0_wv", [NH, 128, D], F32, kind="ExternalInput").ap()
        wo = nc.dram_tensor("a0_wo", [NDC, 128, D], F32, kind="ExternalInput").ap()
        bqkv = nc.dram_tensor("a0_b", [NH, 1, 3 * 128], F32, kind="ExternalInput").ap()
        bo = nc.dram_tensor("a0_bo", [1, D], F32, kind="ExternalInput").ap()
        W["l0attn"] = dict(
            wq_slab=[wq[h] for h in range(NH)],
            wk_slab=[wk[h] for h in range(NH)],
            wv_slab=[wv[h] for h in range(NH)],
            wo_slab=[wo[dc] for dc in range(NDC)],
            bqkv=[bqkv[h] for h in range(NH)],
            bo=bo[0:1, :],
        )
    if "l0ffn" in phases:
        wT = nc.dram_tensor("f0_wT", [NDC, 128, D], F32, kind="ExternalInput").ap()
        rT = nc.dram_tensor("f0_rT", [NDC, 128, D], F32, kind="ExternalInput").ap()
        W["l0ffn"] = dict(
            wT_slab=[wT[oc] for oc in range(NDC)],
            rT_slab=[rT[oc] for oc in range(NDC)],
        )
    if "l1attn" in phases:
        wq = nc.dram_tensor("a1_wqT", [NH, 128, D], F32, kind="ExternalInput").ap()
        wk = nc.dram_tensor("a1_wkT", [NH, 128, D], F32, kind="ExternalInput").ap()
        wv = nc.dram_tensor("a1_wvT", [NH, 128, D], F32, kind="ExternalInput").ap()
        wo = nc.dram_tensor("a1_woT", [NDC, 128, D], F32, kind="ExternalInput").ap()
        W["l1attn"] = dict(
            wq_slab=[wq[h] for h in range(NH)],
            wk_slab=[wk[h] for h in range(NH)],
            wv_slab=[wv[h] for h in range(NH)],
            wo_slab=[wo[dc] for dc in range(NDC)],
        )
    if "l1ffn" in phases:
        gwT = nc.dram_tensor("f1_gwT", [128, NDC * 16], F32, kind="ExternalInput").ap()
        sgT = nc.dram_tensor("f1_sgT", [128, NDC], F32, kind="ExternalInput").ap()
        w13g = nc.dram_tensor("f1_w13g", [E_MOE, NIC, 128, D], F32,
                              kind="ExternalInput").ap()
        w13u = nc.dram_tensor("f1_w13u", [E_MOE, NIC, 128, D], F32,
                              kind="ExternalInput").ap()
        w2T = nc.dram_tensor("f1_w2T", [E_MOE, IE, D], F32, kind="ExternalInput").ap()
        sgug = nc.dram_tensor("f1_sgug", [4, NIC, 128, D], F32,
                              kind="ExternalInput").ap()
        sguu = nc.dram_tensor("f1_sguu", [4, NIC, 128, D], F32,
                              kind="ExternalInput").ap()
        sdwT = nc.dram_tensor("f1_sdwT", [ISH, D], F32, kind="ExternalInput").ap()
        W["l1ffn"] = dict(
            gwT=gwT[:, :],
            sgT=sgT[:, :],
            w13g_sl=[[w13g[e][p] for p in range(NIC)] for e in range(E_MOE)],
            w13u_sl=[[w13u[e][p] for p in range(NIC)] for e in range(E_MOE)],
            w2T_sl=[[w2T[e][ic * 128:(ic + 1) * 128, :] for ic in range(NIC)]
                    for e in range(E_MOE)],
            sgug_sl=[[sgug[s][p] for p in range(NIC)] for s in range(4)],
            sguu_sl=[[sguu[s][p] for p in range(NIC)] for s in range(4)],
            sdwT_sl=[[sdwT[s * IE + ic * 128: s * IE + (ic + 1) * 128, :]
                      for ic in range(NIC)] for s in range(4)],
        )
    return W


PHASE_FNS = dict(
    l0attn=lambda ctx, C, xs, xd, W: emit_attn(ctx, C, 0, xs, xd, W),
    l0ffn=emit_l0ffn,
    l1attn=lambda ctx, C, xs, xd, W: emit_attn(ctx, C, 1, xs, xd, W),
    l1ffn=emit_l1ffn,
)


DEBUG_MIDS = True


def build_program(phases):
    nc = bass.Bass("TRN2", target_bir_lowering=False)
    x_in = nc.dram_tensor("x_in", [D, T], F32, kind="ExternalInput").ap()
    x_out = nc.dram_tensor("x_out", [D, T], F32, kind="ExternalOutput").ap()
    xr = [x_in]
    for i in range(len(phases) - 1):
        xr.append(nc.dram_tensor("x_mid%d" % i, [D, T], F32,
                                 kind="ExternalOutput" if DEBUG_MIDS else "Internal").ap())
    xr.append(x_out)
    W = declare_weight_aps(nc, phases)
    with contextlib.ExitStack() as es:
        ctx = Ctx(nc, es)
        C = declare_consts(ctx, es, set(phases))
        for i, ph in enumerate(phases):
            PHASE_FNS[ph](ctx, C, xr[i], xr[i + 1], W[ph])
            phase_join(ctx)
        ctx.emit()
    return nc


def host_consts():
    ident = np.eye(128)
    maskT = np.zeros((4, 128, 512), dtype=ml_dtypes.bfloat16)
    tkl = np.arange(128)[:, None]
    tql = np.arange(512)[None, :]
    for v in range(4):
        maskT[v] = (tkl <= tql - 128 * v).astype(ml_dtypes.bfloat16)
    inv_freq = 1.0 / (10000.0 ** (np.arange(0, 128, 2, dtype=np.float64) / 128.0))
    t = np.arange(T, dtype=np.float64)
    ang = np.outer(inv_freq, t)
    sin, cos = np.sin(ang), np.cos(ang)
    ropeA = np.concatenate([cos[0:32], sin[0:32]], axis=0)
    ropeB = np.concatenate([sin[32:64], cos[32:64]], axis=0)
    rope = np.stack([ropeA, ropeB]).astype(ml_dtypes.bfloat16)
    return dict(ident_bf=ident.astype(ml_dtypes.bfloat16),
                ident_f=ident.astype(np.float32),
                maskT=maskT, rope=rope)


def host_weight_maps(inputs, phases):
    m = {}

    def tile_in_out(w_out_in, n_out_grp):
        O, I = w_out_in.shape
        G = O // n_out_grp if False else None
        return None

    if "l0attn" in phases:
        wq = inputs["L0_attn_W_Q"]
        wk = inputs["L0_attn_W_K"]
        wv = inputs["L0_attn_W_V"]
        wo = inputs["L0_attn_W_O"]
        def qkv(w):
            return np.ascontiguousarray(
                w.reshape(NH, NDC, 128, HD).transpose(0, 2, 1, 3).reshape(NH, 128, D))
        m["a0_wq"] = qkv(wq)
        m["a0_wk"] = qkv(wk)
        m["a0_wv"] = qkv(wv)
        m["a0_wo"] = np.ascontiguousarray(
            wo.reshape(NH, HD, NDC, 128).transpose(2, 1, 0, 3).reshape(NDC, 128, D))
        m["a0_b"] = np.ascontiguousarray(
            np.stack([inputs["L0_attn_b_Q"], inputs["L0_attn_b_K"],
                      inputs["L0_attn_b_V"]], axis=1).reshape(NH, 1, 3 * 128))
        m["a0_bo"] = np.ascontiguousarray(inputs["L0_attn_b_O"].reshape(1, D))
    if "l0ffn" in phases:
        def outin(w):
            return np.ascontiguousarray(
                w.reshape(NDC, 128, NDC, 128).transpose(0, 3, 2, 1).reshape(NDC, 128, D))
        m["f0_wT"] = outin(inputs["L0_ffn_weight"])
        m["f0_rT"] = outin(inputs["L0_ffn_rram_variation"])
    if "l1attn" in phases:
        def heads(w):
            return np.ascontiguousarray(
                w.reshape(NH, 128, NDC, 128).transpose(0, 3, 2, 1).reshape(NH, 128, D))
        m["a1_wqT"] = heads(inputs["L1_attn_wq"])
        m["a1_wkT"] = heads(inputs["L1_attn_wk"])
        m["a1_wvT"] = heads(inputs["L1_attn_wv"])
        m["a1_woT"] = np.ascontiguousarray(
            inputs["L1_attn_wo"].reshape(NDC, 128, NH, 128)
            .transpose(0, 3, 2, 1).reshape(NDC, 128, D))
    if "l1ffn" in phases:
        gw = inputs["L1_ffn_gate_w"]
        m["f1_gwT"] = np.ascontiguousarray(
            gw.reshape(16, NDC, 128).transpose(2, 1, 0).reshape(128, NDC * 16))
        m["f1_sgT"] = np.ascontiguousarray(
            inputs["L1_ffn_sgate_w"].reshape(NDC, 128).T)
        w13 = inputs["L1_ffn_w13"]
        def uptile(w3):
            E = w3.shape[0]
            return np.ascontiguousarray(
                w3.reshape(E, NIC, 128, NDC, 128).transpose(0, 1, 4, 3, 2)
                .reshape(E, NIC, 128, D))
        m["f1_w13g"] = uptile(w13[:, 0:IE, :])
        m["f1_w13u"] = uptile(w13[:, IE:2 * IE, :])
        m["f1_w2T"] = np.ascontiguousarray(inputs["L1_ffn_w2"].transpose(0, 2, 1))
        sgu = inputs["L1_ffn_shared_gate_up_w"]
        def uptile1(w3):
            return np.ascontiguousarray(
                w3.reshape(4, NIC, 128, NDC, 128).transpose(0, 1, 4, 3, 2)
                .reshape(4, NIC, 128, D))
        m["f1_sgug"] = uptile1(sgu[0:ISH])
        m["f1_sguu"] = uptile1(sgu[ISH:2 * ISH])
        m["f1_sdwT"] = np.ascontiguousarray(inputs["L1_ffn_shared_down_w"].T)
    return m


def const_maps(phases):
    hc = host_consts()
    m = dict(ident_bf=hc["ident_bf"], ident_f=hc["ident_f"])
    if "l0attn" in phases:
        m["maskT"] = hc["maskT"]
    if "l1attn" in phases:
        m["rope"] = hc["rope"]
    return m


def run_phases(x_percore, inputs, phases):
    nc = build_program(phases)
    wm = host_weight_maps(inputs, phases)
    cm = const_maps(phases)
    in_maps = []
    for c in range(8):
        d = dict(x_in=x_percore[c])
        d.update(wm)
        d.update(cm)
        in_maps.append(d)
    res = run_bass_kernel_spmd(nc, in_maps, core_ids=list(range(8)))
    return [res.results[c]["x_out"] for c in range(8)]


PHASE_ORDER = ["l0attn", "l0ffn", "l1attn", "l1ffn"]
FUSED = True


def kernel(**inputs):
    x = inputs["L0_attn_x"]
    x_percore = [np.ascontiguousarray(x[c].T) for c in range(8)]
    if FUSED:
        x_percore = run_phases(x_percore, inputs, PHASE_ORDER)
    else:
        for ph in PHASE_ORDER:
            x_percore = run_phases(x_percore, inputs, [ph])
    out = np.stack([np.ascontiguousarray(xc.T) for xc in x_percore])
    return np.ascontiguousarray(out.astype(np.float32))


# version 07
# speedup vs baseline: 1.1526x; 1.1464x over previous
import contextlib
import numpy as np
import ml_dtypes
import concourse.bass as bass
import concourse.mybir as mybir
from concourse.bass_utils import run_bass_kernel_spmd

F32 = mybir.dt.float32
BF16 = mybir.dt.bfloat16
AF = mybir.ActivationFunctionType
ALU = mybir.AluOpType
AX = mybir.AxisListType

D = 2048
T = 2048
NDC = D // 128
NH = 16
HD = 128
EPS = 1e-5
SCL = float(1.0 / np.sqrt(128.0))
SHIFT = 25.0
IE = 1408
NIC = IE // 128
ISH = 5632
TG = 1024
TG_PASSES = 2
E_MOE = 16
TOPK = 4
NGT = T // 512


class Sem:
    __slots__ = ("h", "v", "name")

    def __init__(self, h, name):
        self.h = h
        self.v = 0
        self.name = name


class Ctx:
    def __init__(self, nc, sem_es):
        self.nc = nc
        self.sem_es = sem_es
        self.steps = []
        self.sems = []
        self._semmap = {}
        self._owner = {}

    def sem(self, name):
        if name in self._semmap:
            return self._semmap[name]
        h = self.sem_es.enter_context(self.nc.semaphore(name))
        s = Sem(h, name)
        self.sems.append(s)
        self._semmap[name] = s
        return s

    def add(self, eng, fn, waits=(), incs=()):
        for (s, n, va) in incs:
            owner = self._owner.setdefault(s, eng)
            if owner != eng:
                self._owner[s] = None
        self.steps.append((eng, fn, list(waits), list(incs)))

    def emit(self):
        steps = self.steps
        with self.nc.Block() as blk:
            for eng_name, reg in (("sync", blk.sync), ("gpsimd", blk.gpsimd),
                                  ("tensor", blk.tensor), ("vector", blk.vector),
                                  ("scalar", blk.scalar)):
                def mk(nm):
                    def f(eng):
                        chain = None
                        for (e, fn, waits, incs) in steps:
                            if e != nm:
                                continue
                            assert len(incs) <= 1, (nm, incs)
                            for (s, val) in waits:
                                if val > 0:
                                    eng.wait_ge(s.h, val)
                            if nm in ("vector", "scalar") and chain is not None:
                                eng.wait_ge(chain[0].h, chain[1])
                            inst = fn(eng)
                            for (s, n, va) in incs:
                                inst.then_inc(s.h, n)
                                if self._owner.get(s) == nm:
                                    chain = (s, va)
                    return f
                reg(mk(eng_name))


def dstep(ctx, eng, fn, waits, sem):
    sem.v += 16
    ctx.add(eng, fn, waits, [(sem, 16, sem.v)])


def cstep(ctx, eng, fn, waits, *sems):
    merged = {}
    order = []
    for s in sems:
        s.v += 1
        if s not in merged:
            merged[s] = 0
            order.append(s)
        merged[s] += 1
    ctx.add(eng, fn, waits, [(s, merged[s], s.v) for s in order])


def sbuf(es, nc, name, shape, dtype):
    return es.enter_context(nc.sbuf_tensor(name, shape, dtype)).ap()


def psum(es, nc, name, shape, dtype=F32):
    return es.enter_context(nc.psum_tensor(name, shape, dtype)).ap()


_uid = [0]


def uname(base):
    _uid[0] += 1
    return "%s_%d" % (base, _uid[0])


def emit_ln(ctx, C, x_src, xb, t0, tn, quantize=False, pre=(), ext=None):
    nc = ctx.nc
    dL = ctx.sem("ln_dL")
    sQ = ctx.sem("ln_sQ")
    tS = ctx.sem("ln_tS")
    vS = ctx.sem("ln_vS")
    tB = ctx.sem("ln_tB")
    sB = ctx.sem("ln_sB")
    vN = ctx.sem("ln_vN")
    dL0, sQ0, tS0, vS0, tB0, sB0, vN0 = dL.v, sQ.v, tS.v, vS.v, tB.v, sB.v, vN.v

    NG = tn // 512
    pre = list(pre)
    with contextlib.ExitStack() as les:
        sqws = sbuf(les, nc, uname("ln_sq"), [128, tn], BF16)
        rows = sbuf(les, nc, uname("ln_rows"), [128, 2 * tn], F32)
        seg_m = rows[0:1, 0:tn]
        seg_r = rows[0:1, tn:2 * tn]
        rstd_b = sbuf(les, nc, uname("ln_rb"), [128, tn], BF16)
        mrstd_b = sbuf(les, nc, uname("ln_mb"), [128, tn], BF16)
        if quantize:
            nws = sbuf(les, nc, uname("ln_nws"), [128, tn], F32)
            qws = sbuf(les, nc, uname("ln_qws"), [128, 2 * tn], F32)

        stats_scope = contextlib.ExitStack()
        if ext is None:
            psM = psum(stats_scope, nc, uname("ln_psM"), [1, tn])
            psS = psum(stats_scope, nc, uname("ln_psS"), [1, tn])
        else:
            psM, psS = ext["psM"], ext["psS"]

        dstep(ctx, "gpsimd",
              lambda g: g.dma_start(
                  xb[:, :].rearrange("p (dc t) -> p dc t", dc=NDC),
                  x_src[:, t0:t0 + tn].rearrange("(dc p) t -> p dc t", p=128)),
              list(pre), dL)
        dL_all = dL0 + 16
        for dc in range(NDC):
            w = [(dL, dL_all)]
            if dc == 0:
                w += pre
            if dc >= 1:
                w.append((tS, tS0 + 2 * dc))
            cstep(ctx, "scalar",
                  (lambda dc: lambda s: s.activation(
                      sqws[:, :], xb[:, dc * tn:(dc + 1) * tn], AF.Square))(dc),
                  w, sQ)

            def mmm(dc):
                def f(t):
                    last = None
                    for g in range(NG):
                        last = t.matmul(psM[0:1, g * 512:(g + 1) * 512],
                                        C["ones_bcol"][:, 0:1],
                                        xb[:, dc * tn + g * 512: dc * tn + (g + 1) * 512],
                                        start=(dc == 0), stop=(dc == NDC - 1))
                    return last
                return f

            def smm(dc):
                def f(t):
                    last = None
                    for g in range(NG):
                        last = t.matmul(psS[0:1, g * 512:(g + 1) * 512],
                                        C["ones_bcol"][:, 0:1],
                                        sqws[:, g * 512:(g + 1) * 512],
                                        start=(dc == 0), stop=(dc == NDC - 1))
                    return last
                return f
            w = [(dL, dL_all)]
            if dc == 0:
                w += pre
            cstep(ctx, "tensor", mmm(dc), w, tS)
            cstep(ctx, "tensor", smm(dc), [(sQ, sQ0 + dc + 1)], tS)
        p1 = psM[0:1, 0:tn]
        w = [(tS, tS.v)] + pre
        cstep(ctx, "vector", lambda v: v.tensor_scalar(
            seg_m, psM[0:1, :], 1.0 / D, None, ALU.mult), w, vS)
        cstep(ctx, "vector", lambda v: v.tensor_scalar(
            seg_r, psS[0:1, :], 1.0 / D, None, ALU.mult), [], vS)
        cstep(ctx, "vector", lambda v: v.tensor_tensor(
            p1, seg_m, seg_m, ALU.mult), [], vS)
        cstep(ctx, "vector", lambda v: v.tensor_tensor(
            seg_r, seg_r, p1, ALU.subtract), [], vS)
        cstep(ctx, "vector", lambda v: v.tensor_scalar(
            seg_r, seg_r, float(EPS), None, ALU.add), [], vS)
        cstep(ctx, "scalar", lambda s: s.activation(
            p1, seg_r, AF.Sqrt), [(vS, vS.v)], sQ)
        cstep(ctx, "vector", lambda v: v.reciprocal(seg_r, p1),
              [(sQ, sQ.v)], vS)
        cstep(ctx, "vector", lambda v: v.tensor_tensor(
            seg_m, seg_m, seg_r, ALU.mult), [], vS)
        stats_done = vS.v
        stats_scope.close()

        bc_scope = contextlib.ExitStack()
        if ext is None:
            psBc = psum(bc_scope, nc, uname("ln_psBc"), [128, tn])
        else:
            psBc = ext["psBc"]

        def bc(row):
            def f(t):
                last = None
                for g in range(NG):
                    last = t.matmul(psBc[:, g * 512:(g + 1) * 512],
                                    C["ones_f32"][0:1, 0:128],
                                    row[0:1, g * 512:(g + 1) * 512],
                                    start=True, stop=True)
                return last
            return f
        cstep(ctx, "tensor", bc(seg_r), [(vS, stats_done)], tB)
        cstep(ctx, "scalar", lambda s: s.copy(rstd_b[:, :], psBc[:, :]),
              [(tB, tB0 + 1)], sB)
        cstep(ctx, "tensor", bc(seg_m), [(sB, sB0 + 1)], tB)
        cstep(ctx, "scalar", lambda s: s.copy(mrstd_b[:, :], psBc[:, :]),
              [(tB, tB0 + 2)], sB)
        bc_scope.close()
        for dc in range(NDC):
            w = [(sB, sB0 + 2)] if dc == 0 else []
            if not quantize:
                cstep(ctx, "vector",
                      (lambda dc: lambda v: v.tensor_tensor(
                          xb[:, dc * tn:(dc + 1) * tn],
                          xb[:, dc * tn:(dc + 1) * tn], rstd_b[:, :],
                          ALU.mult))(dc),
                      w, vN)
                cstep(ctx, "vector",
                      (lambda dc: lambda v: v.tensor_tensor(
                          xb[:, dc * tn:(dc + 1) * tn],
                          xb[:, dc * tn:(dc + 1) * tn], mrstd_b[:, :],
                          ALU.subtract))(dc),
                      [], vN)
            else:
                cstep(ctx, "vector",
                      (lambda dc: lambda v: v.tensor_tensor(
                          nws[:, :], xb[:, dc * tn:(dc + 1) * tn], rstd_b[:, :],
                          ALU.mult))(dc),
                      w, vN)
                cstep(ctx, "vector",
                      (lambda dc: lambda v: v.tensor_tensor(
                          nws[:, :], nws[:, :], mrstd_b[:, :], ALU.subtract))(dc),
                      [], vN)
                cstep(ctx, "vector",
                      (lambda dc: lambda v: v.tensor_scalar(
                          qws[:, 0:tn], nws[:, :], 0.33, 0.5, ALU.is_gt, ALU.mult))(dc),
                      [], vN)
                cstep(ctx, "vector",
                      (lambda dc: lambda v: v.tensor_scalar(
                          qws[:, tn:2 * tn], nws[:, :], -0.33, 0.5, ALU.is_lt,
                          ALU.mult))(dc),
                      [], vN)
                cstep(ctx, "vector",
                      (lambda dc: lambda v: v.tensor_tensor(
                          xb[:, dc * tn:(dc + 1) * tn], qws[:, 0:tn],
                          qws[:, tn:2 * tn], ALU.subtract))(dc),
                      [], vN)
    return (vN, vN.v)


def emit_residual_out(ctx, es, groups, mk_matmuls, extra_first_waits=()):
    nc = ctx.nc
    pe = ctx.sem("o_pe")
    ev = ctx.sem("o_ev")
    dx = [ctx.sem("o_dx%d" % i) for i in range(4)]
    ds = [ctx.sem("o_ds%d" % i) for i in range(4)]
    pe0, ev0 = pe.v, ev.v
    dx0 = [s.v for s in dx]
    ds0 = [s.v for s in ds]
    xt = sbuf(es, nc, uname("ro_xt"), [128, 4 * 512], F32)
    ot = sbuf(es, nc, uname("ro_ot"), [128, 4 * 512], F32)
    for g, grp in enumerate(groups):
        xi = g % 4
        w = [(ev, ev0 + g - 3)]
        if g < 4:
            w += list(extra_first_waits)
        dstep(ctx, "sync",
              (lambda grp, xi: lambda s: s.dma_start(
                  xt[:, xi * 512:(xi + 1) * 512], grp["x_src_ap"]))(grp, xi),
              w, dx[xi])
        w = [(ev, ev0 + g - 1)]
        if g < 2:
            w += list(extra_first_waits)
        cstep(ctx, "tensor", (lambda g: lambda t: mk_matmuls(g, t))(g), w, pe)
        w = [(pe, pe0 + g + 1), (dx[xi], dx0[xi] + 16 * (g // 4 + 1)),
             (ds[xi], ds0[xi] + 16 * (g // 4))]
        cstep(ctx, "vector",
              (lambda grp, xi: lambda v: v.tensor_tensor(
                  ot[:, xi * 512:(xi + 1) * 512], grp["ps"],
                  xt[:, xi * 512:(xi + 1) * 512], ALU.add))(grp, xi),
              w, ev)
        dstep(ctx, "sync",
              (lambda grp, xi: lambda s: s.dma_start(
                  grp["x_dst_ap"], ot[:, xi * 512:(xi + 1) * 512]))(grp, xi),
              [(ev, ev0 + g + 1)], ds[xi])
    return pe


def phase_join(ctx):
    join = ctx.sem("join")
    for eng in ("sync", "gpsimd", "tensor", "vector", "scalar"):
        waits = []
        if eng == "sync":
            for s in ctx.sems:
                if s is not join and s.v > 0:
                    waits.append((s, s.v))
        cstep(ctx, eng, lambda e: e.nop(), waits, join)
    target = join.v
    for eng in ("sync", "gpsimd", "tensor", "vector", "scalar"):
        ctx.add(eng, lambda e: e.nop(), [(join, target)], [])


def emit_attn(ctx, C, layer, x_src, x_dst, W):
    nc = ctx.nc
    causal = (layer == 0)
    with contextlib.ExitStack() as pes:
        xb = sbuf(pes, nc, uname("at_xn"), [128, NDC * T], BF16)
        zall = sbuf(pes, nc, uname("at_z"), [128, NH * T], BF16)
        q_sb = sbuf(pes, nc, uname("at_q"), [128, T], BF16)
        k_sb = sbuf(pes, nc, uname("at_k"), [128, T], BF16)
        vf_sb = sbuf(pes, nc, uname("at_vf"), [128, T], BF16)
        vt_sb = sbuf(pes, nc, uname("at_vt"), [128, T], BF16)
        wst = sbuf(pes, nc, uname("at_wst"), [128, 2 * 3 * T], BF16)
        dIc = ctx.sem("at_dIc")
        if layer == 0:
            mk = sbuf(pes, nc, uname("at_mask"), [128, 4 * 512], BF16)
            maskT = [mk[:, v * 512:(v + 1) * 512] for v in range(4)]
            for v in range(4):
                dstep(ctx, "sync",
                      (lambda v: lambda s: s.dma_start(
                          mk[:, v * 512:(v + 1) * 512], C["maskT_dram"][v]))(v),
                      [], dIc)
        else:
            cs = sbuf(pes, nc, uname("at_rope"), [64, 2 * T], BF16)
            ropeA = cs[:, 0:T]
            ropeB = cs[:, T:2 * T]
            dstep(ctx, "sync", lambda s: s.dma_start(cs[:, 0:T], C["rope_dram"][0]),
                  [], dIc)
            dstep(ctx, "sync", lambda s: s.dma_start(cs[:, T:2 * T], C["rope_dram"][1]),
                  [], dIc)
        dIc_done = dIc.v

        ln_done = emit_ln(ctx, C, x_src, xb, 0, T, quantize=False)

        dWp = [ctx.sem("at_dW0"), ctx.sem("at_dW1")]
        dB = ctx.sem("at_dB")
        tP = ctx.sem("at_tP")
        sE = ctx.sem("at_sE")
        vR = ctx.sem("at_vR")
        tT = ctx.sem("at_tT")
        tQ = ctx.sem("at_tQ")
        sX = ctx.sem("at_sX")
        vM = ctx.sem("at_vM")
        tA = ctx.sem("at_tA")
        tZ = ctx.sem("at_tZ")
        vL = ctx.sem("at_vL")
        vZ = ctx.sem("at_vZ")
        dWp0 = [s.v for s in dWp]
        dB0, tP0 = dB.v, tP.v
        evac_done_at = []

        psP = psum(pes, nc, uname("at_psP"), [128, 2048])
        psB2 = psum(pes, nc, uname("at_psB2"), [128, 1024])
        psT = psB2[:, 0:512].bitcast(BF16)
        psS = psum(pes, nc, uname("at_psS"), [128, 1024])

        def proj_bank(i):
            j = i % 6
            if j < 4:
                return psP[:, j * 512:(j + 1) * 512]
            return psB2[:, (j - 4) * 512:(j - 3) * 512]

        def z_ps(g):
            return psP[:, (g % 2) * 512:((g % 2) + 1) * 512]

        def l_ps(g):
            return psP[0:1, 1024 + (g % 2) * 512: 1024 + ((g % 2) + 1) * 512]

        def rl_ps(g):
            return psB2[:, (g % 2) * 512:((g % 2) + 1) * 512]

        with contextlib.ExitStack() as hes:
            att = sbuf(hes, nc, uname("at_att"), [128, 2 * 512], BF16)
            if layer == 0:
                bias_sb = sbuf(hes, nc, uname("at_bias"), [1, 3 * 128], F32)
            lrow = sbuf(hes, nc, uname("at_lr"), [1, 512], F32)
            rl_b = sbuf(hes, nc, uname("at_rlb"), [128, 512], F32)
            if layer == 1:
                rws = sbuf(hes, nc, uname("at_rws"), [128, T], F32)
                rw2 = sbuf(hes, nc, uname("at_rw2"), [32, 2048], F32)

            nb = 0
            for h in range(NH):
                wbuf = h % 2
                woff = wbuf * 3 * T
                w = [] if h < 2 else [(tP, tP0 + 12 * (h - 1))]
                dstep(ctx, "gpsimd",
                      (lambda h, woff: lambda g: g.dma_start(
                          wst[:, woff: woff + 3 * T], W["wqkv_slab"][h]))(h, woff),
                      w, dWp[h % 2])
                if layer == 0:
                    wb = [(tP, tP0 + 12 * h)]
                    if h == 0:
                        wb.append(ln_done)
                    dstep(ctx, "sync",
                          (lambda h: lambda s: s.dma_start(
                              bias_sb[0:1, :], W["bqkv"][h]))(h),
                          wb, dB)

                for j, dst in enumerate((q_sb, k_sb, vf_sb)):
                    for g in range(4):
                        ps = proj_bank(nb)
                        wts = [(dWp[h % 2], dWp0[h % 2] + 16 * (h // 2 + 1))]
                        if nb >= 6:
                            wts.append((sE, evac_done_at[nb - 6]))
                        if j == 0 and g == 0 and h > 0:
                            wts.append((vZ, vZ.v))
                            wts.append((vL, vL.v))
                        if h == 0 and j == 0 and g == 0:
                            wts.append(ln_done)
                        if layer == 0:
                            wts.append((dB, dB0 + 16 * (h + 1)))

                        def mkp(ps, woff, j, g):
                            def f(t):
                                last = None
                                for dc in range(NDC):
                                    last = t.matmul(
                                        ps,
                                        wst[:, woff + j * T + dc * 128:
                                            woff + j * T + (dc + 1) * 128],
                                        xb[:, dc * T + g * 512: dc * T + (g + 1) * 512],
                                        start=(dc == 0),
                                        stop=(dc == NDC - 1 and layer == 1))
                                if layer == 0:
                                    last = t.matmul(
                                        ps, bias_sb[0:1, j * 128:(j + 1) * 128],
                                        C["ones_f32"][0:1, 0:512],
                                        start=False, stop=True)
                                return last
                            return f
                        cstep(ctx, "tensor", mkp(ps, woff, j, g), wts, tP)
                        if layer == 1 and j < 2:
                            w2_ = [(tP, tP.v)]
                            if g == 0:
                                w2_.append((vR, vR.v))
                            cstep(ctx, "scalar",
                                  (lambda ps, g: lambda s: s.copy(
                                      rws[0:64, g * 512:(g + 1) * 512],
                                      ps[0:64, :]))(ps, g),
                                  w2_, sE)
                            cstep(ctx, "scalar",
                                  (lambda ps, dst, g: lambda s: s.copy(
                                      dst[64:128, g * 512:(g + 1) * 512],
                                      ps[64:128, :]))(ps, dst, g),
                                  [], sE)
                        else:
                            cstep(ctx, "scalar",
                                  (lambda ps, dst, g: lambda s: s.copy(
                                      dst[:, g * 512:(g + 1) * 512], ps[:, :]))(ps, dst, g),
                                  [(tP, tP.v)], sE)
                        evac_done_at.append(sE.v)
                        nb += 1
                    if layer == 1 and j < 2:
                        s0a = rw2[0:32, 0:1024]
                        s0b = rw2[0:32, 1024:2048]
                        for half in range(2):
                            sl = slice(half * 1024, (half + 1) * 1024)
                            w2_ = [(sE, sE.v), (dIc, dIc_done)] if half == 0 else []
                            cstep(ctx, "vector",
                                  (lambda sl: lambda v: v.tensor_tensor(
                                      s0a, rws[0:32, sl], ropeA[0:32, sl],
                                      ALU.mult))(sl),
                                  w2_, vR)
                            cstep(ctx, "vector",
                                  (lambda sl: lambda v: v.tensor_tensor(
                                      s0b, rws[32:64, sl], ropeA[32:64, sl],
                                      ALU.mult))(sl),
                                  [], vR)
                            cstep(ctx, "vector",
                                  (lambda sl, dst: lambda v: v.tensor_tensor(
                                      dst[0:32, sl], s0a, s0b, ALU.subtract))(sl, dst),
                                  [], vR)
                            cstep(ctx, "vector",
                                  (lambda sl: lambda v: v.tensor_tensor(
                                      s0a, rws[0:32, sl], ropeB[0:32, sl],
                                      ALU.mult))(sl),
                                  [], vR)
                            cstep(ctx, "vector",
                                  (lambda sl: lambda v: v.tensor_tensor(
                                      s0b, rws[32:64, sl], ropeB[32:64, sl],
                                      ALU.mult))(sl),
                                  [], vR)
                            cstep(ctx, "vector",
                                  (lambda sl, dst: lambda v: v.tensor_tensor(
                                      dst[32:64, sl], s0b, s0a, ALU.add))(sl, dst),
                                  [], vR)

                for r in range(2):
                    def tr(r):
                        def f(t):
                            last = None
                            for i in range(8):
                                tc = r * 8 + i
                                last = t.transpose(
                                    psT[:, i * 128:(i + 1) * 128],
                                    vf_sb[:, tc * 128:(tc + 1) * 128],
                                    C["ident_bf"][:, :])
                            return last
                        return f
                    w2_ = [(sE, sE.v)] if r == 0 else [(vR, vR.v)]
                    cstep(ctx, "tensor", tr(r), w2_, tT)
                    cstep(ctx, "vector",
                          (lambda r: lambda v: v.tensor_copy(
                              vt_sb[:, r * 1024:(r + 1) * 1024], psT[:, :]))(r),
                          [(tT, tT.v)], vR)

                qk_ready = [(sE, sE.v), (vR, vR.v)]
                ti = 0
                for g in range(4):
                    tc_hi = min(NDC - 1, 4 * g + 3) if causal else NDC - 1
                    tcs = list(range(tc_hi + 1))
                    for ii, tc in enumerate(tcs):
                        sps = psS[:, (ti % 2) * 512:((ti % 2) + 1) * 512]
                        w2_ = list(qk_ready) if ti == 0 else []
                        w2_.append((sX, sX.v - 1))
                        cstep(ctx, "tensor",
                              (lambda sps, tc, g: lambda t: t.matmul(
                                  sps, k_sb[:, tc * 128:(tc + 1) * 128],
                                  q_sb[:, g * 512:(g + 1) * 512],
                                  start=True, stop=True))(sps, tc, g),
                              w2_, tQ)
                        w2_ = [(tQ, tQ.v), (tA, tA.v - 1)]
                        cstep(ctx, "scalar",
                              (lambda sps, ti: lambda s: s.activation(
                                  att[:, (ti % 2) * 512:((ti % 2) + 1) * 512], sps,
                                  AF.Exp, bias=np.float32(-SHIFT),
                                  scale=np.float32(SCL)))(sps, ti),
                              w2_, sX)
                        dep = sX
                        if causal and tc >= 4 * g:
                            vvar = tc - 4 * g
                            cstep(ctx, "vector",
                                  (lambda ti, vvar: lambda v: v.tensor_tensor(
                                      att[:, (ti % 2) * 512:((ti % 2) + 1) * 512],
                                      att[:, (ti % 2) * 512:((ti % 2) + 1) * 512],
                                      maskT[vvar], ALU.mult))(ti, vvar),
                                  [(sX, sX.v), (dIc, dIc_done)], vM)
                            dep = vM
                        first = (ii == 0)
                        last = (ii == len(tcs) - 1)

                        def av(ti, tc, g, first, last):
                            def f(t):
                                t.matmul(l_ps(g), C["ones_bcol"][:, 0:1],
                                         att[:, (ti % 2) * 512:((ti % 2) + 1) * 512],
                                         start=first, stop=last)
                                return t.matmul(
                                    z_ps(g), vt_sb[:, tc * 128:(tc + 1) * 128],
                                    att[:, (ti % 2) * 512:((ti % 2) + 1) * 512],
                                    start=first, stop=last)
                            return f
                        w2_ = [(dep, dep.v)]
                        if first:
                            w2_.append((vZ, vZ.v - 1))
                            w2_.append((vL, vL.v - 3))
                        cstep(ctx, "tensor", av(ti, tc, g, first, last), w2_, tA)
                        ti += 1
                    cstep(ctx, "vector",
                          (lambda g: lambda v: v.tensor_copy(lrow[0:1, :], l_ps(g)))(g),
                          [(tA, tA.v)], vL)
                    cstep(ctx, "vector",
                          lambda v: v.reciprocal(lrow[0:1, :], lrow[0:1, :]),
                          [], vL)
                    cstep(ctx, "tensor",
                          (lambda g: lambda t: t.matmul(
                              rl_ps(g), C["ones_f32"][0:1, 0:128], lrow[0:1, 0:512],
                              start=True, stop=True))(g),
                          [(vL, vL.v)], tZ)
                    cstep(ctx, "scalar",
                          (lambda g: lambda s: s.copy(rl_b[:, :], rl_ps(g)))(g),
                          [(tZ, tZ.v), (vZ, vZ.v)], sE)
                    cstep(ctx, "vector",
                          (lambda g, h: lambda v: v.tensor_tensor(
                              zall[:, h * T + g * 512: h * T + (g + 1) * 512],
                              z_ps(g), rl_b[:, :], ALU.mult))(g, h),
                          [(sE, sE.v)], vZ)
            att_done = [(vZ, vZ.v)]

        with contextlib.ExitStack() as oes:
            dWo = [ctx.sem("at_dWo0"), ctx.sem("at_dWo1")]
            dWo0 = [s.v for s in dWo]
            oPE = ctx.sem("o_pe")
            oPE0 = oPE.v
            wos = sbuf(oes, nc, uname("at_wos"), [128, 2 * T], BF16)
            if layer == 0:
                bo_sb = sbuf(oes, nc, uname("at_bo"), [1, T], F32)
                dB2 = ctx.sem("at_dBo")
                dstep(ctx, "sync", lambda s: s.dma_start(bo_sb[0:1, :], W["bo"]),
                      list(att_done), dB2)
                bo_target = dB2.v

            groups = []
            for dc in range(NDC):
                w = [(oPE, oPE0 + 4 * (dc - 1))]
                if dc < 2:
                    w += list(att_done)
                dstep(ctx, "gpsimd",
                      (lambda dc: lambda g: g.dma_start(
                          wos[:, (dc % 2) * T:((dc % 2) + 1) * T], W["wo_slab"][dc]))(dc),
                      w, dWo[dc % 2])
                for g in range(4):
                    groups.append(dict(dc=dc, g=g))
            psO = psS
            for gi, grp in enumerate(groups):
                dc, g = grp["dc"], grp["g"]
                grp["ps"] = psO[:, (gi % 2) * 512:((gi % 2) + 1) * 512]
                grp["x_src_ap"] = x_src[dc * 128:(dc + 1) * 128, g * 512:(g + 1) * 512]
                grp["x_dst_ap"] = x_dst[dc * 128:(dc + 1) * 128, g * 512:(g + 1) * 512]

            def mk_out(gi, t):
                dc = groups[gi]["dc"]
                g = groups[gi]["g"]
                ps = groups[gi]["ps"]
                t.wait_ge(dWo[dc % 2].h, dWo0[dc % 2] + 16 * (dc // 2 + 1))
                if layer == 0 and gi == 0:
                    t.wait_ge(dB2.h, bo_target)
                last = None
                for h in range(NH):
                    last = t.matmul(
                        ps, wos[:, (dc % 2) * T + h * 128:(dc % 2) * T + (h + 1) * 128],
                        zall[:, h * T + g * 512: h * T + (g + 1) * 512],
                        start=(h == 0), stop=(h == NH - 1 and layer == 1))
                if layer == 0:
                    last = t.matmul(ps, bo_sb[0:1, dc * 128:(dc + 1) * 128],
                                    C["ones_f32"][0:1, 0:512], start=False, stop=True)
                return last

            emit_residual_out(ctx, oes, groups, mk_out, extra_first_waits=att_done)


def emit_l0ffn(ctx, C, x_src, x_dst, W):
    nc = ctx.nc
    with contextlib.ExitStack() as pes:
        xb = sbuf(pes, nc, uname("f0_xq"), [128, NDC * T], BF16)
        wraw = sbuf(pes, nc, uname("f0_wr"), [128, 2 * 2 * T], F32)
        wq = sbuf(pes, nc, uname("f0_wq"), [128, 2 * T], BF16)
        qt1 = sbuf(pes, nc, uname("f0_t1"), [128, T], F32)
        qt2 = sbuf(pes, nc, uname("f0_t2"), [128, T], F32)

        ln_done = emit_ln(ctx, C, x_src, xb, 0, T, quantize=True)

        dWf = [ctx.sem("f0_dW0"), ctx.sem("f0_dW1")]
        vQ = ctx.sem("f0_vQ")
        pe = ctx.sem("o_pe")
        ev = ctx.sem("o_ev")
        dx = [ctx.sem("o_dx%d" % i) for i in range(4)]
        ds = [ctx.sem("o_ds%d" % i) for i in range(4)]
        dWf0 = [s.v for s in dWf]
        vQ0, pe0, ev0 = vQ.v, pe.v, ev.v
        dx0 = [s.v for s in dx]
        ds0 = [s.v for s in ds]
        psO = psum(pes, nc, uname("f0_psO"), [128, 1024])
        xt = sbuf(pes, nc, uname("ro_xt"), [128, 4 * 512], F32)
        ot = sbuf(pes, nc, uname("ro_ot"), [128, 4 * 512], F32)

        for oc in range(NDC):
            w = [(vQ, vQ0 + 4 * (oc - 1))]
            dstep(ctx, "gpsimd",
                  (lambda oc: lambda g: g.dma_start(
                      wraw[:, (oc % 2) * 2 * T:(oc % 2) * 2 * T + 2 * T],
                      W["wr_slab"][oc]))(oc),
                  w, dWf[oc % 2])
            wsl = wraw[:, (oc % 2) * 2 * T:(oc % 2) * 2 * T + T]
            rsl = wraw[:, (oc % 2) * 2 * T + T:(oc % 2) * 2 * T + 2 * T]
            qsl = wq[:, (oc % 2) * T:((oc % 2) + 1) * T]
            w = [(dWf[oc % 2], dWf0[oc % 2] + 16 * (oc // 2 + 1)),
                 (pe, pe0 + 4 * (oc - 1))]
            cstep(ctx, "vector",
                  (lambda wsl: lambda v: v.tensor_scalar(
                      qt1[:, :], wsl, 0.2, 0.6, ALU.is_gt, ALU.mult))(wsl), w, vQ)
            cstep(ctx, "vector",
                  (lambda wsl: lambda v: v.tensor_scalar(
                      qt2[:, :], wsl, -0.2, 0.6, ALU.is_lt, ALU.mult))(wsl), [], vQ)
            cstep(ctx, "vector",
                  lambda v: v.tensor_tensor(qt1[:, :], qt1[:, :], qt2[:, :],
                                            ALU.subtract), [], vQ)
            cstep(ctx, "vector",
                  (lambda rsl, qsl: lambda v: v.tensor_tensor(
                      qsl, qt1[:, :], rsl, ALU.add))(rsl, qsl), [], vQ)
            vq_ready = vQ.v
            for g in range(4):
                gi = 4 * oc + g
                xi = gi % 4
                ps = psO[:, (gi % 2) * 512:((gi % 2) + 1) * 512]
                x_src_ap = x_src[oc * 128:(oc + 1) * 128, g * 512:(g + 1) * 512]
                x_dst_ap = x_dst[oc * 128:(oc + 1) * 128, g * 512:(g + 1) * 512]
                w = [(ev, ev0 + gi - 3)]
                if gi < 4:
                    w.append(ln_done)
                dstep(ctx, "sync",
                      (lambda x_src_ap, xi: lambda s: s.dma_start(
                          xt[:, xi * 512:(xi + 1) * 512], x_src_ap))(x_src_ap, xi),
                      w, dx[xi])
                w = [(ev, ev0 + gi - 1), (vQ, vq_ready)]
                if gi == 0:
                    w.append(ln_done)

                def mk(ps, oc, g):
                    def f(t):
                        last = None
                        for dc in range(NDC):
                            last = t.matmul(
                                ps, wq[:, (oc % 2) * T + dc * 128:
                                       (oc % 2) * T + (dc + 1) * 128],
                                xb[:, dc * T + g * 512: dc * T + (g + 1) * 512],
                                start=(dc == 0), stop=(dc == NDC - 1))
                        return last
                    return f
                cstep(ctx, "tensor", mk(ps, oc, g), w, pe)
                w = [(pe, pe0 + gi + 1), (dx[xi], dx0[xi] + 16 * (gi // 4 + 1)),
                     (ds[xi], ds0[xi] + 16 * (gi // 4))]
                cstep(ctx, "vector",
                      (lambda ps, xi: lambda v: v.tensor_tensor(
                          ot[:, xi * 512:(xi + 1) * 512], ps,
                          xt[:, xi * 512:(xi + 1) * 512], ALU.add))(ps, xi),
                      w, ev)
                dstep(ctx, "sync",
                      (lambda x_dst_ap, xi: lambda s: s.dma_start(
                          x_dst_ap, ot[:, xi * 512:(xi + 1) * 512]))(x_dst_ap, xi),
                      [(ev, ev0 + gi + 1)], ds[xi])


def emit_l1ffn(ctx, C, x_src, x_dst, W):
    nc = ctx.nc
    with contextlib.ExitStack() as pes:
        acc = sbuf(pes, nc, uname("f1_acc"), [128, NDC * TG], F32)
        xb = sbuf(pes, nc, uname("f1_xn"), [128, NDC * TG], BF16)
        lg_tm = sbuf(pes, nc, uname("f1_lg"), [128, 8 * 16], F32)
        rr = sbuf(pes, nc, uname("f1_rr"), [128, 16], F32)
        cf = sbuf(pes, nc, uname("f1_cf"), [128, 16], F32)
        mrow = sbuf(pes, nc, uname("f1_mr"), [128, 2], F32)
        ctile = sbuf(pes, nc, uname("f1_ct"), [16, 8 * 128], F32)
        srow = sbuf(pes, nc, uname("f1_sr"), [1, TG], F32)
        gws = sbuf(pes, nc, uname("f1_gws"), [128, NDC * 16], BF16)
        sgs = sbuf(pes, nc, uname("f1_sgs"), [128, NDC], BF16)

        coef_dram = nc.dram_tensor(uname("coef_dram"), [17, TG], F32,
                                   kind="Internal").ap()

        dG = ctx.sem("f1_dG")
        dWsp = [ctx.sem("f1_dWs0"), ctx.sem("f1_dWs1")]
        dW2 = ctx.sem("f1_dW2")
        dCB = ctx.sem("f1_dCB")
        dCS = ctx.sem("f1_dCS")
        tR = ctx.sem("f1_tR")
        vRt = ctx.sem("f1_vRt")
        sR = ctx.sem("f1_sR")
        tU = ctx.sem("f1_tU")
        sU = ctx.sem("f1_sU")
        vU = ctx.sem("f1_vU")
        tD = ctx.sem("f1_tD")
        vD = ctx.sem("f1_vD")
        oEV = ctx.sem("o_ev")
        oDXs = [ctx.sem("o_dx%d" % i) for i in range(4)]
        oDSs = [ctx.sem("o_ds%d" % i) for i in range(4)]

        psU = psum(pes, nc, uname("f1_psU"), [128, 2 * TG])
        psD = psum(pes, nc, uname("f1_psD"), [128, 2 * TG])

        dstep(ctx, "gpsimd", lambda g: g.dma_start(gws[:, :], W["gwT"]), [], dG)
        dstep(ctx, "gpsimd", lambda g: g.dma_start(sgs[:, :], W["sgT"][:, :]),
              [], dG)
        dG_done = dG.v

        units = []
        for e in range(E_MOE):
            units.append(dict(kind="exp", e=e, gate_row=e))
        for s in range(4):
            units.append(dict(kind="sh", e=s, gate_row=16))

        for tgi in range(TG_PASSES):
            t0 = tgi * TG
            pre = [(tU, tU.v), (tD, tD.v), (vD, vD.v), (vU, vU.v), (sU, sU.v),
                   (oEV, oEV.v), (dCB, dCB.v)] + [(s, s.v) for s in oDSs]
            ln_done = emit_ln(ctx, C, x_src, xb, t0, TG, quantize=False, pre=pre,
                              ext=dict(psM=psU[0:1, 0:TG], psS=psU[0:1, TG:2 * TG],
                                       psBc=psD[:, 0:TG]))

            def rmm(t):
                last = None
                for g in range(TG // 512):
                    for dc in range(NDC):
                        last = t.matmul(
                            psU[0:16, g * 512:(g + 1) * 512],
                            gws[:, dc * 16:(dc + 1) * 16],
                            xb[:, dc * TG + g * 512: dc * TG + (g + 1) * 512],
                            start=(dc == 0), stop=(dc == NDC - 1))
                return last
            cstep(ctx, "tensor", rmm, [ln_done, (dG, dG_done)], tR)

            def smm(t):
                last = None
                for g in range(TG // 512):
                    for dc in range(NDC):
                        last = t.matmul(
                            psU[0:1, TG + g * 512: TG + (g + 1) * 512],
                            sgs[:, dc:dc + 1],
                            xb[:, dc * TG + g * 512: dc * TG + (g + 1) * 512],
                            start=(dc == 0), stop=(dc == NDC - 1))
                return last
            cstep(ctx, "tensor", smm, [], tR)
            cstep(ctx, "scalar", lambda s: s.activation(
                srow[0:1, :], psU[0:1, TG:2 * TG], AF.Sigmoid), [(tR, tR.v)], sR)
            dstep(ctx, "sync",
                  lambda s: s.dma_start(coef_dram[16:17, :], srow[0:1, :]),
                  [(sR, sR.v), (dCB, dCB.v)] + pre, dCS)
            cstep(ctx, "vector", lambda v: v.tensor_copy(
                ctile[0:16, :], psU[0:16, 0:TG]), [(sR, sR.v)], vRt)
            for tc in range(8):
                cstep(ctx, "tensor",
                      (lambda tc: lambda t: t.transpose(
                          psD[0:128, 0:16], ctile[0:16, tc * 128:(tc + 1) * 128],
                          C["ident_f32"][0:16, 0:16]))(tc),
                      [(vRt, vRt.v)], tR)
                cstep(ctx, "vector",
                      (lambda tc: lambda v: v.tensor_copy(
                          lg_tm[:, tc * 16:(tc + 1) * 16], psD[0:128, 0:16]))(tc),
                      [(tR, tR.v)], vRt)
            for tc in range(8):
                sl = slice(tc * 16, (tc + 1) * 16)
                cstep(ctx, "vector",
                      (lambda sl: lambda v: v.reduce_max(
                          mrow[:, 0:1], lg_tm[:, sl], axis=AX.X, negate=True))(sl),
                      [], vRt)
                cstep(ctx, "scalar",
                      (lambda sl: lambda s: s.activation(
                          rr[:, :], lg_tm[:, sl], AF.Exp, bias=mrow[:, 0:1],
                          accum_out=mrow[:, 1:2]))(sl),
                      [(vRt, vRt.v)], sR)
                cstep(ctx, "vector",
                      lambda v: v.reciprocal(mrow[:, 1:2], mrow[:, 1:2]),
                      [(sR, sR.v)], vRt)
                cstep(ctx, "vector", lambda v: v.tensor_single_scalar(
                    rr[:, :], rr[:, :], mrow[:, 1:2], ALU.mult), [], vRt)
                cstep(ctx, "vector", lambda v: v.memset(cf[:, :], 0.0), [], vRt)
                for r in range(TOPK):
                    cstep(ctx, "vector", lambda v: v.reduce_max(
                        mrow[:, 0:1], rr[:, :], axis=AX.X), [], vRt)
                    cstep(ctx, "vector", lambda v: v.tensor_single_scalar(
                        lg_tm[:, 0:16], rr[:, :], mrow[:, 0:1], ALU.is_ge), [], vRt)
                    cstep(ctx, "vector", lambda v: v.tensor_tensor(
                        lg_tm[:, 0:16], lg_tm[:, 0:16], rr[:, :], ALU.mult), [], vRt)
                    cstep(ctx, "vector", lambda v: v.tensor_tensor(
                        cf[:, :], cf[:, :], lg_tm[:, 0:16], ALU.add), [], vRt)
                    cstep(ctx, "vector", lambda v: v.tensor_tensor(
                        rr[:, :], rr[:, :], lg_tm[:, 0:16], ALU.subtract), [], vRt)
                cstep(ctx, "tensor",
                      (lambda tc: lambda t: t.transpose(
                          psD[0:16, 0:128], cf[:, 0:16], C["ident_f32"][:, :]))(tc),
                      [(vRt, vRt.v)], tR)
                cstep(ctx, "vector",
                      (lambda tc: lambda v: v.tensor_copy(
                          ctile[0:16, tc * 128:(tc + 1) * 128], psD[0:16, 0:128]))(tc),
                      [(tR, tR.v)], vRt)
            dstep(ctx, "sync",
                  lambda s: s.dma_start(coef_dram[0:16, :], ctile[0:16, :]),
                  [(vRt, vRt.v)], dCS)
            for i in range(NDC):
                w = [(oEV, oEV.v)] if i == 0 else []
                cstep(ctx, "vector",
                      (lambda i: lambda v: v.memset(acc[:, i * TG:(i + 1) * TG],
                                                    0.0))(i),
                      w, vRt)
            acc_zeroed = vRt.v
            coef_stored = dCS.v

            with contextlib.ExitStack() as ues:
                he = sbuf(ues, nc, uname("f1_h"), [128, NIC * TG], BF16)
                w2s = sbuf(ues, nc, uname("f1_w2"), [128, NIC * 2048], BF16)
                w13s = sbuf(ues, nc, uname("f1_w13"), [128, 4 * NDC * 128], BF16)
                coefb = sbuf(ues, nc, uname("f1_cb"), [128, TG], F32)
                sg = sbuf(ues, nc, uname("f1_sg"), [128, 2 * TG], F32)

                tU0u, tD0u, vD0u, vU0u = tU.v, tD.v, vD.v, vU.v
                pairno = [0]
                sU_after = []
                vU_after = []
                for ui, u in enumerate(units):
                    dstep(ctx, "sync",
                          (lambda u: lambda s: s.dma_start(
                              coefb[:, :],
                              coef_dram[u["gate_row"]:u["gate_row"] + 1, :]
                              .partition_broadcast(128)))(u),
                          [(dCS, coef_stored), (vU, vU.v)], dCB)

                    def pair_load(p):
                        pn = ui * NIC + p
                        pb = pn % 2
                        w = [(tU, tU0u + pn - 1)]
                        if pn == 0:
                            w.append(ln_done)
                        src = (W["w13gu_sl"][u["e"]][p] if u["kind"] == "exp"
                               else W["sgugu_sl"][u["e"]][p])
                        dstep(ctx, "gpsimd",
                              (lambda src, pb: lambda g: g.dma_start(
                                  w13s[:, 2 * pb * NDC * 128:
                                       (2 * pb + 2) * NDC * 128], src))(src, pb),
                              w, dWsp[pb])
                    pair_load(0)
                    pair_load(1)
                    w = [(tD, tD0u + 32 * ui)]
                    if ui == 0:
                        w.append(ln_done)
                    src_w2 = (W["w2_unit"][u["e"]] if u["kind"] == "exp"
                              else W["sdw_unit"][u["e"]])
                    dstep(ctx, "gpsimd",
                          (lambda src_w2: lambda g: g.dma_start(
                              w2s[:, :].rearrange("p (ic o) -> p ic o", ic=NIC),
                              src_w2))(src_w2),
                          w, dW2)
                    w2_done = dW2.v
                    for p in range(NIC):
                        pn = ui * NIC + p
                        pb = pn % 2
                        if p >= 2:
                            pair_load(p)
                        psGU = psU if pb == 0 else psD

                        def upmm(pb, psGU):
                            def f(t):
                                last = None
                                for j in range(2):
                                    ps = psGU[:, j * TG:(j + 1) * TG]
                                    for dc in range(NDC):
                                        for g5 in range(TG // 512):
                                            last = t.matmul(
                                                ps[:, g5 * 512:(g5 + 1) * 512],
                                                w13s[:, (2 * pb + j) * NDC * 128 + dc * 128:
                                                     (2 * pb + j) * NDC * 128 + (dc + 1) * 128],
                                                xb[:, dc * TG + g5 * 512:
                                                   dc * TG + (g5 + 1) * 512],
                                                start=(dc == 0), stop=(dc == NDC - 1))
                                return last
                            return f
                        w = [(dWsp[pb], dWsp[pb].v)]
                        if pn >= 2:
                            w.append((sU, sU_after[pn - 2]))
                            w.append((vU, vU_after[pn - 2]))
                        if pn == 0:
                            w.append(ln_done)
                        if pb == 1 and p <= 1:
                            w.append((vD, vD.v))
                        if ui == 0 and p <= 1:
                            w.append((vRt, acc_zeroed))
                            w.append((sR, sR.v))
                        cstep(ctx, "tensor", upmm(pb, psGU), w, tU)
                        cstep(ctx, "scalar",
                              (lambda psGU, pb: lambda s: s.activation(
                                  sg[:, pb * TG:(pb + 1) * TG], psGU[:, 0:TG],
                                  AF.Silu))(psGU, pb),
                              [(tU, tU.v)], sU)
                        sU_after.append(sU.v)
                        w = [(sU, sU.v), (dCB, dCB.v)]
                        if p == 0:
                            w.append((tD, tD0u + 32 * ui))
                        cstep(ctx, "vector",
                              (lambda p, psGU, pb: lambda v: v.tensor_tensor(
                                  he[:, p * TG:(p + 1) * TG],
                                  sg[:, pb * TG:(pb + 1) * TG],
                                  psGU[:, TG:2 * TG], ALU.mult))(p, psGU, pb),
                              w, vU)
                        cstep(ctx, "vector",
                              (lambda p: lambda v: v.tensor_tensor(
                                  he[:, p * TG:(p + 1) * TG],
                                  he[:, p * TG:(p + 1) * TG],
                                  coefb[:, :], ALU.mult))(p),
                              [], vU)
                        vU_after.append(vU.v)
                    for oc in range(NDC):
                        ri = oc % 2

                        def dnmm(ri, oc):
                            def f(t):
                                last = None
                                for g5 in range(TG // 512):
                                    for ic in range(NIC):
                                        last = t.matmul(
                                            psD[:, ri * TG + g5 * 512:
                                                ri * TG + (g5 + 1) * 512],
                                            w2s[:, ic * 2048 + oc * 128:
                                                ic * 2048 + (oc + 1) * 128],
                                            he[:, ic * TG + g5 * 512:
                                               ic * TG + (g5 + 1) * 512],
                                            start=(ic == 0), stop=(ic == NIC - 1))
                                return last
                            return f
                        w = [(dW2, w2_done), (vD, vD.v - 1)]
                        if oc == 0:
                            w.append((vU, vU.v))
                        cstep(ctx, "tensor", dnmm(ri, oc), w, tD, tD)
                        w = [(tD, tD.v)]
                        if ui == 0 and oc == 0:
                            w.append((vRt, acc_zeroed))
                        cstep(ctx, "vector",
                              (lambda ri, oc: lambda v: v.tensor_tensor(
                                  acc[:, oc * TG:(oc + 1) * TG],
                                  acc[:, oc * TG:(oc + 1) * TG],
                                  psD[:, ri * TG:(ri + 1) * TG], ALU.add))(ri, oc),
                              w, vD)
                units_done = (tD.v, vD.v)

            with contextlib.ExitStack() as fes:
                xt = sbuf(fes, nc, uname("f1_xt"), [128, 4 * 512], F32)
                ot = sbuf(fes, nc, uname("f1_ot"), [128, 4 * 512], F32)
                oEVb = oEV.v
                oDXb = [s.v for s in oDXs]
                oDSb = [s.v for s in oDSs]
                groups = []
                for oc in range(NDC):
                    for g in range(TG // 512):
                        groups.append((oc, g))
                for gi, (oc, g) in enumerate(groups):
                    xi = gi % 4
                    w = [(oEV, oEVb + gi - 3)]
                    if gi < 4:
                        w.append((tD, units_done[0]))
                    dstep(ctx, "sync",
                          (lambda oc, g, xi, t0: lambda s: s.dma_start(
                              xt[:, xi * 512:(xi + 1) * 512],
                              x_src[oc * 128:(oc + 1) * 128,
                                    t0 + g * 512: t0 + (g + 1) * 512]))(oc, g, xi, t0),
                          w, oDXs[xi])
                    w = [(oDXs[xi], oDXb[xi] + 16 * (gi // 4 + 1)), (vD, units_done[1]),
                         (oDSs[xi], oDSb[xi] + 16 * (gi // 4))]
                    cstep(ctx, "vector",
                          (lambda oc, g, xi: lambda v: v.tensor_tensor(
                              ot[:, xi * 512:(xi + 1) * 512],
                              acc[:, oc * TG + g * 512: oc * TG + (g + 1) * 512],
                              xt[:, xi * 512:(xi + 1) * 512], ALU.add))(oc, g, xi),
                          w, oEV)
                    dstep(ctx, "sync",
                          (lambda oc, g, xi, t0: lambda s: s.dma_start(
                              x_dst[oc * 128:(oc + 1) * 128,
                                    t0 + g * 512: t0 + (g + 1) * 512],
                              ot[:, xi * 512:(xi + 1) * 512]))(oc, g, xi, t0),
                          [(oEV, oEVb + gi + 1)], oDSs[xi])


def declare_consts(ctx, es, layer_set):
    nc = ctx.nc
    C = {}
    C["ones_bcol"] = sbuf(es, nc, "c_ones_b", [128, 1], BF16)
    C["ones_f32"] = sbuf(es, nc, "c_ones_f", [1, 512], F32)
    C["ident_bf"] = sbuf(es, nc, "c_id_bf", [128, 128], BF16)
    C["ident_f32"] = sbuf(es, nc, "c_id_f", [128, 128], F32)
    ident_bf = nc.dram_tensor("ident_bf", [128, 128], BF16, kind="ExternalInput").ap()
    ident_f = nc.dram_tensor("ident_f", [128, 128], F32, kind="ExternalInput").ap()
    dI = ctx.sem("c_dI")
    cM = ctx.sem("c_cM")
    dstep(ctx, "sync", lambda s: s.dma_start(C["ident_bf"][:, :], ident_bf[:, :]),
          [], dI)
    dstep(ctx, "sync", lambda s: s.dma_start(C["ident_f32"][:, :], ident_f[:, :]),
          [], dI)
    if "l0attn" in layer_set:
        C["maskT_dram"] = nc.dram_tensor("maskT", [4, 128, 512], BF16,
                                         kind="ExternalInput").ap()
    if "l1attn" in layer_set:
        C["rope_dram"] = nc.dram_tensor("rope", [2, 64, T], BF16,
                                        kind="ExternalInput").ap()
    cstep(ctx, "vector", lambda v: v.memset(C["ones_bcol"][:, :], 1.0),
          [(dI, dI.v)], cM)
    cstep(ctx, "vector", lambda v: v.memset(C["ones_f32"][:, :], 1.0), [], cM)
    ctx.add("tensor", lambda t: t.nop(), [(cM, cM.v), (dI, dI.v)], [])
    ctx.add("scalar", lambda t: t.nop(), [(cM, cM.v), (dI, dI.v)], [])
    ctx.add("vector", lambda t: t.nop(), [(dI, dI.v)], [])
    ctx.add("gpsimd", lambda t: t.nop(), [(cM, cM.v), (dI, dI.v)], [])
    return C


def declare_weight_aps(nc, phases):
    W = {}
    if "l0attn" in phases:
        wqkv = nc.dram_tensor("a0_wqkv", [NH, 128, 3 * D], F32,
                              kind="ExternalInput").ap()
        wo = nc.dram_tensor("a0_wo", [NDC, 128, D], F32, kind="ExternalInput").ap()
        bqkv = nc.dram_tensor("a0_b", [NH, 1, 3 * 128], F32, kind="ExternalInput").ap()
        bo = nc.dram_tensor("a0_bo", [1, D], F32, kind="ExternalInput").ap()
        W["l0attn"] = dict(
            wqkv_slab=[wqkv[h] for h in range(NH)],
            wo_slab=[wo[dc] for dc in range(NDC)],
            bqkv=[bqkv[h] for h in range(NH)],
            bo=bo[0:1, :],
        )
    if "l0ffn" in phases:
        wr = nc.dram_tensor("f0_wr", [NDC, 128, 2 * D], F32, kind="ExternalInput").ap()
        W["l0ffn"] = dict(wr_slab=[wr[oc] for oc in range(NDC)])
    if "l1attn" in phases:
        wqkv = nc.dram_tensor("a1_wqkv", [NH, 128, 3 * D], F32,
                              kind="ExternalInput").ap()
        wo = nc.dram_tensor("a1_woT", [NDC, 128, D], F32, kind="ExternalInput").ap()
        W["l1attn"] = dict(
            wqkv_slab=[wqkv[h] for h in range(NH)],
            wo_slab=[wo[dc] for dc in range(NDC)],
        )
    if "l1ffn" in phases:
        gwT = nc.dram_tensor("f1_gwT", [128, NDC * 16], F32, kind="ExternalInput").ap()
        sgT = nc.dram_tensor("f1_sgT", [128, NDC], F32, kind="ExternalInput").ap()
        w13 = nc.dram_tensor("f1_w13gu", [E_MOE, NIC, 128, 2 * D], F32,
                             kind="ExternalInput").ap()
        w2T = nc.dram_tensor("f1_w2T", [E_MOE, IE, D], F32, kind="ExternalInput").ap()
        sgu2 = nc.dram_tensor("f1_sgugu", [4, NIC, 128, 2 * D], F32,
                              kind="ExternalInput").ap()
        sdwT = nc.dram_tensor("f1_sdwT", [ISH, D], F32, kind="ExternalInput").ap()
        W["l1ffn"] = dict(
            gwT=gwT[:, :],
            sgT=sgT[:, :],
            w13gu_sl=[[w13[e][p] for p in range(NIC)] for e in range(E_MOE)],
            w2_unit=[w2T[e].rearrange("(ic p) o -> p ic o", p=128)
                     for e in range(E_MOE)],
            sgugu_sl=[[sgu2[s][p] for p in range(NIC)] for s in range(4)],
            sdw_unit=[sdwT[s * IE:(s + 1) * IE, :].rearrange("(ic p) o -> p ic o", p=128)
                      for s in range(4)],
        )
    return W


PHASE_FNS = dict(
    l0attn=lambda ctx, C, xs, xd, W: emit_attn(ctx, C, 0, xs, xd, W),
    l0ffn=emit_l0ffn,
    l1attn=lambda ctx, C, xs, xd, W: emit_attn(ctx, C, 1, xs, xd, W),
    l1ffn=emit_l1ffn,
)


DEBUG_MIDS = True


def build_program(phases):
    nc = bass.Bass("TRN2", target_bir_lowering=False)
    x_in = nc.dram_tensor("x_in", [D, T], F32, kind="ExternalInput").ap()
    x_out = nc.dram_tensor("x_out", [D, T], F32, kind="ExternalOutput").ap()
    xr = [x_in]
    for i in range(len(phases) - 1):
        xr.append(nc.dram_tensor("x_mid%d" % i, [D, T], F32,
                                 kind="ExternalOutput" if DEBUG_MIDS else "Internal").ap())
    xr.append(x_out)
    W = declare_weight_aps(nc, phases)
    with contextlib.ExitStack() as es:
        ctx = Ctx(nc, es)
        C = declare_consts(ctx, es, set(phases))
        for i, ph in enumerate(phases):
            PHASE_FNS[ph](ctx, C, xr[i], xr[i + 1], W[ph])
            phase_join(ctx)
        ctx.emit()
    return nc


def host_consts():
    ident = np.eye(128)
    maskT = np.zeros((4, 128, 512), dtype=ml_dtypes.bfloat16)
    tkl = np.arange(128)[:, None]
    tql = np.arange(512)[None, :]
    for v in range(4):
        maskT[v] = (tkl <= tql - 128 * v).astype(ml_dtypes.bfloat16)
    inv_freq = 1.0 / (10000.0 ** (np.arange(0, 128, 2, dtype=np.float64) / 128.0))
    t = np.arange(T, dtype=np.float64)
    ang = np.outer(inv_freq, t)
    sin, cos = np.sin(ang), np.cos(ang)
    ropeA = np.concatenate([cos[0:32], sin[0:32]], axis=0)
    ropeB = np.concatenate([sin[32:64], cos[32:64]], axis=0)
    rope = np.stack([ropeA, ropeB]).astype(ml_dtypes.bfloat16)
    return dict(ident_bf=ident.astype(ml_dtypes.bfloat16),
                ident_f=ident.astype(np.float32),
                maskT=maskT, rope=rope)


def host_weight_maps(inputs, phases):
    m = {}

    def tile_in_out(w_out_in, n_out_grp):
        O, I = w_out_in.shape
        G = O // n_out_grp if False else None
        return None

    if "l0attn" in phases:
        wq = inputs["L0_attn_W_Q"]
        wk = inputs["L0_attn_W_K"]
        wv = inputs["L0_attn_W_V"]
        wo = inputs["L0_attn_W_O"]
        def qkv(w):
            return w.reshape(NH, NDC, 128, HD).transpose(0, 2, 1, 3).reshape(NH, 128, D)
        m["a0_wqkv"] = np.ascontiguousarray(
            np.concatenate([qkv(wq), qkv(wk), qkv(wv)], axis=2))
        m["a0_wo"] = np.ascontiguousarray(
            wo.reshape(NH, HD, NDC, 128).transpose(2, 1, 0, 3).reshape(NDC, 128, D))
        m["a0_b"] = np.ascontiguousarray(
            np.stack([inputs["L0_attn_b_Q"], inputs["L0_attn_b_K"],
                      inputs["L0_attn_b_V"]], axis=1).reshape(NH, 1, 3 * 128))
        m["a0_bo"] = np.ascontiguousarray(inputs["L0_attn_b_O"].reshape(1, D))
    if "l0ffn" in phases:
        def outin(w):
            return w.reshape(NDC, 128, NDC, 128).transpose(0, 3, 2, 1).reshape(NDC, 128, D)
        m["f0_wr"] = np.ascontiguousarray(
            np.concatenate([outin(inputs["L0_ffn_weight"]),
                            outin(inputs["L0_ffn_rram_variation"])], axis=2))
    if "l1attn" in phases:
        def heads(w):
            return w.reshape(NH, 128, NDC, 128).transpose(0, 3, 2, 1).reshape(NH, 128, D)
        m["a1_wqkv"] = np.ascontiguousarray(
            np.concatenate([heads(inputs["L1_attn_wq"]),
                            heads(inputs["L1_attn_wk"]),
                            heads(inputs["L1_attn_wv"])], axis=2))
        m["a1_woT"] = np.ascontiguousarray(
            inputs["L1_attn_wo"].reshape(NDC, 128, NH, 128)
            .transpose(0, 3, 2, 1).reshape(NDC, 128, D))
    if "l1ffn" in phases:
        gw = inputs["L1_ffn_gate_w"]
        m["f1_gwT"] = np.ascontiguousarray(
            gw.reshape(16, NDC, 128).transpose(2, 1, 0).reshape(128, NDC * 16))
        m["f1_sgT"] = np.ascontiguousarray(
            inputs["L1_ffn_sgate_w"].reshape(NDC, 128).T)
        w13 = inputs["L1_ffn_w13"]
        def uptile(w3):
            E = w3.shape[0]
            return w3.reshape(E, NIC, 128, NDC, 128).transpose(0, 1, 4, 3, 2) \
                .reshape(E, NIC, 128, D)
        m["f1_w13gu"] = np.ascontiguousarray(
            np.concatenate([uptile(w13[:, 0:IE, :]),
                            uptile(w13[:, IE:2 * IE, :])], axis=3))
        m["f1_w2T"] = np.ascontiguousarray(inputs["L1_ffn_w2"].transpose(0, 2, 1))
        sgu = inputs["L1_ffn_shared_gate_up_w"]
        def uptile1(w3):
            return w3.reshape(4, NIC, 128, NDC, 128).transpose(0, 1, 4, 3, 2) \
                .reshape(4, NIC, 128, D)
        m["f1_sgugu"] = np.ascontiguousarray(
            np.concatenate([uptile1(sgu[0:ISH]), uptile1(sgu[ISH:2 * ISH])], axis=3))
        m["f1_sdwT"] = np.ascontiguousarray(inputs["L1_ffn_shared_down_w"].T)
    return m


def const_maps(phases):
    hc = host_consts()
    m = dict(ident_bf=hc["ident_bf"], ident_f=hc["ident_f"])
    if "l0attn" in phases:
        m["maskT"] = hc["maskT"]
    if "l1attn" in phases:
        m["rope"] = hc["rope"]
    return m


def run_phases(x_percore, inputs, phases):
    nc = build_program(phases)
    wm = host_weight_maps(inputs, phases)
    cm = const_maps(phases)
    in_maps = []
    for c in range(8):
        d = dict(x_in=x_percore[c])
        d.update(wm)
        d.update(cm)
        in_maps.append(d)
    res = run_bass_kernel_spmd(nc, in_maps, core_ids=list(range(8)))
    return [res.results[c]["x_out"] for c in range(8)]


PHASE_ORDER = ["l0attn", "l0ffn", "l1attn", "l1ffn"]
FUSED = True


def kernel(**inputs):
    x = inputs["L0_attn_x"]
    x_percore = [np.ascontiguousarray(x[c].T) for c in range(8)]
    if FUSED:
        x_percore = run_phases(x_percore, inputs, PHASE_ORDER)
    else:
        for ph in PHASE_ORDER:
            x_percore = run_phases(x_percore, inputs, [ph])
    out = np.stack([np.ascontiguousarray(xc.T) for xc in x_percore])
    return np.ascontiguousarray(out.astype(np.float32))


# version 08
# speedup vs baseline: 1.2065x; 1.0468x over previous
import contextlib
import numpy as np
import ml_dtypes
import concourse.bass as bass
import concourse.mybir as mybir
from concourse.bass_utils import run_bass_kernel_spmd

F32 = mybir.dt.float32
BF16 = mybir.dt.bfloat16
AF = mybir.ActivationFunctionType
ALU = mybir.AluOpType
AX = mybir.AxisListType

D = 2048
T = 2048
NDC = D // 128
NH = 16
HD = 128
EPS = 1e-5
SCL = float(1.0 / np.sqrt(128.0))
SHIFT = 25.0
IE = 1408
NIC = IE // 128
ISH = 5632
TG = 1024
TG_PASSES = 2
E_MOE = 16
TOPK = 4
NGT = T // 512


class Sem:
    __slots__ = ("h", "v", "name")

    def __init__(self, h, name):
        self.h = h
        self.v = 0
        self.name = name


class Ctx:
    def __init__(self, nc, sem_es):
        self.nc = nc
        self.sem_es = sem_es
        self.steps = []
        self.sems = []
        self._semmap = {}
        self._owner = {}

    def sem(self, name):
        if name in self._semmap:
            return self._semmap[name]
        h = self.sem_es.enter_context(self.nc.semaphore(name))
        s = Sem(h, name)
        self.sems.append(s)
        self._semmap[name] = s
        return s

    def add(self, eng, fn, waits=(), incs=()):
        for (s, n, va) in incs:
            owner = self._owner.setdefault(s, eng)
            if owner != eng:
                self._owner[s] = None
        self.steps.append((eng, fn, list(waits), list(incs)))

    def emit(self):
        steps = self.steps
        with self.nc.Block() as blk:
            for eng_name, reg in (("sync", blk.sync), ("gpsimd", blk.gpsimd),
                                  ("tensor", blk.tensor), ("vector", blk.vector),
                                  ("scalar", blk.scalar)):
                def mk(nm):
                    def f(eng):
                        chain = None
                        for (e, fn, waits, incs) in steps:
                            if e != nm:
                                continue
                            assert len(incs) <= 1, (nm, incs)
                            for (s, val) in waits:
                                if val > 0:
                                    eng.wait_ge(s.h, val)
                            if nm in ("vector", "scalar") and chain is not None:
                                eng.wait_ge(chain[0].h, chain[1])
                            inst = fn(eng)
                            for (s, n, va) in incs:
                                inst.then_inc(s.h, n)
                                if self._owner.get(s) == nm:
                                    chain = (s, va)
                    return f
                reg(mk(eng_name))


def dstep(ctx, eng, fn, waits, sem):
    sem.v += 16
    ctx.add(eng, fn, waits, [(sem, 16, sem.v)])


def cstep(ctx, eng, fn, waits, *sems):
    merged = {}
    order = []
    for s in sems:
        s.v += 1
        if s not in merged:
            merged[s] = 0
            order.append(s)
        merged[s] += 1
    ctx.add(eng, fn, waits, [(s, merged[s], s.v) for s in order])


def sbuf(es, nc, name, shape, dtype):
    return es.enter_context(nc.sbuf_tensor(name, shape, dtype)).ap()


def psum(es, nc, name, shape, dtype=F32):
    return es.enter_context(nc.psum_tensor(name, shape, dtype)).ap()


_uid = [0]


def uname(base):
    _uid[0] += 1
    return "%s_%d" % (base, _uid[0])


def emit_ln(ctx, C, x_src, xb, t0, tn, quantize=False, pre=(), ext=None):
    nc = ctx.nc
    dL = ctx.sem("ln_dL")
    sQ = ctx.sem("ln_sQ")
    tS = ctx.sem("ln_tS")
    vS = ctx.sem("ln_vS")
    tB = ctx.sem("ln_tB")
    sB = ctx.sem("ln_sB")
    vN = ctx.sem("ln_vN")
    dL0, sQ0, tS0, vS0, tB0, sB0, vN0 = dL.v, sQ.v, tS.v, vS.v, tB.v, sB.v, vN.v

    NG = tn // 512
    pre = list(pre)
    with contextlib.ExitStack() as les:
        sqws = sbuf(les, nc, uname("ln_sq"), [128, 2 * tn], BF16)
        rows = sbuf(les, nc, uname("ln_rows"), [128, 2 * tn], F32)
        seg_m = rows[0:1, 0:tn]
        seg_r = rows[0:1, tn:2 * tn]
        rstd_b = sbuf(les, nc, uname("ln_rb"), [128, tn], BF16)
        mrstd_b = sbuf(les, nc, uname("ln_mb"), [128, tn], BF16)
        if quantize:
            nws = sbuf(les, nc, uname("ln_nws"), [128, tn], F32)
            qws = sbuf(les, nc, uname("ln_qws"), [128, 2 * tn], F32)

        stats_scope = contextlib.ExitStack()
        if ext is None:
            psM = psum(stats_scope, nc, uname("ln_psM"), [1, tn])
            psS = psum(stats_scope, nc, uname("ln_psS"), [1, tn])
        else:
            psM, psS = ext["psM"], ext["psS"]

        dstep(ctx, "gpsimd",
              lambda g: g.dma_start(
                  xb[:, :].rearrange("p (dc t) -> p dc t", dc=NDC),
                  x_src[:, t0:t0 + tn].rearrange("(dc p) t -> p dc t", p=128)),
              list(pre), dL)
        dL_all = dL0 + 16
        for dc in range(NDC):
            w = [(dL, dL_all)]
            if dc == 0:
                w += pre
            if dc >= 2:
                w.append((tS, tS0 + 2 * (dc - 1)))
            cstep(ctx, "scalar",
                  (lambda dc: lambda s: s.activation(
                      sqws[:, (dc % 2) * tn:(dc % 2 + 1) * tn],
                      xb[:, dc * tn:(dc + 1) * tn], AF.Square))(dc),
                  w, sQ)

            def mmm(dc):
                def f(t):
                    last = None
                    for g in range(NG):
                        last = t.matmul(psM[0:1, g * 512:(g + 1) * 512],
                                        C["ones_bcol"][:, 0:1],
                                        xb[:, dc * tn + g * 512: dc * tn + (g + 1) * 512],
                                        start=(dc == 0), stop=(dc == NDC - 1))
                    return last
                return f

            def smm(dc):
                def f(t):
                    last = None
                    for g in range(NG):
                        last = t.matmul(psS[0:1, g * 512:(g + 1) * 512],
                                        C["ones_bcol"][:, 0:1],
                                        sqws[:, (dc % 2) * tn + g * 512:
                                             (dc % 2) * tn + (g + 1) * 512],
                                        start=(dc == 0), stop=(dc == NDC - 1))
                    return last
                return f
            w = [(dL, dL_all)]
            if dc == 0:
                w += pre
            cstep(ctx, "tensor", mmm(dc), w, tS)
            cstep(ctx, "tensor", smm(dc), [(sQ, sQ0 + dc + 1)], tS)
        p1 = psM[0:1, 0:tn]
        w = [(tS, tS.v)] + pre
        cstep(ctx, "vector", lambda v: v.tensor_scalar(
            seg_m, psM[0:1, :], 1.0 / D, None, ALU.mult), w, vS)
        cstep(ctx, "vector", lambda v: v.tensor_scalar(
            seg_r, psS[0:1, :], 1.0 / D, None, ALU.mult), [], vS)
        cstep(ctx, "vector", lambda v: v.tensor_tensor(
            p1, seg_m, seg_m, ALU.mult), [], vS)
        cstep(ctx, "vector", lambda v: v.tensor_tensor(
            seg_r, seg_r, p1, ALU.subtract), [], vS)
        cstep(ctx, "vector", lambda v: v.tensor_scalar(
            seg_r, seg_r, float(EPS), None, ALU.add), [], vS)
        cstep(ctx, "scalar", lambda s: s.activation(
            p1, seg_r, AF.Sqrt), [(vS, vS.v)], sQ)
        cstep(ctx, "vector", lambda v: v.reciprocal(seg_r, p1),
              [(sQ, sQ.v)], vS)
        cstep(ctx, "vector", lambda v: v.tensor_tensor(
            seg_m, seg_m, seg_r, ALU.mult), [], vS)
        cstep(ctx, "vector", lambda v: v.tensor_copy(
            sqws[0:1, 0:tn], seg_r), [], vS)
        cstep(ctx, "vector", lambda v: v.tensor_copy(
            sqws[0:1, tn:2 * tn], seg_m), [], vS)
        stats_done = vS.v
        stats_scope.close()

        bc_scope = contextlib.ExitStack()
        if ext is None:
            psBc = psum(bc_scope, nc, uname("ln_psBc"), [128, tn])
        else:
            psBc = ext["psBc"]

        def bc(row):
            def f(t):
                last = None
                for g in range(NG):
                    last = t.matmul(psBc[:, g * 512:(g + 1) * 512],
                                    C["ones_bfr"][0:1, 0:128],
                                    row[0:1, g * 512:(g + 1) * 512],
                                    start=True, stop=True)
                return last
            return f
        cstep(ctx, "tensor", bc(sqws[0:1, 0:tn]), [(vS, stats_done)], tB)
        cstep(ctx, "scalar", lambda s: s.copy(rstd_b[:, :], psBc[:, :]),
              [(tB, tB0 + 1)], sB)
        cstep(ctx, "tensor", bc(sqws[0:1, tn:2 * tn]), [(sB, sB0 + 1)], tB)
        cstep(ctx, "scalar", lambda s: s.copy(mrstd_b[:, :], psBc[:, :]),
              [(tB, tB0 + 2)], sB)
        bc_scope.close()
        for dc in range(NDC):
            w = [(sB, sB0 + 2)] if dc == 0 else []
            if not quantize:
                cstep(ctx, "vector",
                      (lambda dc: lambda v: v.tensor_tensor(
                          xb[:, dc * tn:(dc + 1) * tn],
                          xb[:, dc * tn:(dc + 1) * tn], rstd_b[:, :],
                          ALU.mult))(dc),
                      w, vN)
                cstep(ctx, "vector",
                      (lambda dc: lambda v: v.tensor_tensor(
                          xb[:, dc * tn:(dc + 1) * tn],
                          xb[:, dc * tn:(dc + 1) * tn], mrstd_b[:, :],
                          ALU.subtract))(dc),
                      [], vN)
            else:
                cstep(ctx, "vector",
                      (lambda dc: lambda v: v.tensor_tensor(
                          nws[:, :], xb[:, dc * tn:(dc + 1) * tn], rstd_b[:, :],
                          ALU.mult))(dc),
                      w, vN)
                cstep(ctx, "vector",
                      (lambda dc: lambda v: v.tensor_tensor(
                          nws[:, :], nws[:, :], mrstd_b[:, :], ALU.subtract))(dc),
                      [], vN)
                cstep(ctx, "vector",
                      (lambda dc: lambda v: v.tensor_scalar(
                          qws[:, 0:tn], nws[:, :], 0.33, 0.5, ALU.is_gt, ALU.mult))(dc),
                      [], vN)
                cstep(ctx, "vector",
                      (lambda dc: lambda v: v.tensor_scalar(
                          qws[:, tn:2 * tn], nws[:, :], -0.33, 0.5, ALU.is_lt,
                          ALU.mult))(dc),
                      [], vN)
                cstep(ctx, "vector",
                      (lambda dc: lambda v: v.tensor_tensor(
                          xb[:, dc * tn:(dc + 1) * tn], qws[:, 0:tn],
                          qws[:, tn:2 * tn], ALU.subtract))(dc),
                      [], vN)
    return (vN, vN.v)


def emit_residual_out(ctx, es, groups, mk_matmuls, extra_first_waits=()):
    nc = ctx.nc
    pe = ctx.sem("o_pe")
    ev = ctx.sem("o_ev")
    dx = [ctx.sem("o_dx%d" % i) for i in range(4)]
    ds = [ctx.sem("o_ds%d" % i) for i in range(4)]
    pe0, ev0 = pe.v, ev.v
    dx0 = [s.v for s in dx]
    ds0 = [s.v for s in ds]
    xt = sbuf(es, nc, uname("ro_xt"), [128, 4 * 512], F32)
    ot = sbuf(es, nc, uname("ro_ot"), [128, 4 * 512], F32)
    for g, grp in enumerate(groups):
        xi = g % 4
        w = [(ev, ev0 + g - 3)]
        if g < 4:
            w += list(extra_first_waits)
        dstep(ctx, "sync",
              (lambda grp, xi: lambda s: s.dma_start(
                  xt[:, xi * 512:(xi + 1) * 512], grp["x_src_ap"]))(grp, xi),
              w, dx[xi])
        w = [(ev, ev0 + g - 1)]
        if g < 2:
            w += list(extra_first_waits)
        cstep(ctx, "tensor", (lambda g: lambda t: mk_matmuls(g, t))(g), w, pe)
        w = [(pe, pe0 + g + 1), (dx[xi], dx0[xi] + 16 * (g // 4 + 1)),
             (ds[xi], ds0[xi] + 16 * (g // 4))]
        cstep(ctx, "vector",
              (lambda grp, xi: lambda v: v.tensor_tensor(
                  ot[:, xi * 512:(xi + 1) * 512], grp["ps"],
                  xt[:, xi * 512:(xi + 1) * 512], ALU.add))(grp, xi),
              w, ev)
        dstep(ctx, "sync",
              (lambda grp, xi: lambda s: s.dma_start(
                  grp["x_dst_ap"], ot[:, xi * 512:(xi + 1) * 512]))(grp, xi),
              [(ev, ev0 + g + 1)], ds[xi])
    return pe


def phase_join(ctx):
    join = ctx.sem("join")
    for eng in ("sync", "gpsimd", "tensor", "vector", "scalar"):
        waits = []
        if eng == "sync":
            for s in ctx.sems:
                if s is not join and s.v > 0:
                    waits.append((s, s.v))
        cstep(ctx, eng, lambda e: e.nop(), waits, join)
    target = join.v
    for eng in ("sync", "gpsimd", "tensor", "vector", "scalar"):
        ctx.add(eng, lambda e: e.nop(), [(join, target)], [])


def emit_attn(ctx, C, layer, x_src, x_dst, W):
    nc = ctx.nc
    causal = (layer == 0)
    with contextlib.ExitStack() as pes:
        xb = sbuf(pes, nc, uname("at_xn"), [128, NDC * T], BF16)
        zall = sbuf(pes, nc, uname("at_z"), [128, NH * T], BF16)
        q_sb = sbuf(pes, nc, uname("at_q"), [128, T], BF16)
        k_sb = sbuf(pes, nc, uname("at_k"), [128, T], BF16)
        vf_sb = sbuf(pes, nc, uname("at_vf"), [128, T], BF16)
        vt_sb = sbuf(pes, nc, uname("at_vt"), [128, T], BF16)
        wst = sbuf(pes, nc, uname("at_wst"), [128, 2 * 3 * T], BF16)
        dIc = ctx.sem("at_dIc")
        if layer == 0:
            bias_sb = sbuf(pes, nc, uname("at_bias"), [1, 3 * 128], BF16)
            mk = sbuf(pes, nc, uname("at_mask"), [128, 4 * 512], BF16)
            maskT = [mk[:, v * 512:(v + 1) * 512] for v in range(4)]
            for v in range(4):
                dstep(ctx, "sync",
                      (lambda v: lambda s: s.dma_start(
                          mk[:, v * 512:(v + 1) * 512], C["maskT_dram"][v]))(v),
                      [], dIc)
        dIc_done = dIc.v

        ln_done = emit_ln(ctx, C, x_src, xb, 0, T, quantize=False)

        dWp = [ctx.sem("at_dW0"), ctx.sem("at_dW1")]
        dB = ctx.sem("at_dB")
        tP = ctx.sem("at_tP")
        sE = ctx.sem("at_sE")
        vR = ctx.sem("at_vR")
        tT = ctx.sem("at_tT")
        tQ = ctx.sem("at_tQ")
        sX = ctx.sem("at_sX")
        vM = ctx.sem("at_vM")
        tA = ctx.sem("at_tA")
        tZ = ctx.sem("at_tZ")
        vL = ctx.sem("at_vL")
        vZ = ctx.sem("at_vZ")
        dWp0 = [s.v for s in dWp]
        dB0, tP0 = dB.v, tP.v
        evac_done_at = []

        psP = psum(pes, nc, uname("at_psP"), [128, 2048])
        psB2 = psum(pes, nc, uname("at_psB2"), [128, 1024])
        psS = psum(pes, nc, uname("at_psS"), [128, 1024])
        psT = psS[:, 0:512].bitcast(BF16)

        def proj_bank(i):
            j = i % 6
            if j < 4:
                return psP[:, j * 512:(j + 1) * 512]
            return psB2[:, (j - 4) * 512:(j - 3) * 512]

        def st_ps(ti):
            j = ti % 4
            return (psB2 if j < 2 else psS)[:, (j % 2) * 512:((j % 2) + 1) * 512]

        def z_ps(g):
            return psP[:, (g % 2) * 512:((g % 2) + 1) * 512]

        def l_ps(g):
            return psP[0:1, 1024:1536]

        def rl_ps(g):
            return psP[:, 1536:2048]

        with contextlib.ExitStack() as hes:
            att = sbuf(hes, nc, uname("at_att"), [128, 4 * 512], BF16)
            lrow = sbuf(hes, nc, uname("at_lr"), [1, 512], F32)
            lrow_bf = sbuf(hes, nc, uname("at_lrb"), [1, 512], BF16)
            rl_b = sbuf(hes, nc, uname("at_rlb"), [128, 512], F32)
            if layer == 1:
                rws = sbuf(hes, nc, uname("at_rws"), [128, T], BF16)
                rw2 = sbuf(hes, nc, uname("at_rw2"), [32, 2048], BF16)
                cs = sbuf(hes, nc, uname("at_rope"), [64, 2 * T], BF16)
                ropeA = cs[:, 0:T]
                ropeB = cs[:, T:2 * T]
                dstep(ctx, "sync", lambda s: s.dma_start(
                    cs[:, 0:T], C["rope_dram"][0]), [ln_done], dIc)
                dstep(ctx, "sync", lambda s: s.dma_start(
                    cs[:, T:2 * T], C["rope_dram"][1]), [], dIc)
                dIc_done = dIc.v

            nb = 0
            for h in range(NH):
                wbuf = h % 2
                woff = wbuf * 3 * T
                w = [] if h < 2 else [(tP, tP0 + 12 * (h - 1))]
                dstep(ctx, "gpsimd",
                      (lambda h, woff: lambda g: g.dma_start(
                          wst[:, woff: woff + 3 * T], W["wqkv_slab"][h]))(h, woff),
                      w, dWp[h % 2])
                if layer == 0:
                    wb = [(tP, tP0 + 12 * h)]
                    dstep(ctx, "gpsimd",
                          (lambda h: lambda s: s.dma_start(
                              bias_sb[0:1, :], W["bqkv"][h]))(h),
                          wb, dB)

                for j, dst in enumerate((q_sb, k_sb, vf_sb)):
                    for g in range(4):
                        ps = proj_bank(nb)
                        wts = [(dWp[h % 2], dWp0[h % 2] + 16 * (h // 2 + 1))]
                        if nb >= 6:
                            wts.append((sE, evac_done_at[nb - 6]))
                        if j == 0 and g == 0 and h > 0:
                            wts.append((vZ, vZ.v))
                            wts.append((vL, vL.v))
                        if h == 0 and j == 0 and g == 0:
                            wts.append(ln_done)
                        if layer == 0:
                            wts.append((dB, dB0 + 16 * (h + 1)))

                        def mkp(ps, woff, j, g):
                            def f(t):
                                last = None
                                for dc in range(NDC):
                                    last = t.matmul(
                                        ps,
                                        wst[:, woff + j * T + dc * 128:
                                            woff + j * T + (dc + 1) * 128],
                                        xb[:, dc * T + g * 512: dc * T + (g + 1) * 512],
                                        start=(dc == 0),
                                        stop=(dc == NDC - 1 and layer == 1))
                                if layer == 0:
                                    last = t.matmul(
                                        ps, bias_sb[0:1, j * 128:(j + 1) * 128],
                                        C["ones_bfr"][0:1, 0:512],
                                        start=False, stop=True)
                                return last
                            return f
                        cstep(ctx, "tensor", mkp(ps, woff, j, g), wts, tP)
                        if layer == 1 and j < 2:
                            w2_ = [(tP, tP.v)]
                            if g == 0:
                                w2_.append((vR, vR.v))
                            cstep(ctx, "scalar",
                                  (lambda ps, g: lambda s: s.copy(
                                      rws[0:64, g * 512:(g + 1) * 512],
                                      ps[0:64, :]))(ps, g),
                                  w2_, sE)
                            cstep(ctx, "scalar",
                                  (lambda ps, dst, g: lambda s: s.copy(
                                      dst[64:128, g * 512:(g + 1) * 512],
                                      ps[64:128, :]))(ps, dst, g),
                                  [], sE)
                        else:
                            cstep(ctx, "scalar",
                                  (lambda ps, dst, g: lambda s: s.copy(
                                      dst[:, g * 512:(g + 1) * 512], ps[:, :]))(ps, dst, g),
                                  [(tP, tP.v)], sE)
                        evac_done_at.append(sE.v)
                        nb += 1
                    if layer == 1 and j < 2:
                        s0a = rw2[0:32, 0:1024]
                        s0b = rw2[0:32, 1024:2048]
                        for half in range(2):
                            sl = slice(half * 1024, (half + 1) * 1024)
                            w2_ = [(sE, sE.v), (dIc, dIc_done)] if half == 0 else []
                            cstep(ctx, "vector",
                                  (lambda sl: lambda v: v.tensor_tensor(
                                      s0a, rws[0:32, sl], ropeA[0:32, sl],
                                      ALU.mult))(sl),
                                  w2_, vR)
                            cstep(ctx, "vector",
                                  (lambda sl: lambda v: v.tensor_tensor(
                                      s0b, rws[32:64, sl], ropeA[32:64, sl],
                                      ALU.mult))(sl),
                                  [], vR)
                            cstep(ctx, "vector",
                                  (lambda sl, dst: lambda v: v.tensor_tensor(
                                      dst[0:32, sl], s0a, s0b, ALU.subtract))(sl, dst),
                                  [], vR)
                            cstep(ctx, "vector",
                                  (lambda sl: lambda v: v.tensor_tensor(
                                      s0a, rws[0:32, sl], ropeB[0:32, sl],
                                      ALU.mult))(sl),
                                  [], vR)
                            cstep(ctx, "vector",
                                  (lambda sl: lambda v: v.tensor_tensor(
                                      s0b, rws[32:64, sl], ropeB[32:64, sl],
                                      ALU.mult))(sl),
                                  [], vR)
                            cstep(ctx, "vector",
                                  (lambda sl, dst: lambda v: v.tensor_tensor(
                                      dst[32:64, sl], s0b, s0a, ALU.add))(sl, dst),
                                  [], vR)

                for r in range(2):
                    def tr(r):
                        def f(t):
                            last = None
                            for i in range(8):
                                tc = r * 8 + i
                                last = t.transpose(
                                    psT[:, i * 128:(i + 1) * 128],
                                    vf_sb[:, tc * 128:(tc + 1) * 128],
                                    C["ident_bf"][:, :])
                            return last
                        return f
                    w2_ = [(sE, sE.v)] if r == 0 else [(vR, vR.v)]
                    cstep(ctx, "tensor", tr(r), w2_, tT)
                    cstep(ctx, "vector",
                          (lambda r: lambda v: v.tensor_copy(
                              vt_sb[:, r * 1024:(r + 1) * 1024], psT[:, :]))(r),
                          [(tT, tT.v)], vR)

                qk_ready = [(sE, sE.v), (vR, vR.v)]
                ti = 0
                for g in range(4):
                    tc_hi = min(NDC - 1, 4 * g + 3) if causal else NDC - 1
                    tcs = list(range(tc_hi + 1))
                    for ii, tc in enumerate(tcs):
                        sps = st_ps(ti)
                        w2_ = list(qk_ready) if ti < 4 else []
                        w2_.append((sX, sX.v - 3))
                        cstep(ctx, "tensor",
                              (lambda sps, tc, g: lambda t: t.matmul(
                                  sps, k_sb[:, tc * 128:(tc + 1) * 128],
                                  q_sb[:, g * 512:(g + 1) * 512],
                                  start=True, stop=True))(sps, tc, g),
                              w2_, tQ)
                        w2_ = [(tQ, tQ.v), (tA, tA.v - 3)]
                        cstep(ctx, "scalar",
                              (lambda sps, ti: lambda s: s.activation(
                                  att[:, (ti % 4) * 512:((ti % 4) + 1) * 512], sps,
                                  AF.Exp, bias=np.float32(-SHIFT),
                                  scale=np.float32(SCL)))(sps, ti),
                              w2_, sX)
                        dep = sX
                        if causal and tc >= 4 * g:
                            vvar = tc - 4 * g
                            cstep(ctx, "vector",
                                  (lambda ti, vvar: lambda v: v.tensor_tensor(
                                      att[:, (ti % 4) * 512:((ti % 4) + 1) * 512],
                                      att[:, (ti % 4) * 512:((ti % 4) + 1) * 512],
                                      maskT[vvar], ALU.mult))(ti, vvar),
                                  [(sX, sX.v), (dIc, dIc_done)], vM)
                            dep = vM
                        first = (ii == 0)
                        last = (ii == len(tcs) - 1)

                        def av(ti, tc, g, first, last):
                            def f(t):
                                t.matmul(l_ps(g), C["ones_bcol"][:, 0:1],
                                         att[:, (ti % 4) * 512:((ti % 4) + 1) * 512],
                                         start=first, stop=last)
                                return t.matmul(
                                    z_ps(g), vt_sb[:, tc * 128:(tc + 1) * 128],
                                    att[:, (ti % 4) * 512:((ti % 4) + 1) * 512],
                                    start=first, stop=last)
                            return f
                        w2_ = [(dep, dep.v)]
                        if first:
                            w2_.append((vZ, vZ.v - 1))
                            w2_.append((vL, vL.v))
                        cstep(ctx, "tensor", av(ti, tc, g, first, last), w2_, tA)
                        ti += 1
                    cstep(ctx, "vector",
                          (lambda g: lambda v: v.tensor_copy(lrow[0:1, :], l_ps(g)))(g),
                          [(tA, tA.v)], vL)
                    cstep(ctx, "vector",
                          lambda v: v.reciprocal(lrow[0:1, :], lrow[0:1, :]),
                          [], vL)
                    cstep(ctx, "vector",
                          lambda v: v.tensor_copy(lrow_bf[0:1, :], lrow[0:1, :]),
                          [], vL)
                    cstep(ctx, "tensor",
                          (lambda g: lambda t: t.matmul(
                              rl_ps(g), C["ones_bfr"][0:1, 0:128], lrow_bf[0:1, 0:512],
                              start=True, stop=True))(g),
                          [(vL, vL.v), (sE, sE.v)], tZ)
                    cstep(ctx, "scalar",
                          (lambda g: lambda s: s.copy(rl_b[:, :], rl_ps(g)))(g),
                          [(tZ, tZ.v), (vZ, vZ.v)], sE)
                    cstep(ctx, "vector",
                          (lambda g, h: lambda v: v.tensor_tensor(
                              zall[:, h * T + g * 512: h * T + (g + 1) * 512],
                              z_ps(g), rl_b[:, :], ALU.mult))(g, h),
                          [(sE, sE.v)], vZ)
            att_done = [(vZ, vZ.v)]

        with contextlib.ExitStack() as oes:
            dWo = [ctx.sem("at_dWo0"), ctx.sem("at_dWo1")]
            dWo0 = [s.v for s in dWo]
            oPE = ctx.sem("o_pe")
            oPE0 = oPE.v
            wos = sbuf(oes, nc, uname("at_wos"), [128, 2 * T], BF16)
            if layer == 0:
                bo_sb = sbuf(oes, nc, uname("at_bo"), [1, T], BF16)
                dB2 = ctx.sem("at_dBo")
                dstep(ctx, "gpsimd", lambda s: s.dma_start(bo_sb[0:1, :], W["bo"]),
                      list(att_done), dB2)
                bo_target = dB2.v

            groups = []
            for dc in range(NDC):
                w = [(oPE, oPE0 + 4 * (dc - 1))]
                if dc < 2:
                    w += list(att_done)
                dstep(ctx, "gpsimd",
                      (lambda dc: lambda g: g.dma_start(
                          wos[:, (dc % 2) * T:((dc % 2) + 1) * T], W["wo_slab"][dc]))(dc),
                      w, dWo[dc % 2])
                for g in range(4):
                    groups.append(dict(dc=dc, g=g))
            psO = psS
            for gi, grp in enumerate(groups):
                dc, g = grp["dc"], grp["g"]
                grp["ps"] = psO[:, (gi % 2) * 512:((gi % 2) + 1) * 512]
                grp["x_src_ap"] = x_src[dc * 128:(dc + 1) * 128, g * 512:(g + 1) * 512]
                grp["x_dst_ap"] = x_dst[dc * 128:(dc + 1) * 128, g * 512:(g + 1) * 512]

            def mk_out(gi, t):
                dc = groups[gi]["dc"]
                g = groups[gi]["g"]
                ps = groups[gi]["ps"]
                t.wait_ge(dWo[dc % 2].h, dWo0[dc % 2] + 16 * (dc // 2 + 1))
                if layer == 0 and gi == 0:
                    t.wait_ge(dB2.h, bo_target)
                last = None
                for h in range(NH):
                    last = t.matmul(
                        ps, wos[:, (dc % 2) * T + h * 128:(dc % 2) * T + (h + 1) * 128],
                        zall[:, h * T + g * 512: h * T + (g + 1) * 512],
                        start=(h == 0), stop=(h == NH - 1 and layer == 1))
                if layer == 0:
                    last = t.matmul(ps, bo_sb[0:1, dc * 128:(dc + 1) * 128],
                                    C["ones_bfr"][0:1, 0:512], start=False, stop=True)
                return last

            emit_residual_out(ctx, oes, groups, mk_out, extra_first_waits=att_done)


def emit_l0ffn(ctx, C, x_src, x_dst, W):
    nc = ctx.nc
    with contextlib.ExitStack() as pes:
        xb = sbuf(pes, nc, uname("f0_xq"), [128, NDC * T], BF16)
        wraw = sbuf(pes, nc, uname("f0_wr"), [128, 2 * 2 * T], F32)
        wq = sbuf(pes, nc, uname("f0_wq"), [128, 2 * T], BF16)
        qt1 = sbuf(pes, nc, uname("f0_t1"), [128, T], F32)
        qt2 = sbuf(pes, nc, uname("f0_t2"), [128, T], F32)

        ln_done = emit_ln(ctx, C, x_src, xb, 0, T, quantize=True)

        dWf = [ctx.sem("f0_dW0"), ctx.sem("f0_dW1")]
        vQ = ctx.sem("f0_vQ")
        pe = ctx.sem("o_pe")
        ev = ctx.sem("o_ev")
        dx = [ctx.sem("o_dx%d" % i) for i in range(4)]
        ds = [ctx.sem("o_ds%d" % i) for i in range(4)]
        dWf0 = [s.v for s in dWf]
        vQ0, pe0, ev0 = vQ.v, pe.v, ev.v
        dx0 = [s.v for s in dx]
        ds0 = [s.v for s in ds]
        psO = psum(pes, nc, uname("f0_psO"), [128, 1024])
        xt = sbuf(pes, nc, uname("ro_xt"), [128, 4 * 512], F32)
        ot = sbuf(pes, nc, uname("ro_ot"), [128, 4 * 512], F32)

        for oc in range(NDC):
            w = [(vQ, vQ0 + 4 * (oc - 1))]
            dstep(ctx, "gpsimd",
                  (lambda oc: lambda g: g.dma_start(
                      wraw[:, (oc % 2) * 2 * T:(oc % 2) * 2 * T + 2 * T],
                      W["wr_slab"][oc]))(oc),
                  w, dWf[oc % 2])
            wsl = wraw[:, (oc % 2) * 2 * T:(oc % 2) * 2 * T + T]
            rsl = wraw[:, (oc % 2) * 2 * T + T:(oc % 2) * 2 * T + 2 * T]
            qsl = wq[:, (oc % 2) * T:((oc % 2) + 1) * T]
            w = [(dWf[oc % 2], dWf0[oc % 2] + 16 * (oc // 2 + 1)),
                 (pe, pe0 + 4 * (oc - 1))]
            cstep(ctx, "vector",
                  (lambda wsl: lambda v: v.tensor_scalar(
                      qt1[:, :], wsl, 0.2, 0.6, ALU.is_gt, ALU.mult))(wsl), w, vQ)
            cstep(ctx, "vector",
                  (lambda wsl: lambda v: v.tensor_scalar(
                      qt2[:, :], wsl, -0.2, 0.6, ALU.is_lt, ALU.mult))(wsl), [], vQ)
            cstep(ctx, "vector",
                  lambda v: v.tensor_tensor(qt1[:, :], qt1[:, :], qt2[:, :],
                                            ALU.subtract), [], vQ)
            cstep(ctx, "vector",
                  (lambda rsl, qsl: lambda v: v.tensor_tensor(
                      qsl, qt1[:, :], rsl, ALU.add))(rsl, qsl), [], vQ)
            vq_ready = vQ.v
            for g in range(4):
                gi = 4 * oc + g
                xi = gi % 4
                ps = psO[:, (gi % 2) * 512:((gi % 2) + 1) * 512]
                x_src_ap = x_src[oc * 128:(oc + 1) * 128, g * 512:(g + 1) * 512]
                x_dst_ap = x_dst[oc * 128:(oc + 1) * 128, g * 512:(g + 1) * 512]
                w = [(ev, ev0 + gi - 3)]
                if gi < 4:
                    w.append(ln_done)
                dstep(ctx, "sync",
                      (lambda x_src_ap, xi: lambda s: s.dma_start(
                          xt[:, xi * 512:(xi + 1) * 512], x_src_ap))(x_src_ap, xi),
                      w, dx[xi])
                w = [(ev, ev0 + gi - 1), (vQ, vq_ready)]
                if gi == 0:
                    w.append(ln_done)

                def mk(ps, oc, g):
                    def f(t):
                        last = None
                        for dc in range(NDC):
                            last = t.matmul(
                                ps, wq[:, (oc % 2) * T + dc * 128:
                                       (oc % 2) * T + (dc + 1) * 128],
                                xb[:, dc * T + g * 512: dc * T + (g + 1) * 512],
                                start=(dc == 0), stop=(dc == NDC - 1))
                        return last
                    return f
                cstep(ctx, "tensor", mk(ps, oc, g), w, pe)
                w = [(pe, pe0 + gi + 1), (dx[xi], dx0[xi] + 16 * (gi // 4 + 1)),
                     (ds[xi], ds0[xi] + 16 * (gi // 4))]
                cstep(ctx, "vector",
                      (lambda ps, xi: lambda v: v.tensor_tensor(
                          ot[:, xi * 512:(xi + 1) * 512], ps,
                          xt[:, xi * 512:(xi + 1) * 512], ALU.add))(ps, xi),
                      w, ev)
                dstep(ctx, "sync",
                      (lambda x_dst_ap, xi: lambda s: s.dma_start(
                          x_dst_ap, ot[:, xi * 512:(xi + 1) * 512]))(x_dst_ap, xi),
                      [(ev, ev0 + gi + 1)], ds[xi])


def emit_l1ffn(ctx, C, x_src, x_dst, W):
    nc = ctx.nc
    with contextlib.ExitStack() as pes:
        acc = sbuf(pes, nc, uname("f1_acc"), [128, NDC * TG], F32)
        xb = sbuf(pes, nc, uname("f1_xn"), [128, NDC * TG], BF16)
        lg_tm = sbuf(pes, nc, uname("f1_lg"), [128, 8 * 16], F32)
        rr = sbuf(pes, nc, uname("f1_rr"), [128, 16], F32)
        cf = sbuf(pes, nc, uname("f1_cf"), [128, 16], F32)
        mrow = sbuf(pes, nc, uname("f1_mr"), [128, 2], F32)
        ctile = sbuf(pes, nc, uname("f1_ct"), [16, 8 * 128], F32)
        srow = sbuf(pes, nc, uname("f1_sr"), [1, TG], F32)
        gws = sbuf(pes, nc, uname("f1_gws"), [128, NDC * 16], BF16)
        sgs = sbuf(pes, nc, uname("f1_sgs"), [128, NDC], BF16)

        coef_dram = nc.dram_tensor(uname("coef_dram"), [17, TG], F32,
                                   kind="Internal").ap()

        dG = ctx.sem("f1_dG")
        dWsp = [ctx.sem("f1_dWs0"), ctx.sem("f1_dWs1")]
        dW2 = ctx.sem("f1_dW2")
        dCB = ctx.sem("f1_dCB")
        dCS = ctx.sem("f1_dCS")
        tR = ctx.sem("f1_tR")
        vRt = ctx.sem("f1_vRt")
        sR = ctx.sem("f1_sR")
        tU = ctx.sem("f1_tU")
        sU = ctx.sem("f1_sU")
        vU = ctx.sem("f1_vU")
        tD = ctx.sem("f1_tD")
        vD = ctx.sem("f1_vD")
        oEV = ctx.sem("o_ev")
        oDXs = [ctx.sem("o_dx%d" % i) for i in range(4)]
        oDSs = [ctx.sem("o_ds%d" % i) for i in range(4)]

        psU = psum(pes, nc, uname("f1_psU"), [128, 2 * TG])
        psD = psum(pes, nc, uname("f1_psD"), [128, 2 * TG])

        dstep(ctx, "gpsimd", lambda g: g.dma_start(gws[:, :], W["gwT"]), [], dG)
        dstep(ctx, "gpsimd", lambda g: g.dma_start(sgs[:, :], W["sgT"][:, :]),
              [], dG)
        dG_done = dG.v

        units = []
        for e in range(E_MOE):
            units.append(dict(kind="exp", e=e, gate_row=e))
        for s in range(4):
            units.append(dict(kind="sh", e=s, gate_row=16))

        for tgi in range(TG_PASSES):
            t0 = tgi * TG
            pre = [(tU, tU.v), (tD, tD.v), (vD, vD.v), (vU, vU.v), (sU, sU.v),
                   (oEV, oEV.v), (dCB, dCB.v)] + [(s, s.v) for s in oDSs]
            ln_done = emit_ln(ctx, C, x_src, xb, t0, TG, quantize=False, pre=pre,
                              ext=dict(psM=psU[0:1, 0:TG], psS=psU[0:1, TG:2 * TG],
                                       psBc=psD[:, 0:TG]))

            def rmm(t):
                last = None
                for g in range(TG // 512):
                    for dc in range(NDC):
                        last = t.matmul(
                            psU[0:16, g * 512:(g + 1) * 512],
                            gws[:, dc * 16:(dc + 1) * 16],
                            xb[:, dc * TG + g * 512: dc * TG + (g + 1) * 512],
                            start=(dc == 0), stop=(dc == NDC - 1))
                return last
            cstep(ctx, "tensor", rmm, [ln_done, (dG, dG_done)], tR)

            def smm(t):
                last = None
                for g in range(TG // 512):
                    for dc in range(NDC):
                        last = t.matmul(
                            psU[0:1, TG + g * 512: TG + (g + 1) * 512],
                            sgs[:, dc:dc + 1],
                            xb[:, dc * TG + g * 512: dc * TG + (g + 1) * 512],
                            start=(dc == 0), stop=(dc == NDC - 1))
                return last
            cstep(ctx, "tensor", smm, [], tR)
            cstep(ctx, "scalar", lambda s: s.activation(
                srow[0:1, :], psU[0:1, TG:2 * TG], AF.Sigmoid), [(tR, tR.v)], sR)
            dstep(ctx, "sync",
                  lambda s: s.dma_start(coef_dram[16:17, :], srow[0:1, :]),
                  [(sR, sR.v), (dCB, dCB.v)] + pre, dCS)
            cstep(ctx, "vector", lambda v: v.tensor_copy(
                ctile[0:16, :], psU[0:16, 0:TG]), [(sR, sR.v)], vRt)
            for tc in range(8):
                cstep(ctx, "tensor",
                      (lambda tc: lambda t: t.transpose(
                          psD[0:128, 0:16], ctile[0:16, tc * 128:(tc + 1) * 128],
                          C["ident_f32"][0:16, 0:16]))(tc),
                      [(vRt, vRt.v)], tR)
                cstep(ctx, "vector",
                      (lambda tc: lambda v: v.tensor_copy(
                          lg_tm[:, tc * 16:(tc + 1) * 16], psD[0:128, 0:16]))(tc),
                      [(tR, tR.v)], vRt)
            for tc in range(8):
                sl = slice(tc * 16, (tc + 1) * 16)
                cstep(ctx, "vector",
                      (lambda sl: lambda v: v.reduce_max(
                          mrow[:, 0:1], lg_tm[:, sl], axis=AX.X, negate=True))(sl),
                      [], vRt)
                cstep(ctx, "scalar",
                      (lambda sl: lambda s: s.activation(
                          rr[:, :], lg_tm[:, sl], AF.Exp, bias=mrow[:, 0:1],
                          accum_out=mrow[:, 1:2]))(sl),
                      [(vRt, vRt.v)], sR)
                cstep(ctx, "vector",
                      lambda v: v.reciprocal(mrow[:, 1:2], mrow[:, 1:2]),
                      [(sR, sR.v)], vRt)
                cstep(ctx, "vector", lambda v: v.tensor_single_scalar(
                    rr[:, :], rr[:, :], mrow[:, 1:2], ALU.mult), [], vRt)
                cstep(ctx, "vector", lambda v: v.memset(cf[:, :], 0.0), [], vRt)
                for r in range(TOPK):
                    cstep(ctx, "vector", lambda v: v.reduce_max(
                        mrow[:, 0:1], rr[:, :], axis=AX.X), [], vRt)
                    cstep(ctx, "vector", lambda v: v.tensor_single_scalar(
                        lg_tm[:, 0:16], rr[:, :], mrow[:, 0:1], ALU.is_ge), [], vRt)
                    cstep(ctx, "vector", lambda v: v.tensor_tensor(
                        lg_tm[:, 0:16], lg_tm[:, 0:16], rr[:, :], ALU.mult), [], vRt)
                    cstep(ctx, "vector", lambda v: v.tensor_tensor(
                        cf[:, :], cf[:, :], lg_tm[:, 0:16], ALU.add), [], vRt)
                    cstep(ctx, "vector", lambda v: v.tensor_tensor(
                        rr[:, :], rr[:, :], lg_tm[:, 0:16], ALU.subtract), [], vRt)
                cstep(ctx, "tensor",
                      (lambda tc: lambda t: t.transpose(
                          psD[0:16, 0:128], cf[:, 0:16], C["ident_f32"][:, :]))(tc),
                      [(vRt, vRt.v)], tR)
                cstep(ctx, "vector",
                      (lambda tc: lambda v: v.tensor_copy(
                          ctile[0:16, tc * 128:(tc + 1) * 128], psD[0:16, 0:128]))(tc),
                      [(tR, tR.v)], vRt)
            dstep(ctx, "sync",
                  lambda s: s.dma_start(coef_dram[0:16, :], ctile[0:16, :]),
                  [(vRt, vRt.v)], dCS)
            for i in range(NDC):
                w = [(oEV, oEV.v)] if i == 0 else []
                cstep(ctx, "vector",
                      (lambda i: lambda v: v.memset(acc[:, i * TG:(i + 1) * TG],
                                                    0.0))(i),
                      w, vRt)
            acc_zeroed = vRt.v
            coef_stored = dCS.v

            with contextlib.ExitStack() as ues:
                he = sbuf(ues, nc, uname("f1_h"), [128, NIC * TG], BF16)
                w2s = sbuf(ues, nc, uname("f1_w2"), [128, NIC * 2048], BF16)
                w13s = sbuf(ues, nc, uname("f1_w13"), [128, 4 * NDC * 128], BF16)
                coefb = sbuf(ues, nc, uname("f1_cb"), [128, TG], F32)
                sg = sbuf(ues, nc, uname("f1_sg"), [128, 2 * TG], F32)

                tU0u, tD0u, vD0u, vU0u = tU.v, tD.v, vD.v, vU.v
                pairno = [0]
                sU_after = []
                vU_after = []
                for ui, u in enumerate(units):
                    dstep(ctx, "sync",
                          (lambda u: lambda s: s.dma_start(
                              coefb[:, :],
                              coef_dram[u["gate_row"]:u["gate_row"] + 1, :]
                              .partition_broadcast(128)))(u),
                          [(dCS, coef_stored), (vU, vU.v)], dCB)

                    def pair_load(p):
                        pn = ui * NIC + p
                        pb = pn % 2
                        w = [(tU, tU0u + pn - 1)]
                        if pn == 0:
                            w.append(ln_done)
                        src = (W["w13gu_sl"][u["e"]][p] if u["kind"] == "exp"
                               else W["sgugu_sl"][u["e"]][p])
                        dstep(ctx, "gpsimd",
                              (lambda src, pb: lambda g: g.dma_start(
                                  w13s[:, 2 * pb * NDC * 128:
                                       (2 * pb + 2) * NDC * 128], src))(src, pb),
                              w, dWsp[pb])
                    pair_load(0)
                    pair_load(1)
                    w = [(tD, tD0u + 32 * ui)]
                    if ui == 0:
                        w.append(ln_done)
                    src_w2 = (W["w2_unit"][u["e"]] if u["kind"] == "exp"
                              else W["sdw_unit"][u["e"]])
                    dstep(ctx, "gpsimd",
                          (lambda src_w2: lambda g: g.dma_start(
                              w2s[:, :].rearrange("p (ic o) -> p ic o", ic=NIC),
                              src_w2))(src_w2),
                          w, dW2)
                    w2_done = dW2.v
                    for p in range(NIC):
                        pn = ui * NIC + p
                        pb = pn % 2
                        if p >= 2:
                            pair_load(p)
                        psGU = psU if pb == 0 else psD

                        def upmm(pb, psGU):
                            def f(t):
                                last = None
                                for j in range(2):
                                    ps = psGU[:, j * TG:(j + 1) * TG]
                                    for dc in range(NDC):
                                        for g5 in range(TG // 512):
                                            last = t.matmul(
                                                ps[:, g5 * 512:(g5 + 1) * 512],
                                                w13s[:, (2 * pb + j) * NDC * 128 + dc * 128:
                                                     (2 * pb + j) * NDC * 128 + (dc + 1) * 128],
                                                xb[:, dc * TG + g5 * 512:
                                                   dc * TG + (g5 + 1) * 512],
                                                start=(dc == 0), stop=(dc == NDC - 1))
                                return last
                            return f
                        w = [(dWsp[pb], dWsp[pb].v)]
                        if pn >= 2:
                            w.append((sU, sU_after[pn - 2]))
                            w.append((vU, vU_after[pn - 2]))
                        if pn == 0:
                            w.append(ln_done)
                        if pb == 1 and p <= 1:
                            w.append((vD, vD.v))
                        if ui == 0 and p <= 1:
                            w.append((vRt, acc_zeroed))
                            w.append((sR, sR.v))
                        cstep(ctx, "tensor", upmm(pb, psGU), w, tU)
                        cstep(ctx, "scalar",
                              (lambda psGU, pb: lambda s: s.activation(
                                  sg[:, pb * TG:(pb + 1) * TG], psGU[:, 0:TG],
                                  AF.Silu))(psGU, pb),
                              [(tU, tU.v)], sU)
                        sU_after.append(sU.v)
                        w = [(sU, sU.v), (dCB, dCB.v)]
                        if p == 0:
                            w.append((tD, tD0u + 32 * ui))
                        cstep(ctx, "vector",
                              (lambda p, psGU, pb: lambda v: v.tensor_tensor(
                                  he[:, p * TG:(p + 1) * TG],
                                  sg[:, pb * TG:(pb + 1) * TG],
                                  psGU[:, TG:2 * TG], ALU.mult))(p, psGU, pb),
                              w, vU)
                        cstep(ctx, "vector",
                              (lambda p: lambda v: v.tensor_tensor(
                                  he[:, p * TG:(p + 1) * TG],
                                  he[:, p * TG:(p + 1) * TG],
                                  coefb[:, :], ALU.mult))(p),
                              [], vU)
                        vU_after.append(vU.v)
                    for oc in range(NDC):
                        ri = oc % 2

                        def dnmm(ri, oc):
                            def f(t):
                                last = None
                                for g5 in range(TG // 512):
                                    for ic in range(NIC):
                                        last = t.matmul(
                                            psD[:, ri * TG + g5 * 512:
                                                ri * TG + (g5 + 1) * 512],
                                            w2s[:, ic * 2048 + oc * 128:
                                                ic * 2048 + (oc + 1) * 128],
                                            he[:, ic * TG + g5 * 512:
                                               ic * TG + (g5 + 1) * 512],
                                            start=(ic == 0), stop=(ic == NIC - 1))
                                return last
                            return f
                        w = [(dW2, w2_done), (vD, vD.v - 1)]
                        if oc == 0:
                            w.append((vU, vU.v))
                        cstep(ctx, "tensor", dnmm(ri, oc), w, tD, tD)
                        w = [(tD, tD.v)]
                        if ui == 0 and oc == 0:
                            w.append((vRt, acc_zeroed))
                        cstep(ctx, "vector",
                              (lambda ri, oc: lambda v: v.tensor_tensor(
                                  acc[:, oc * TG:(oc + 1) * TG],
                                  acc[:, oc * TG:(oc + 1) * TG],
                                  psD[:, ri * TG:(ri + 1) * TG], ALU.add))(ri, oc),
                              w, vD)
                units_done = (tD.v, vD.v)

            with contextlib.ExitStack() as fes:
                xt = sbuf(fes, nc, uname("f1_xt"), [128, 4 * 512], F32)
                ot = sbuf(fes, nc, uname("f1_ot"), [128, 4 * 512], F32)
                oEVb = oEV.v
                oDXb = [s.v for s in oDXs]
                oDSb = [s.v for s in oDSs]
                groups = []
                for oc in range(NDC):
                    for g in range(TG // 512):
                        groups.append((oc, g))
                for gi, (oc, g) in enumerate(groups):
                    xi = gi % 4
                    w = [(oEV, oEVb + gi - 3)]
                    if gi < 4:
                        w.append((tD, units_done[0]))
                    dstep(ctx, "sync",
                          (lambda oc, g, xi, t0: lambda s: s.dma_start(
                              xt[:, xi * 512:(xi + 1) * 512],
                              x_src[oc * 128:(oc + 1) * 128,
                                    t0 + g * 512: t0 + (g + 1) * 512]))(oc, g, xi, t0),
                          w, oDXs[xi])
                    w = [(oDXs[xi], oDXb[xi] + 16 * (gi // 4 + 1)), (vD, units_done[1]),
                         (oDSs[xi], oDSb[xi] + 16 * (gi // 4))]
                    cstep(ctx, "vector",
                          (lambda oc, g, xi: lambda v: v.tensor_tensor(
                              ot[:, xi * 512:(xi + 1) * 512],
                              acc[:, oc * TG + g * 512: oc * TG + (g + 1) * 512],
                              xt[:, xi * 512:(xi + 1) * 512], ALU.add))(oc, g, xi),
                          w, oEV)
                    dstep(ctx, "sync",
                          (lambda oc, g, xi, t0: lambda s: s.dma_start(
                              x_dst[oc * 128:(oc + 1) * 128,
                                    t0 + g * 512: t0 + (g + 1) * 512],
                              ot[:, xi * 512:(xi + 1) * 512]))(oc, g, xi, t0),
                          [(oEV, oEVb + gi + 1)], oDSs[xi])


def declare_consts(ctx, es, layer_set):
    nc = ctx.nc
    C = {}
    C["ones_bcol"] = sbuf(es, nc, "c_ones_b", [128, 1], BF16)
    C["ones_bfr"] = sbuf(es, nc, "c_ones_br", [1, 512], BF16)
    C["ident_bf"] = sbuf(es, nc, "c_id_bf", [128, 128], BF16)
    C["ident_f32"] = sbuf(es, nc, "c_id_f", [128, 128], F32)
    ident_bf = nc.dram_tensor("ident_bf", [128, 128], BF16, kind="ExternalInput").ap()
    ident_f = nc.dram_tensor("ident_f", [128, 128], F32, kind="ExternalInput").ap()
    dI = ctx.sem("c_dI")
    cM = ctx.sem("c_cM")
    dstep(ctx, "sync", lambda s: s.dma_start(C["ident_bf"][:, :], ident_bf[:, :]),
          [], dI)
    dstep(ctx, "sync", lambda s: s.dma_start(C["ident_f32"][:, :], ident_f[:, :]),
          [], dI)
    if "l0attn" in layer_set:
        C["maskT_dram"] = nc.dram_tensor("maskT", [4, 128, 512], BF16,
                                         kind="ExternalInput").ap()
    if "l1attn" in layer_set:
        C["rope_dram"] = nc.dram_tensor("rope", [2, 64, T], BF16,
                                        kind="ExternalInput").ap()
    cstep(ctx, "vector", lambda v: v.memset(C["ones_bcol"][:, :], 1.0),
          [(dI, dI.v)], cM)
    cstep(ctx, "vector", lambda v: v.memset(C["ones_bfr"][:, :], 1.0), [], cM)
    ctx.add("tensor", lambda t: t.nop(), [(cM, cM.v), (dI, dI.v)], [])
    ctx.add("scalar", lambda t: t.nop(), [(cM, cM.v), (dI, dI.v)], [])
    ctx.add("vector", lambda t: t.nop(), [(dI, dI.v)], [])
    ctx.add("gpsimd", lambda t: t.nop(), [(cM, cM.v), (dI, dI.v)], [])
    return C


def declare_weight_aps(nc, phases):
    W = {}
    if "l0attn" in phases:
        wqkv = nc.dram_tensor("a0_wqkv", [NH, 128, 3 * D], F32,
                              kind="ExternalInput").ap()
        wo = nc.dram_tensor("a0_wo", [NDC, 128, D], F32, kind="ExternalInput").ap()
        bqkv = nc.dram_tensor("a0_b", [NH, 1, 3 * 128], F32, kind="ExternalInput").ap()
        bo = nc.dram_tensor("a0_bo", [1, D], F32, kind="ExternalInput").ap()
        W["l0attn"] = dict(
            wqkv_slab=[wqkv[h] for h in range(NH)],
            wo_slab=[wo[dc] for dc in range(NDC)],
            bqkv=[bqkv[h] for h in range(NH)],
            bo=bo[0:1, :],
        )
    if "l0ffn" in phases:
        wr = nc.dram_tensor("f0_wr", [NDC, 128, 2 * D], F32, kind="ExternalInput").ap()
        W["l0ffn"] = dict(wr_slab=[wr[oc] for oc in range(NDC)])
    if "l1attn" in phases:
        wqkv = nc.dram_tensor("a1_wqkv", [NH, 128, 3 * D], F32,
                              kind="ExternalInput").ap()
        wo = nc.dram_tensor("a1_woT", [NDC, 128, D], F32, kind="ExternalInput").ap()
        W["l1attn"] = dict(
            wqkv_slab=[wqkv[h] for h in range(NH)],
            wo_slab=[wo[dc] for dc in range(NDC)],
        )
    if "l1ffn" in phases:
        gwT = nc.dram_tensor("f1_gwT", [128, NDC * 16], F32, kind="ExternalInput").ap()
        sgT = nc.dram_tensor("f1_sgT", [128, NDC], F32, kind="ExternalInput").ap()
        w13 = nc.dram_tensor("f1_w13gu", [E_MOE, NIC, 128, 2 * D], F32,
                             kind="ExternalInput").ap()
        w2T = nc.dram_tensor("f1_w2T", [E_MOE, IE, D], F32, kind="ExternalInput").ap()
        sgu2 = nc.dram_tensor("f1_sgugu", [4, NIC, 128, 2 * D], F32,
                              kind="ExternalInput").ap()
        sdwT = nc.dram_tensor("f1_sdwT", [ISH, D], F32, kind="ExternalInput").ap()
        W["l1ffn"] = dict(
            gwT=gwT[:, :],
            sgT=sgT[:, :],
            w13gu_sl=[[w13[e][p] for p in range(NIC)] for e in range(E_MOE)],
            w2_unit=[w2T[e].rearrange("(ic p) o -> p ic o", p=128)
                     for e in range(E_MOE)],
            sgugu_sl=[[sgu2[s][p] for p in range(NIC)] for s in range(4)],
            sdw_unit=[sdwT[s * IE:(s + 1) * IE, :].rearrange("(ic p) o -> p ic o", p=128)
                      for s in range(4)],
        )
    return W


PHASE_FNS = dict(
    l0attn=lambda ctx, C, xs, xd, W: emit_attn(ctx, C, 0, xs, xd, W),
    l0ffn=emit_l0ffn,
    l1attn=lambda ctx, C, xs, xd, W: emit_attn(ctx, C, 1, xs, xd, W),
    l1ffn=emit_l1ffn,
)


DEBUG_MIDS = True


def build_program(phases):
    nc = bass.Bass("TRN2", target_bir_lowering=False)
    x_in = nc.dram_tensor("x_in", [D, T], F32, kind="ExternalInput").ap()
    x_out = nc.dram_tensor("x_out", [D, T], F32, kind="ExternalOutput").ap()
    xr = [x_in]
    for i in range(len(phases) - 1):
        xr.append(nc.dram_tensor("x_mid%d" % i, [D, T], F32,
                                 kind="ExternalOutput" if DEBUG_MIDS else "Internal").ap())
    xr.append(x_out)
    W = declare_weight_aps(nc, phases)
    with contextlib.ExitStack() as es:
        ctx = Ctx(nc, es)
        C = declare_consts(ctx, es, set(phases))
        for i, ph in enumerate(phases):
            PHASE_FNS[ph](ctx, C, xr[i], xr[i + 1], W[ph])
            phase_join(ctx)
        ctx.emit()
    return nc


def host_consts():
    ident = np.eye(128)
    maskT = np.zeros((4, 128, 512), dtype=ml_dtypes.bfloat16)
    tkl = np.arange(128)[:, None]
    tql = np.arange(512)[None, :]
    for v in range(4):
        maskT[v] = (tkl <= tql - 128 * v).astype(ml_dtypes.bfloat16)
    inv_freq = 1.0 / (10000.0 ** (np.arange(0, 128, 2, dtype=np.float64) / 128.0))
    t = np.arange(T, dtype=np.float64)
    ang = np.outer(inv_freq, t)
    sin, cos = np.sin(ang), np.cos(ang)
    ropeA = np.concatenate([cos[0:32], sin[0:32]], axis=0)
    ropeB = np.concatenate([sin[32:64], cos[32:64]], axis=0)
    rope = np.stack([ropeA, ropeB]).astype(ml_dtypes.bfloat16)
    return dict(ident_bf=ident.astype(ml_dtypes.bfloat16),
                ident_f=ident.astype(np.float32),
                maskT=maskT, rope=rope)


def host_weight_maps(inputs, phases):
    m = {}

    def tile_in_out(w_out_in, n_out_grp):
        O, I = w_out_in.shape
        G = O // n_out_grp if False else None
        return None

    if "l0attn" in phases:
        wq = inputs["L0_attn_W_Q"]
        wk = inputs["L0_attn_W_K"]
        wv = inputs["L0_attn_W_V"]
        wo = inputs["L0_attn_W_O"]
        def qkv(w):
            return w.reshape(NH, NDC, 128, HD).transpose(0, 2, 1, 3).reshape(NH, 128, D)
        m["a0_wqkv"] = np.ascontiguousarray(
            np.concatenate([qkv(wq), qkv(wk), qkv(wv)], axis=2))
        m["a0_wo"] = np.ascontiguousarray(
            wo.reshape(NH, HD, NDC, 128).transpose(2, 1, 0, 3).reshape(NDC, 128, D))
        m["a0_b"] = np.ascontiguousarray(
            np.stack([inputs["L0_attn_b_Q"], inputs["L0_attn_b_K"],
                      inputs["L0_attn_b_V"]], axis=1).reshape(NH, 1, 3 * 128))
        m["a0_bo"] = np.ascontiguousarray(inputs["L0_attn_b_O"].reshape(1, D))
    if "l0ffn" in phases:
        def outin(w):
            return w.reshape(NDC, 128, NDC, 128).transpose(0, 3, 2, 1).reshape(NDC, 128, D)
        m["f0_wr"] = np.ascontiguousarray(
            np.concatenate([outin(inputs["L0_ffn_weight"]),
                            outin(inputs["L0_ffn_rram_variation"])], axis=2))
    if "l1attn" in phases:
        def heads(w):
            return w.reshape(NH, 128, NDC, 128).transpose(0, 3, 2, 1).reshape(NH, 128, D)
        m["a1_wqkv"] = np.ascontiguousarray(
            np.concatenate([heads(inputs["L1_attn_wq"]),
                            heads(inputs["L1_attn_wk"]),
                            heads(inputs["L1_attn_wv"])], axis=2))
        m["a1_woT"] = np.ascontiguousarray(
            inputs["L1_attn_wo"].reshape(NDC, 128, NH, 128)
            .transpose(0, 3, 2, 1).reshape(NDC, 128, D))
    if "l1ffn" in phases:
        gw = inputs["L1_ffn_gate_w"]
        m["f1_gwT"] = np.ascontiguousarray(
            gw.reshape(16, NDC, 128).transpose(2, 1, 0).reshape(128, NDC * 16))
        m["f1_sgT"] = np.ascontiguousarray(
            inputs["L1_ffn_sgate_w"].reshape(NDC, 128).T)
        w13 = inputs["L1_ffn_w13"]
        def uptile(w3):
            E = w3.shape[0]
            return w3.reshape(E, NIC, 128, NDC, 128).transpose(0, 1, 4, 3, 2) \
                .reshape(E, NIC, 128, D)
        m["f1_w13gu"] = np.ascontiguousarray(
            np.concatenate([uptile(w13[:, 0:IE, :]),
                            uptile(w13[:, IE:2 * IE, :])], axis=3))
        m["f1_w2T"] = np.ascontiguousarray(inputs["L1_ffn_w2"].transpose(0, 2, 1))
        sgu = inputs["L1_ffn_shared_gate_up_w"]
        def uptile1(w3):
            return w3.reshape(4, NIC, 128, NDC, 128).transpose(0, 1, 4, 3, 2) \
                .reshape(4, NIC, 128, D)
        m["f1_sgugu"] = np.ascontiguousarray(
            np.concatenate([uptile1(sgu[0:ISH]), uptile1(sgu[ISH:2 * ISH])], axis=3))
        m["f1_sdwT"] = np.ascontiguousarray(inputs["L1_ffn_shared_down_w"].T)
    return m


def const_maps(phases):
    hc = host_consts()
    m = dict(ident_bf=hc["ident_bf"], ident_f=hc["ident_f"])
    if "l0attn" in phases:
        m["maskT"] = hc["maskT"]
    if "l1attn" in phases:
        m["rope"] = hc["rope"]
    return m


def run_phases(x_percore, inputs, phases):
    nc = build_program(phases)
    wm = host_weight_maps(inputs, phases)
    cm = const_maps(phases)
    in_maps = []
    for c in range(8):
        d = dict(x_in=x_percore[c])
        d.update(wm)
        d.update(cm)
        in_maps.append(d)
    res = run_bass_kernel_spmd(nc, in_maps, core_ids=list(range(8)))
    return [res.results[c]["x_out"] for c in range(8)]


PHASE_ORDER = ["l0attn", "l0ffn", "l1attn", "l1ffn"]
FUSED = True


def kernel(**inputs):
    x = inputs["L0_attn_x"]
    x_percore = [np.ascontiguousarray(x[c].T) for c in range(8)]
    if FUSED:
        x_percore = run_phases(x_percore, inputs, PHASE_ORDER)
    else:
        for ph in PHASE_ORDER:
            x_percore = run_phases(x_percore, inputs, [ph])
    out = np.stack([np.ascontiguousarray(xc.T) for xc in x_percore])
    return np.ascontiguousarray(out.astype(np.float32))


# version 09 (final)
# speedup vs baseline: 1.4399x; 1.1935x over previous
import contextlib
import numpy as np
import ml_dtypes
import concourse.bass as bass
import concourse.mybir as mybir
from concourse.bass_utils import run_bass_kernel_spmd

F32 = mybir.dt.float32
BF16 = mybir.dt.bfloat16
AF = mybir.ActivationFunctionType
ALU = mybir.AluOpType
AX = mybir.AxisListType

D = 2048
T = 2048
NDC = D // 128
NH = 16
HD = 128
EPS = 1e-5
SCL = float(1.0 / np.sqrt(128.0))
SHIFT = 25.0
IE = 1408
NIC = IE // 128
ISH = 5632
TG = 1024
TG_PASSES = 2
E_MOE = 16
TOPK = 4
NGT = T // 512


class Sem:
    __slots__ = ("h", "v", "name")

    def __init__(self, h, name):
        self.h = h
        self.v = 0
        self.name = name


class Ctx:
    def __init__(self, nc, sem_es):
        self.nc = nc
        self.sem_es = sem_es
        self.steps = []
        self.sems = []
        self._semmap = {}
        self._owner = {}

    def sem(self, name):
        if name in self._semmap:
            return self._semmap[name]
        h = self.sem_es.enter_context(self.nc.semaphore(name))
        s = Sem(h, name)
        self.sems.append(s)
        self._semmap[name] = s
        return s

    def add(self, eng, fn, waits=(), incs=()):
        for (s, n, va) in incs:
            owner = self._owner.setdefault(s, eng)
            if owner != eng:
                self._owner[s] = None
        self.steps.append((eng, fn, list(waits), list(incs)))

    def emit(self):
        steps = self.steps
        with self.nc.Block() as blk:
            for eng_name, reg in (("sync", blk.sync), ("gpsimd", blk.gpsimd),
                                  ("tensor", blk.tensor), ("vector", blk.vector),
                                  ("scalar", blk.scalar)):
                def mk(nm):
                    def f(eng):
                        chain = None
                        for (e, fn, waits, incs) in steps:
                            if e != nm:
                                continue
                            assert len(incs) <= 1, (nm, incs)
                            for (s, val) in waits:
                                if val > 0:
                                    eng.wait_ge(s.h, val)
                            if nm in ("vector", "scalar") and chain is not None:
                                eng.wait_ge(chain[0].h, chain[1])
                            inst = fn(eng)
                            for (s, n, va) in incs:
                                inst.then_inc(s.h, n)
                                if self._owner.get(s) == nm:
                                    chain = (s, va)
                    return f
                reg(mk(eng_name))


def dstep(ctx, eng, fn, waits, sem):
    sem.v += 16
    ctx.add(eng, fn, waits, [(sem, 16, sem.v)])


def cstep(ctx, eng, fn, waits, *sems):
    merged = {}
    order = []
    for s in sems:
        s.v += 1
        if s not in merged:
            merged[s] = 0
            order.append(s)
        merged[s] += 1
    ctx.add(eng, fn, waits, [(s, merged[s], s.v) for s in order])


def sbuf(es, nc, name, shape, dtype):
    return es.enter_context(nc.sbuf_tensor(name, shape, dtype)).ap()


def psum(es, nc, name, shape, dtype=F32):
    return es.enter_context(nc.psum_tensor(name, shape, dtype)).ap()


_uid = [0]


def uname(base):
    _uid[0] += 1
    return "%s_%d" % (base, _uid[0])


def emit_ln(ctx, C, x_src, xb, t0, tn, quantize=False, pre=(), ext=None):
    nc = ctx.nc
    dL = ctx.sem("ln_dL")
    sQ = ctx.sem("ln_sQ")
    tS = ctx.sem("ln_tS")
    vS = ctx.sem("ln_vS")
    tB = ctx.sem("ln_tB")
    sB = ctx.sem("ln_sB")
    vN = ctx.sem("ln_vN")
    dL0, sQ0, tS0, vS0, tB0, sB0, vN0 = dL.v, sQ.v, tS.v, vS.v, tB.v, sB.v, vN.v

    NG = tn // 512
    pre = list(pre)
    with contextlib.ExitStack() as les:
        sqws = sbuf(les, nc, uname("ln_sq"), [128, 2 * tn], BF16)
        rows = sbuf(les, nc, uname("ln_rows"), [128, 2 * tn], F32)
        seg_m = rows[0:1, 0:tn]
        seg_r = rows[0:1, tn:2 * tn]
        rstd_b = sbuf(les, nc, uname("ln_rb"), [128, tn], BF16)
        mrstd_b = sbuf(les, nc, uname("ln_mb"), [128, tn], BF16)
        if quantize:
            nws = sbuf(les, nc, uname("ln_nws"), [128, tn], F32)
            qws = sbuf(les, nc, uname("ln_qws"), [128, 2 * tn], F32)

        stats_scope = contextlib.ExitStack()
        if ext is None:
            psM = psum(stats_scope, nc, uname("ln_psM"), [1, tn])
            psS = psum(stats_scope, nc, uname("ln_psS"), [1, tn])
        else:
            psM, psS = ext["psM"], ext["psS"]

        dstep(ctx, "gpsimd",
              lambda g: g.dma_start(
                  xb[:, :].rearrange("p (dc t) -> p dc t", dc=NDC),
                  x_src[:, t0:t0 + tn].rearrange("(dc p) t -> p dc t", p=128)),
              list(pre), dL)
        dL_all = dL0 + 16
        for dc in range(NDC):
            w = [(dL, dL_all)]
            if dc == 0:
                w += pre
            if dc >= 2:
                w.append((tS, tS0 + 2 * (dc - 1)))
            cstep(ctx, "scalar",
                  (lambda dc: lambda s: s.activation(
                      sqws[:, (dc % 2) * tn:(dc % 2 + 1) * tn],
                      xb[:, dc * tn:(dc + 1) * tn], AF.Square))(dc),
                  w, sQ)

            def mmm(dc):
                def f(t):
                    last = None
                    for g in range(NG):
                        last = t.matmul(psM[0:1, g * 512:(g + 1) * 512],
                                        C["ones_bcol"][:, 0:1],
                                        xb[:, dc * tn + g * 512: dc * tn + (g + 1) * 512],
                                        start=(dc == 0), stop=(dc == NDC - 1))
                    return last
                return f

            def smm(dc):
                def f(t):
                    last = None
                    for g in range(NG):
                        last = t.matmul(psS[0:1, g * 512:(g + 1) * 512],
                                        C["ones_bcol"][:, 0:1],
                                        sqws[:, (dc % 2) * tn + g * 512:
                                             (dc % 2) * tn + (g + 1) * 512],
                                        start=(dc == 0), stop=(dc == NDC - 1))
                    return last
                return f
            w = [(dL, dL_all)]
            if dc == 0:
                w += pre
            cstep(ctx, "tensor", mmm(dc), w, tS)
            cstep(ctx, "tensor", smm(dc), [(sQ, sQ0 + dc + 1)], tS)
        p1 = psM[0:1, 0:tn]
        w = [(tS, tS.v)] + pre
        cstep(ctx, "vector", lambda v: v.tensor_scalar(
            seg_m, psM[0:1, :], 1.0 / D, None, ALU.mult), w, vS)
        cstep(ctx, "vector", lambda v: v.tensor_scalar(
            seg_r, psS[0:1, :], 1.0 / D, None, ALU.mult), [], vS)
        cstep(ctx, "vector", lambda v: v.tensor_tensor(
            p1, seg_m, seg_m, ALU.mult), [], vS)
        cstep(ctx, "vector", lambda v: v.tensor_tensor(
            seg_r, seg_r, p1, ALU.subtract), [], vS)
        cstep(ctx, "vector", lambda v: v.tensor_scalar(
            seg_r, seg_r, float(EPS), None, ALU.add), [], vS)
        cstep(ctx, "scalar", lambda s: s.activation(
            p1, seg_r, AF.Sqrt), [(vS, vS.v)], sQ)
        cstep(ctx, "vector", lambda v: v.reciprocal(seg_r, p1),
              [(sQ, sQ.v)], vS)
        cstep(ctx, "vector", lambda v: v.tensor_tensor(
            seg_m, seg_m, seg_r, ALU.mult), [], vS)
        cstep(ctx, "vector", lambda v: v.tensor_copy(
            sqws[0:1, 0:tn], seg_r), [], vS)
        cstep(ctx, "vector", lambda v: v.tensor_copy(
            sqws[0:1, tn:2 * tn], seg_m), [], vS)
        stats_done = vS.v
        stats_scope.close()

        bc_scope = contextlib.ExitStack()
        if ext is None:
            psBc = psum(bc_scope, nc, uname("ln_psBc"), [128, tn])
        else:
            psBc = ext["psBc"]

        def bc(row):
            def f(t):
                last = None
                for g in range(NG):
                    last = t.matmul(psBc[:, g * 512:(g + 1) * 512],
                                    C["ones_bfr"][0:1, 0:128],
                                    row[0:1, g * 512:(g + 1) * 512],
                                    start=True, stop=True)
                return last
            return f
        cstep(ctx, "tensor", bc(sqws[0:1, 0:tn]), [(vS, stats_done)], tB)
        cstep(ctx, "scalar", lambda s: s.copy(rstd_b[:, :], psBc[:, :]),
              [(tB, tB0 + 1)], sB)
        cstep(ctx, "tensor", bc(sqws[0:1, tn:2 * tn]), [(sB, sB0 + 1)], tB)
        cstep(ctx, "scalar", lambda s: s.copy(mrstd_b[:, :], psBc[:, :]),
              [(tB, tB0 + 2)], sB)
        bc_scope.close()
        for dc in range(NDC):
            w = [(sB, sB0 + 2)] if dc == 0 else []
            if not quantize:
                cstep(ctx, "vector",
                      (lambda dc: lambda v: v.tensor_tensor(
                          xb[:, dc * tn:(dc + 1) * tn],
                          xb[:, dc * tn:(dc + 1) * tn], rstd_b[:, :],
                          ALU.mult))(dc),
                      w, vN)
                cstep(ctx, "vector",
                      (lambda dc: lambda v: v.tensor_tensor(
                          xb[:, dc * tn:(dc + 1) * tn],
                          xb[:, dc * tn:(dc + 1) * tn], mrstd_b[:, :],
                          ALU.subtract))(dc),
                      [], vN)
            else:
                cstep(ctx, "vector",
                      (lambda dc: lambda v: v.tensor_tensor(
                          nws[:, :], xb[:, dc * tn:(dc + 1) * tn], rstd_b[:, :],
                          ALU.mult))(dc),
                      w, vN)
                cstep(ctx, "vector",
                      (lambda dc: lambda v: v.tensor_tensor(
                          nws[:, :], nws[:, :], mrstd_b[:, :], ALU.subtract))(dc),
                      [], vN)
                cstep(ctx, "vector",
                      (lambda dc: lambda v: v.tensor_scalar(
                          qws[:, 0:tn], nws[:, :], 0.33, 0.5, ALU.is_gt, ALU.mult))(dc),
                      [], vN)
                cstep(ctx, "vector",
                      (lambda dc: lambda v: v.tensor_scalar(
                          qws[:, tn:2 * tn], nws[:, :], -0.33, 0.5, ALU.is_lt,
                          ALU.mult))(dc),
                      [], vN)
                cstep(ctx, "vector",
                      (lambda dc: lambda v: v.tensor_tensor(
                          xb[:, dc * tn:(dc + 1) * tn], qws[:, 0:tn],
                          qws[:, tn:2 * tn], ALU.subtract))(dc),
                      [], vN)
    return (vN, vN.v)


def emit_residual_out(ctx, es, groups, mk_matmuls, extra_first_waits=()):
    nc = ctx.nc
    pe = ctx.sem("o_pe")
    ev = ctx.sem("o_ev")
    dx = [ctx.sem("o_dx%d" % i) for i in range(4)]
    ds = [ctx.sem("o_ds%d" % i) for i in range(4)]
    pe0, ev0 = pe.v, ev.v
    dx0 = [s.v for s in dx]
    ds0 = [s.v for s in ds]
    xt = sbuf(es, nc, uname("ro_xt"), [128, 4 * 512], F32)
    ot = sbuf(es, nc, uname("ro_ot"), [128, 4 * 512], F32)
    for g, grp in enumerate(groups):
        xi = g % 4
        w = [(ev, ev0 + g - 3)]
        if g < 4:
            w += list(extra_first_waits)
        dstep(ctx, "sync",
              (lambda grp, xi: lambda s: s.dma_start(
                  xt[:, xi * 512:(xi + 1) * 512], grp["x_src_ap"]))(grp, xi),
              w, dx[xi])
        w = [(ev, ev0 + g - 1)]
        if g < 2:
            w += list(extra_first_waits)
        cstep(ctx, "tensor", (lambda g: lambda t: mk_matmuls(g, t))(g), w, pe)
        w = [(pe, pe0 + g + 1), (dx[xi], dx0[xi] + 16 * (g // 4 + 1)),
             (ds[xi], ds0[xi] + 16 * (g // 4))]
        cstep(ctx, "vector",
              (lambda grp, xi: lambda v: v.tensor_tensor(
                  ot[:, xi * 512:(xi + 1) * 512], grp["ps"],
                  xt[:, xi * 512:(xi + 1) * 512], ALU.add))(grp, xi),
              w, ev)
        dstep(ctx, "sync",
              (lambda grp, xi: lambda s: s.dma_start(
                  grp["x_dst_ap"], ot[:, xi * 512:(xi + 1) * 512]))(grp, xi),
              [(ev, ev0 + g + 1)], ds[xi])
    return pe


def phase_join(ctx):
    join = ctx.sem("join")
    for eng in ("sync", "gpsimd", "tensor", "vector", "scalar"):
        waits = []
        if eng == "sync":
            for s in ctx.sems:
                if s is not join and s.v > 0:
                    waits.append((s, s.v))
        cstep(ctx, eng, lambda e: e.nop(), waits, join)
    target = join.v
    for eng in ("sync", "gpsimd", "tensor", "vector", "scalar"):
        ctx.add(eng, lambda e: e.nop(), [(join, target)], [])


def emit_attn(ctx, C, layer, x_src, x_dst, W):
    nc = ctx.nc
    causal = (layer == 0)
    with contextlib.ExitStack() as pes:
        xb = sbuf(pes, nc, uname("at_xn"), [128, NDC * T], BF16)
        zall = sbuf(pes, nc, uname("at_z"), [128, NH * T], BF16)
        q_sb = sbuf(pes, nc, uname("at_q"), [128, T], BF16)
        k_sb = sbuf(pes, nc, uname("at_k"), [128, T], BF16)
        vf_sb = sbuf(pes, nc, uname("at_vf"), [128, T], BF16)
        vt_sb = sbuf(pes, nc, uname("at_vt"), [128, T], BF16)
        wst = sbuf(pes, nc, uname("at_wst"), [128, 2 * 3 * T], BF16)
        dIc = ctx.sem("at_dIc")
        if layer == 0:
            bias_sb = sbuf(pes, nc, uname("at_bias"), [1, 3 * 128], BF16)
            mk = sbuf(pes, nc, uname("at_mask"), [128, 4 * 512], BF16)
            maskT = [mk[:, v * 512:(v + 1) * 512] for v in range(4)]
            for v in range(4):
                dstep(ctx, "sync",
                      (lambda v: lambda s: s.dma_start(
                          mk[:, v * 512:(v + 1) * 512], C["maskT_dram"][v]))(v),
                      [], dIc)
        dIc_done = dIc.v

        ln_done = emit_ln(ctx, C, x_src, xb, 0, T, quantize=False)

        dWp = [ctx.sem("at_dW0"), ctx.sem("at_dW1")]
        dB = ctx.sem("at_dB")
        tP = ctx.sem("at_tP")
        sE = ctx.sem("at_sE")
        vR = ctx.sem("at_vR")
        tT = ctx.sem("at_tT")
        tQ = ctx.sem("at_tQ")
        sX = ctx.sem("at_sX")
        vM = ctx.sem("at_vM")
        tA = ctx.sem("at_tA")
        tZ = ctx.sem("at_tZ")
        vL = ctx.sem("at_vL")
        vZ = ctx.sem("at_vZ")
        dWp0 = [s.v for s in dWp]
        dB0, tP0 = dB.v, tP.v
        evac_done_at = []

        psP = psum(pes, nc, uname("at_psP"), [128, 2048])
        psB2 = psum(pes, nc, uname("at_psB2"), [128, 1024])
        psS = psum(pes, nc, uname("at_psS"), [128, 1024])
        psT = psS[:, 0:512].bitcast(BF16)

        def proj_bank(i):
            j = i % 6
            if j < 4:
                return psP[:, j * 512:(j + 1) * 512]
            return psB2[:, (j - 4) * 512:(j - 3) * 512]

        def st_ps(ti):
            j = ti % 4
            return (psB2 if j < 2 else psS)[:, (j % 2) * 512:((j % 2) + 1) * 512]

        def z_ps(g):
            return psP[:, (g % 2) * 512:((g % 2) + 1) * 512]

        def l_ps(g):
            return psP[0:1, 1024:1536]

        def rl_ps(g):
            return psP[:, 1536:2048]

        with contextlib.ExitStack() as hes:
            att = sbuf(hes, nc, uname("at_att"), [128, 4 * 512], BF16)
            lrow = sbuf(hes, nc, uname("at_lr"), [1, 512], F32)
            lrow_bf = sbuf(hes, nc, uname("at_lrb"), [1, 512], BF16)
            rl_b = sbuf(hes, nc, uname("at_rlb"), [128, 512], F32)
            if layer == 1:
                rws = sbuf(hes, nc, uname("at_rws"), [128, T], BF16)
                rw2 = sbuf(hes, nc, uname("at_rw2"), [32, 2048], BF16)
                cs = sbuf(hes, nc, uname("at_rope"), [64, 2 * T], BF16)
                ropeA = cs[:, 0:T]
                ropeB = cs[:, T:2 * T]
                dstep(ctx, "sync", lambda s: s.dma_start(
                    cs[:, 0:T], C["rope_dram"][0]), [ln_done], dIc)
                dstep(ctx, "sync", lambda s: s.dma_start(
                    cs[:, T:2 * T], C["rope_dram"][1]), [], dIc)
                dIc_done = dIc.v

            nb = 0
            for h in range(NH):
                wbuf = h % 2
                woff = wbuf * 3 * T
                w = [] if h < 2 else [(tP, tP0 + 12 * (h - 1))]
                dstep(ctx, "gpsimd",
                      (lambda h, woff: lambda g: g.dma_start(
                          wst[:, woff: woff + 3 * T], W["wqkv_slab"][h]))(h, woff),
                      w, dWp[h % 2])
                if layer == 0:
                    wb = [(tP, tP0 + 12 * h)]
                    dstep(ctx, "gpsimd",
                          (lambda h: lambda s: s.dma_start(
                              bias_sb[0:1, :], W["bqkv"][h]))(h),
                          wb, dB)

                for j, dst in enumerate((q_sb, k_sb, vf_sb)):
                    for g in range(4):
                        ps = proj_bank(nb)
                        wts = [(dWp[h % 2], dWp0[h % 2] + 16 * (h // 2 + 1))]
                        if nb >= 6:
                            wts.append((sE, evac_done_at[nb - 6]))
                        if j == 0 and g == 0 and h > 0:
                            wts.append((vZ, vZ.v))
                            wts.append((vL, vL.v))
                        if h == 0 and j == 0 and g == 0:
                            wts.append(ln_done)
                        if layer == 0:
                            wts.append((dB, dB0 + 16 * (h + 1)))

                        def mkp(ps, woff, j, g):
                            def f(t):
                                last = None
                                for dc in range(NDC):
                                    last = t.matmul(
                                        ps,
                                        wst[:, woff + j * T + dc * 128:
                                            woff + j * T + (dc + 1) * 128],
                                        xb[:, dc * T + g * 512: dc * T + (g + 1) * 512],
                                        start=(dc == 0),
                                        stop=(dc == NDC - 1 and layer == 1))
                                if layer == 0:
                                    last = t.matmul(
                                        ps, bias_sb[0:1, j * 128:(j + 1) * 128],
                                        C["ones_bfr"][0:1, 0:512],
                                        start=False, stop=True)
                                return last
                            return f
                        cstep(ctx, "tensor", mkp(ps, woff, j, g), wts, tP)
                        if layer == 1 and j < 2:
                            w2_ = [(tP, tP.v)]
                            if g == 0:
                                w2_.append((vR, vR.v))
                            cstep(ctx, "scalar",
                                  (lambda ps, g: lambda s: s.copy(
                                      rws[0:64, g * 512:(g + 1) * 512],
                                      ps[0:64, :]))(ps, g),
                                  w2_, sE)
                            cstep(ctx, "scalar",
                                  (lambda ps, dst, g: lambda s: s.copy(
                                      dst[64:128, g * 512:(g + 1) * 512],
                                      ps[64:128, :]))(ps, dst, g),
                                  [], sE)
                        else:
                            cstep(ctx, "scalar",
                                  (lambda ps, dst, g: lambda s: s.copy(
                                      dst[:, g * 512:(g + 1) * 512], ps[:, :]))(ps, dst, g),
                                  [(tP, tP.v)], sE)
                        evac_done_at.append(sE.v)
                        nb += 1
                    if layer == 1 and j < 2:
                        s0a = rw2[0:32, 0:1024]
                        s0b = rw2[0:32, 1024:2048]
                        for half in range(2):
                            sl = slice(half * 1024, (half + 1) * 1024)
                            w2_ = [(sE, sE.v), (dIc, dIc_done)] if half == 0 else []
                            cstep(ctx, "vector",
                                  (lambda sl: lambda v: v.tensor_tensor(
                                      s0a, rws[0:32, sl], ropeA[0:32, sl],
                                      ALU.mult))(sl),
                                  w2_, vR)
                            cstep(ctx, "vector",
                                  (lambda sl: lambda v: v.tensor_tensor(
                                      s0b, rws[32:64, sl], ropeA[32:64, sl],
                                      ALU.mult))(sl),
                                  [], vR)
                            cstep(ctx, "vector",
                                  (lambda sl, dst: lambda v: v.tensor_tensor(
                                      dst[0:32, sl], s0a, s0b, ALU.subtract))(sl, dst),
                                  [], vR)
                            cstep(ctx, "vector",
                                  (lambda sl: lambda v: v.tensor_tensor(
                                      s0a, rws[0:32, sl], ropeB[0:32, sl],
                                      ALU.mult))(sl),
                                  [], vR)
                            cstep(ctx, "vector",
                                  (lambda sl: lambda v: v.tensor_tensor(
                                      s0b, rws[32:64, sl], ropeB[32:64, sl],
                                      ALU.mult))(sl),
                                  [], vR)
                            cstep(ctx, "vector",
                                  (lambda sl, dst: lambda v: v.tensor_tensor(
                                      dst[32:64, sl], s0b, s0a, ALU.add))(sl, dst),
                                  [], vR)

                for r in range(2):
                    def tr(r):
                        def f(t):
                            last = None
                            for i in range(8):
                                tc = r * 8 + i
                                last = t.transpose(
                                    psT[:, i * 128:(i + 1) * 128],
                                    vf_sb[:, tc * 128:(tc + 1) * 128],
                                    C["ident_bf"][:, :])
                            return last
                        return f
                    w2_ = [(sE, sE.v)] if r == 0 else [(vR, vR.v)]
                    cstep(ctx, "tensor", tr(r), w2_, tT)
                    cstep(ctx, "vector",
                          (lambda r: lambda v: v.tensor_copy(
                              vt_sb[:, r * 1024:(r + 1) * 1024], psT[:, :]))(r),
                          [(tT, tT.v)], vR)

                qk_ready = [(sE, sE.v), (vR, vR.v)]
                tiles = []
                for g in range(4):
                    tc_hi = min(NDC - 1, 4 * g + 3) if causal else NDC - 1
                    tcs = list(range(tc_hi + 1))
                    for ii, tc in enumerate(tcs):
                        tiles.append(dict(g=g, tc=tc, first=(ii == 0),
                                          last=(ii == len(tcs) - 1)))

                def emit_qk(ti):
                    t_ = tiles[ti]
                    sps = st_ps(ti)
                    w2_ = list(qk_ready) if ti < 4 else []
                    w2_.append((sX, sX.v - 3))
                    cstep(ctx, "tensor",
                          (lambda sps, tc, g: lambda t: t.matmul(
                              sps, k_sb[:, tc * 128:(tc + 1) * 128],
                              q_sb[:, g * 512:(g + 1) * 512],
                              start=True, stop=True))(sps, t_["tc"], t_["g"]),
                          w2_, tQ)
                    w2_ = [(tQ, tQ.v), (tA, tA.v - 3)]
                    cstep(ctx, "scalar",
                          (lambda sps, ti: lambda s: s.activation(
                              att[:, (ti % 4) * 512:((ti % 4) + 1) * 512], sps,
                              AF.Exp, bias=np.float32(-SHIFT),
                              scale=np.float32(SCL)))(sps, ti),
                          w2_, sX)
                    t_["dep"] = sX
                    t_["depv"] = sX.v
                    if causal and t_["tc"] >= 4 * t_["g"]:
                        vvar = t_["tc"] - 4 * t_["g"]
                        cstep(ctx, "vector",
                              (lambda ti, vvar: lambda v: v.tensor_tensor(
                                  att[:, (ti % 4) * 512:((ti % 4) + 1) * 512],
                                  att[:, (ti % 4) * 512:((ti % 4) + 1) * 512],
                                  maskT[vvar], ALU.mult))(ti, vvar),
                              [(sX, sX.v), (dIc, dIc_done)], vM)
                        t_["dep"] = vM
                        t_["depv"] = vM.v

                def emit_av(ti):
                    t_ = tiles[ti]
                    g, tc, first, last = t_["g"], t_["tc"], t_["first"], t_["last"]

                    def av(ti, tc, g, first, last):
                        def f(t):
                            t.matmul(l_ps(g), C["ones_bcol"][:, 0:1],
                                     att[:, (ti % 4) * 512:((ti % 4) + 1) * 512],
                                     start=first, stop=last)
                            return t.matmul(
                                z_ps(g), vt_sb[:, tc * 128:(tc + 1) * 128],
                                att[:, (ti % 4) * 512:((ti % 4) + 1) * 512],
                                start=first, stop=last)
                        return f
                    w2_ = [(t_["dep"], t_["depv"])]
                    if first:
                        w2_.append((vZ, vZ.v - 1))
                        w2_.append((vL, vL.v))
                    cstep(ctx, "tensor", av(ti, tc, g, first, last), w2_, tA)
                    if last:
                        cstep(ctx, "vector",
                              (lambda g: lambda v: v.tensor_copy(
                                  lrow[0:1, :], l_ps(g)))(g),
                              [(tA, tA.v)], vL)
                        cstep(ctx, "vector",
                              lambda v: v.reciprocal(lrow[0:1, :], lrow[0:1, :]),
                              [], vL)
                        cstep(ctx, "vector",
                              lambda v: v.tensor_copy(lrow_bf[0:1, :], lrow[0:1, :]),
                              [], vL)
                        cstep(ctx, "tensor",
                              (lambda g: lambda t: t.matmul(
                                  rl_ps(g), C["ones_bfr"][0:1, 0:128],
                                  lrow_bf[0:1, 0:512], start=True, stop=True))(g),
                              [(vL, vL.v), (sE, sE.v)], tZ)
                        cstep(ctx, "scalar",
                              (lambda g: lambda s: s.copy(rl_b[:, :], rl_ps(g)))(g),
                              [(tZ, tZ.v), (vZ, vZ.v)], sE)
                        cstep(ctx, "vector",
                              (lambda g, h: lambda v: v.tensor_tensor(
                                  zall[:, h * T + g * 512: h * T + (g + 1) * 512],
                                  z_ps(g), rl_b[:, :], ALU.mult))(g, h),
                              [(sE, sE.v)], vZ)

                emit_qk(0)
                for ti in range(1, len(tiles)):
                    emit_qk(ti)
                    emit_av(ti - 1)
                emit_av(len(tiles) - 1)
            att_done = [(vZ, vZ.v)]

        with contextlib.ExitStack() as oes:
            dWo = [ctx.sem("at_dWo0"), ctx.sem("at_dWo1")]
            dWo0 = [s.v for s in dWo]
            oPE = ctx.sem("o_pe")
            oPE0 = oPE.v
            wos = sbuf(oes, nc, uname("at_wos"), [128, 2 * T], BF16)
            if layer == 0:
                bo_sb = sbuf(oes, nc, uname("at_bo"), [1, T], BF16)
                dB2 = ctx.sem("at_dBo")
                dstep(ctx, "gpsimd", lambda s: s.dma_start(bo_sb[0:1, :], W["bo"]),
                      list(att_done), dB2)
                bo_target = dB2.v

            groups = []
            for dc in range(NDC):
                w = [(oPE, oPE0 + 4 * (dc - 1))]
                if dc < 2:
                    w += list(att_done)
                dstep(ctx, "gpsimd",
                      (lambda dc: lambda g: g.dma_start(
                          wos[:, (dc % 2) * T:((dc % 2) + 1) * T], W["wo_slab"][dc]))(dc),
                      w, dWo[dc % 2])
                for g in range(4):
                    groups.append(dict(dc=dc, g=g))
            psO = psS
            for gi, grp in enumerate(groups):
                dc, g = grp["dc"], grp["g"]
                grp["ps"] = psO[:, (gi % 2) * 512:((gi % 2) + 1) * 512]
                grp["x_src_ap"] = x_src[dc * 128:(dc + 1) * 128, g * 512:(g + 1) * 512]
                grp["x_dst_ap"] = x_dst[dc * 128:(dc + 1) * 128, g * 512:(g + 1) * 512]

            def mk_out(gi, t):
                dc = groups[gi]["dc"]
                g = groups[gi]["g"]
                ps = groups[gi]["ps"]
                t.wait_ge(dWo[dc % 2].h, dWo0[dc % 2] + 16 * (dc // 2 + 1))
                if layer == 0 and gi == 0:
                    t.wait_ge(dB2.h, bo_target)
                last = None
                for h in range(NH):
                    last = t.matmul(
                        ps, wos[:, (dc % 2) * T + h * 128:(dc % 2) * T + (h + 1) * 128],
                        zall[:, h * T + g * 512: h * T + (g + 1) * 512],
                        start=(h == 0), stop=(h == NH - 1 and layer == 1))
                if layer == 0:
                    last = t.matmul(ps, bo_sb[0:1, dc * 128:(dc + 1) * 128],
                                    C["ones_bfr"][0:1, 0:512], start=False, stop=True)
                return last

            emit_residual_out(ctx, oes, groups, mk_out, extra_first_waits=att_done)


def emit_l0ffn(ctx, C, x_src, x_dst, W):
    nc = ctx.nc
    with contextlib.ExitStack() as pes:
        xb = sbuf(pes, nc, uname("f0_xq"), [128, NDC * T], BF16)
        wraw = sbuf(pes, nc, uname("f0_wr"), [128, 2 * 2 * T], F32)
        wq = sbuf(pes, nc, uname("f0_wq"), [128, 2 * T], BF16)
        qt1 = sbuf(pes, nc, uname("f0_t1"), [128, T], F32)
        qt2 = sbuf(pes, nc, uname("f0_t2"), [128, T], F32)

        ln_done = emit_ln(ctx, C, x_src, xb, 0, T, quantize=True)

        dWf = [ctx.sem("f0_dW0"), ctx.sem("f0_dW1")]
        vQ = ctx.sem("f0_vQ")
        pe = ctx.sem("o_pe")
        ev = ctx.sem("o_ev")
        dx = [ctx.sem("o_dx%d" % i) for i in range(4)]
        ds = [ctx.sem("o_ds%d" % i) for i in range(4)]
        dWf0 = [s.v for s in dWf]
        vQ0, pe0, ev0 = vQ.v, pe.v, ev.v
        dx0 = [s.v for s in dx]
        ds0 = [s.v for s in ds]
        psO = psum(pes, nc, uname("f0_psO"), [128, 1024])
        xt = sbuf(pes, nc, uname("ro_xt"), [128, 4 * 512], F32)
        ot = sbuf(pes, nc, uname("ro_ot"), [128, 4 * 512], F32)

        for oc in range(NDC):
            w = [(vQ, vQ0 + 4 * (oc - 1))]
            dstep(ctx, "gpsimd",
                  (lambda oc: lambda g: g.dma_start(
                      wraw[:, (oc % 2) * 2 * T:(oc % 2) * 2 * T + 2 * T],
                      W["wr_slab"][oc]))(oc),
                  w, dWf[oc % 2])
            wsl = wraw[:, (oc % 2) * 2 * T:(oc % 2) * 2 * T + T]
            rsl = wraw[:, (oc % 2) * 2 * T + T:(oc % 2) * 2 * T + 2 * T]
            qsl = wq[:, (oc % 2) * T:((oc % 2) + 1) * T]
            w = [(dWf[oc % 2], dWf0[oc % 2] + 16 * (oc // 2 + 1)),
                 (pe, pe0 + 4 * (oc - 1))]
            cstep(ctx, "vector",
                  (lambda wsl: lambda v: v.tensor_scalar(
                      qt1[:, :], wsl, 0.2, 0.6, ALU.is_gt, ALU.mult))(wsl), w, vQ)
            cstep(ctx, "vector",
                  (lambda wsl: lambda v: v.tensor_scalar(
                      qt2[:, :], wsl, -0.2, 0.6, ALU.is_lt, ALU.mult))(wsl), [], vQ)
            cstep(ctx, "vector",
                  lambda v: v.tensor_tensor(qt1[:, :], qt1[:, :], qt2[:, :],
                                            ALU.subtract), [], vQ)
            cstep(ctx, "vector",
                  (lambda rsl, qsl: lambda v: v.tensor_tensor(
                      qsl, qt1[:, :], rsl, ALU.add))(rsl, qsl), [], vQ)
            vq_ready = vQ.v
            for g in range(4):
                gi = 4 * oc + g
                xi = gi % 4
                ps = psO[:, (gi % 2) * 512:((gi % 2) + 1) * 512]
                x_src_ap = x_src[oc * 128:(oc + 1) * 128, g * 512:(g + 1) * 512]
                x_dst_ap = x_dst[oc * 128:(oc + 1) * 128, g * 512:(g + 1) * 512]
                w = [(ev, ev0 + gi - 3)]
                if gi < 4:
                    w.append(ln_done)
                dstep(ctx, "sync",
                      (lambda x_src_ap, xi: lambda s: s.dma_start(
                          xt[:, xi * 512:(xi + 1) * 512], x_src_ap))(x_src_ap, xi),
                      w, dx[xi])
                w = [(ev, ev0 + gi - 1), (vQ, vq_ready)]
                if gi == 0:
                    w.append(ln_done)

                def mk(ps, oc, g):
                    def f(t):
                        last = None
                        for dc in range(NDC):
                            last = t.matmul(
                                ps, wq[:, (oc % 2) * T + dc * 128:
                                       (oc % 2) * T + (dc + 1) * 128],
                                xb[:, dc * T + g * 512: dc * T + (g + 1) * 512],
                                start=(dc == 0), stop=(dc == NDC - 1))
                        return last
                    return f
                cstep(ctx, "tensor", mk(ps, oc, g), w, pe)
                w = [(pe, pe0 + gi + 1), (dx[xi], dx0[xi] + 16 * (gi // 4 + 1)),
                     (ds[xi], ds0[xi] + 16 * (gi // 4))]
                cstep(ctx, "vector",
                      (lambda ps, xi: lambda v: v.tensor_tensor(
                          ot[:, xi * 512:(xi + 1) * 512], ps,
                          xt[:, xi * 512:(xi + 1) * 512], ALU.add))(ps, xi),
                      w, ev)
                dstep(ctx, "sync",
                      (lambda x_dst_ap, xi: lambda s: s.dma_start(
                          x_dst_ap, ot[:, xi * 512:(xi + 1) * 512]))(x_dst_ap, xi),
                      [(ev, ev0 + gi + 1)], ds[xi])


def emit_l1ffn(ctx, C, x_src, x_dst, W):
    nc = ctx.nc
    with contextlib.ExitStack() as pes:
        acc = sbuf(pes, nc, uname("f1_acc"), [128, NDC * TG], F32)
        xb = sbuf(pes, nc, uname("f1_xn"), [128, NDC * TG], BF16)
        lg_tm = sbuf(pes, nc, uname("f1_lg"), [128, 8 * 16], F32)
        rr = sbuf(pes, nc, uname("f1_rr"), [128, 16], F32)
        cf = sbuf(pes, nc, uname("f1_cf"), [128, 16], F32)
        mrow = sbuf(pes, nc, uname("f1_mr"), [128, 2], F32)
        ctile = sbuf(pes, nc, uname("f1_ct"), [16, 8 * 128], F32)
        srow = sbuf(pes, nc, uname("f1_sr"), [1, TG], F32)
        gws = sbuf(pes, nc, uname("f1_gws"), [128, NDC * 16], BF16)
        sgs = sbuf(pes, nc, uname("f1_sgs"), [128, NDC], BF16)

        coef_dram = nc.dram_tensor(uname("coef_dram"), [17, TG], F32,
                                   kind="Internal").ap()

        dG = ctx.sem("f1_dG")
        dWsp = [ctx.sem("f1_dWs0"), ctx.sem("f1_dWs1")]
        dW2 = ctx.sem("f1_dW2")
        dCB = ctx.sem("f1_dCB")
        dCS = ctx.sem("f1_dCS")
        tR = ctx.sem("f1_tR")
        vRt = ctx.sem("f1_vRt")
        sR = ctx.sem("f1_sR")
        tU = ctx.sem("f1_tU")
        sU = ctx.sem("f1_sU")
        vU = ctx.sem("f1_vU")
        tD = ctx.sem("f1_tD")
        vD = ctx.sem("f1_vD")
        oEV = ctx.sem("o_ev")
        oDXs = [ctx.sem("o_dx%d" % i) for i in range(4)]
        oDSs = [ctx.sem("o_ds%d" % i) for i in range(4)]

        psU = psum(pes, nc, uname("f1_psU"), [128, 2 * TG])
        psD = psum(pes, nc, uname("f1_psD"), [128, 2 * TG])

        dstep(ctx, "gpsimd", lambda g: g.dma_start(gws[:, :], W["gwT"]), [], dG)
        dstep(ctx, "gpsimd", lambda g: g.dma_start(sgs[:, :], W["sgT"][:, :]),
              [], dG)
        dG_done = dG.v

        units = []
        for e in range(E_MOE):
            units.append(dict(kind="exp", e=e, gate_row=e))
        for s in range(4):
            units.append(dict(kind="sh", e=s, gate_row=16))

        for tgi in range(TG_PASSES):
            t0 = tgi * TG
            pre = [(tU, tU.v), (tD, tD.v), (vD, vD.v), (vU, vU.v), (sU, sU.v),
                   (oEV, oEV.v), (dCB, dCB.v)] + [(s, s.v) for s in oDSs]
            ln_done = emit_ln(ctx, C, x_src, xb, t0, TG, quantize=False, pre=pre,
                              ext=dict(psM=psU[0:1, 0:TG], psS=psU[0:1, TG:2 * TG],
                                       psBc=psD[:, 0:TG]))

            def rmm(t):
                last = None
                for g in range(TG // 512):
                    for dc in range(NDC):
                        last = t.matmul(
                            psU[0:16, g * 512:(g + 1) * 512],
                            gws[:, dc * 16:(dc + 1) * 16],
                            xb[:, dc * TG + g * 512: dc * TG + (g + 1) * 512],
                            start=(dc == 0), stop=(dc == NDC - 1))
                return last
            cstep(ctx, "tensor", rmm, [ln_done, (dG, dG_done)], tR)

            def smm(t):
                last = None
                for g in range(TG // 512):
                    for dc in range(NDC):
                        last = t.matmul(
                            psU[0:1, TG + g * 512: TG + (g + 1) * 512],
                            sgs[:, dc:dc + 1],
                            xb[:, dc * TG + g * 512: dc * TG + (g + 1) * 512],
                            start=(dc == 0), stop=(dc == NDC - 1))
                return last
            cstep(ctx, "tensor", smm, [], tR)
            cstep(ctx, "scalar", lambda s: s.activation(
                srow[0:1, :], psU[0:1, TG:2 * TG], AF.Sigmoid), [(tR, tR.v)], sR)
            dstep(ctx, "sync",
                  lambda s: s.dma_start(coef_dram[16:17, :], srow[0:1, :]),
                  [(sR, sR.v), (dCB, dCB.v)] + pre, dCS)
            cstep(ctx, "vector", lambda v: v.tensor_copy(
                ctile[0:16, :], psU[0:16, 0:TG]), [(sR, sR.v)], vRt)
            for tc in range(8):
                cstep(ctx, "tensor",
                      (lambda tc: lambda t: t.transpose(
                          psD[0:128, 0:16], ctile[0:16, tc * 128:(tc + 1) * 128],
                          C["ident_f32"][0:16, 0:16]))(tc),
                      [(vRt, vRt.v)], tR)
                cstep(ctx, "vector",
                      (lambda tc: lambda v: v.tensor_copy(
                          lg_tm[:, tc * 16:(tc + 1) * 16], psD[0:128, 0:16]))(tc),
                      [(tR, tR.v)], vRt)
            for tc in range(8):
                sl = slice(tc * 16, (tc + 1) * 16)
                cstep(ctx, "vector",
                      (lambda sl: lambda v: v.reduce_max(
                          mrow[:, 0:1], lg_tm[:, sl], axis=AX.X, negate=True))(sl),
                      [], vRt)
                cstep(ctx, "scalar",
                      (lambda sl: lambda s: s.activation(
                          rr[:, :], lg_tm[:, sl], AF.Exp, bias=mrow[:, 0:1],
                          accum_out=mrow[:, 1:2]))(sl),
                      [(vRt, vRt.v)], sR)
                cstep(ctx, "vector",
                      lambda v: v.reciprocal(mrow[:, 1:2], mrow[:, 1:2]),
                      [(sR, sR.v)], vRt)
                cstep(ctx, "vector", lambda v: v.tensor_single_scalar(
                    rr[:, :], rr[:, :], mrow[:, 1:2], ALU.mult), [], vRt)
                cstep(ctx, "vector", lambda v: v.memset(cf[:, :], 0.0), [], vRt)
                for r in range(TOPK):
                    cstep(ctx, "vector", lambda v: v.reduce_max(
                        mrow[:, 0:1], rr[:, :], axis=AX.X), [], vRt)
                    cstep(ctx, "vector", lambda v: v.tensor_single_scalar(
                        lg_tm[:, 0:16], rr[:, :], mrow[:, 0:1], ALU.is_ge), [], vRt)
                    cstep(ctx, "vector", lambda v: v.tensor_tensor(
                        lg_tm[:, 0:16], lg_tm[:, 0:16], rr[:, :], ALU.mult), [], vRt)
                    cstep(ctx, "vector", lambda v: v.tensor_tensor(
                        cf[:, :], cf[:, :], lg_tm[:, 0:16], ALU.add), [], vRt)
                    cstep(ctx, "vector", lambda v: v.tensor_tensor(
                        rr[:, :], rr[:, :], lg_tm[:, 0:16], ALU.subtract), [], vRt)
                cstep(ctx, "tensor",
                      (lambda tc: lambda t: t.transpose(
                          psD[0:16, 0:128], cf[:, 0:16], C["ident_f32"][:, :]))(tc),
                      [(vRt, vRt.v)], tR)
                cstep(ctx, "vector",
                      (lambda tc: lambda v: v.tensor_copy(
                          ctile[0:16, tc * 128:(tc + 1) * 128], psD[0:16, 0:128]))(tc),
                      [(tR, tR.v)], vRt)
            dstep(ctx, "sync",
                  lambda s: s.dma_start(coef_dram[0:16, :], ctile[0:16, :]),
                  [(vRt, vRt.v)], dCS)
            for i in range(NDC):
                w = [(oEV, oEV.v)] if i == 0 else []
                cstep(ctx, "vector",
                      (lambda i: lambda v: v.memset(acc[:, i * TG:(i + 1) * TG],
                                                    0.0))(i),
                      w, vRt)
            acc_zeroed = vRt.v
            coef_stored = dCS.v

            with contextlib.ExitStack() as ues:
                he = sbuf(ues, nc, uname("f1_h"), [128, NIC * TG], BF16)
                w2s = sbuf(ues, nc, uname("f1_w2"), [128, NIC * 2048], BF16)
                w13s = sbuf(ues, nc, uname("f1_w13"), [128, 4 * NDC * 128], BF16)
                coefb = sbuf(ues, nc, uname("f1_cb"), [128, TG], F32)
                sg = sbuf(ues, nc, uname("f1_sg"), [128, 2 * TG], F32)

                tU0u, tD0u, vD0u, vU0u = tU.v, tD.v, vD.v, vU.v
                pairno = [0]
                sU_after = []
                vU_after = []
                for ui, u in enumerate(units):
                    dstep(ctx, "sync",
                          (lambda u: lambda s: s.dma_start(
                              coefb[:, :],
                              coef_dram[u["gate_row"]:u["gate_row"] + 1, :]
                              .partition_broadcast(128)))(u),
                          [(dCS, coef_stored), (vU, vU.v)], dCB)

                    def pair_load(p):
                        pn = ui * NIC + p
                        pb = pn % 2
                        w = [(tU, tU0u + pn - 1)]
                        if pn == 0:
                            w.append(ln_done)
                        src = (W["w13gu_sl"][u["e"]][p] if u["kind"] == "exp"
                               else W["sgugu_sl"][u["e"]][p])
                        dstep(ctx, "gpsimd",
                              (lambda src, pb: lambda g: g.dma_start(
                                  w13s[:, 2 * pb * NDC * 128:
                                       (2 * pb + 2) * NDC * 128], src))(src, pb),
                              w, dWsp[pb])
                    pair_load(0)
                    pair_load(1)
                    w = [(tD, tD0u + 32 * ui)]
                    if ui == 0:
                        w.append(ln_done)
                    src_w2 = (W["w2_unit"][u["e"]] if u["kind"] == "exp"
                              else W["sdw_unit"][u["e"]])
                    dstep(ctx, "gpsimd",
                          (lambda src_w2: lambda g: g.dma_start(
                              w2s[:, :].rearrange("p (ic o) -> p ic o", ic=NIC),
                              src_w2))(src_w2),
                          w, dW2)
                    w2_done = dW2.v
                    for p in range(NIC):
                        pn = ui * NIC + p
                        pb = pn % 2
                        if p >= 2:
                            pair_load(p)
                        psGU = psU if pb == 0 else psD

                        def upmm(pb, psGU):
                            def f(t):
                                last = None
                                for j in range(2):
                                    ps = psGU[:, j * TG:(j + 1) * TG]
                                    for dc in range(NDC):
                                        for g5 in range(TG // 512):
                                            last = t.matmul(
                                                ps[:, g5 * 512:(g5 + 1) * 512],
                                                w13s[:, (2 * pb + j) * NDC * 128 + dc * 128:
                                                     (2 * pb + j) * NDC * 128 + (dc + 1) * 128],
                                                xb[:, dc * TG + g5 * 512:
                                                   dc * TG + (g5 + 1) * 512],
                                                start=(dc == 0), stop=(dc == NDC - 1))
                                return last
                            return f
                        w = [(dWsp[pb], dWsp[pb].v)]
                        if pn >= 2:
                            w.append((sU, sU_after[pn - 2]))
                            w.append((vU, vU_after[pn - 2]))
                        if pn == 0:
                            w.append(ln_done)
                        if pb == 1 and p <= 1:
                            w.append((vD, vD.v))
                        if ui == 0 and p <= 1:
                            w.append((vRt, acc_zeroed))
                            w.append((sR, sR.v))
                        cstep(ctx, "tensor", upmm(pb, psGU), w, tU)
                        cstep(ctx, "scalar",
                              (lambda psGU, pb: lambda s: s.activation(
                                  sg[:, pb * TG:(pb + 1) * TG], psGU[:, 0:TG],
                                  AF.Silu))(psGU, pb),
                              [(tU, tU.v)], sU)
                        sU_after.append(sU.v)
                        w = [(sU, sU.v), (dCB, dCB.v)]
                        if p == 0:
                            w.append((tD, tD0u + 32 * ui))
                        cstep(ctx, "vector",
                              (lambda p, psGU, pb: lambda v: v.tensor_tensor(
                                  he[:, p * TG:(p + 1) * TG],
                                  sg[:, pb * TG:(pb + 1) * TG],
                                  psGU[:, TG:2 * TG], ALU.mult))(p, psGU, pb),
                              w, vU)
                        cstep(ctx, "vector",
                              (lambda p: lambda v: v.tensor_tensor(
                                  he[:, p * TG:(p + 1) * TG],
                                  he[:, p * TG:(p + 1) * TG],
                                  coefb[:, :], ALU.mult))(p),
                              [], vU)
                        vU_after.append(vU.v)
                    for oc in range(NDC):
                        ri = oc % 2

                        def dnmm(ri, oc):
                            def f(t):
                                last = None
                                for g5 in range(TG // 512):
                                    for ic in range(NIC):
                                        last = t.matmul(
                                            psD[:, ri * TG + g5 * 512:
                                                ri * TG + (g5 + 1) * 512],
                                            w2s[:, ic * 2048 + oc * 128:
                                                ic * 2048 + (oc + 1) * 128],
                                            he[:, ic * TG + g5 * 512:
                                               ic * TG + (g5 + 1) * 512],
                                            start=(ic == 0), stop=(ic == NIC - 1))
                                return last
                            return f
                        w = [(dW2, w2_done), (vD, vD.v - 1)]
                        if oc == 0:
                            w.append((vU, vU.v))
                        cstep(ctx, "tensor", dnmm(ri, oc), w, tD, tD)
                        w = [(tD, tD.v)]
                        if ui == 0 and oc == 0:
                            w.append((vRt, acc_zeroed))
                        cstep(ctx, "vector",
                              (lambda ri, oc: lambda v: v.tensor_tensor(
                                  acc[:, oc * TG:(oc + 1) * TG],
                                  acc[:, oc * TG:(oc + 1) * TG],
                                  psD[:, ri * TG:(ri + 1) * TG], ALU.add))(ri, oc),
                              w, vD)
                units_done = (tD.v, vD.v)

            with contextlib.ExitStack() as fes:
                xt = sbuf(fes, nc, uname("f1_xt"), [128, 4 * 512], F32)
                ot = sbuf(fes, nc, uname("f1_ot"), [128, 4 * 512], F32)
                oEVb = oEV.v
                oDXb = [s.v for s in oDXs]
                oDSb = [s.v for s in oDSs]
                groups = []
                for oc in range(NDC):
                    for g in range(TG // 512):
                        groups.append((oc, g))
                for gi, (oc, g) in enumerate(groups):
                    xi = gi % 4
                    w = [(oEV, oEVb + gi - 3)]
                    if gi < 4:
                        w.append((tD, units_done[0]))
                    dstep(ctx, "sync",
                          (lambda oc, g, xi, t0: lambda s: s.dma_start(
                              xt[:, xi * 512:(xi + 1) * 512],
                              x_src[oc * 128:(oc + 1) * 128,
                                    t0 + g * 512: t0 + (g + 1) * 512]))(oc, g, xi, t0),
                          w, oDXs[xi])
                    w = [(oDXs[xi], oDXb[xi] + 16 * (gi // 4 + 1)), (vD, units_done[1]),
                         (oDSs[xi], oDSb[xi] + 16 * (gi // 4))]
                    cstep(ctx, "vector",
                          (lambda oc, g, xi: lambda v: v.tensor_tensor(
                              ot[:, xi * 512:(xi + 1) * 512],
                              acc[:, oc * TG + g * 512: oc * TG + (g + 1) * 512],
                              xt[:, xi * 512:(xi + 1) * 512], ALU.add))(oc, g, xi),
                          w, oEV)
                    dstep(ctx, "sync",
                          (lambda oc, g, xi, t0: lambda s: s.dma_start(
                              x_dst[oc * 128:(oc + 1) * 128,
                                    t0 + g * 512: t0 + (g + 1) * 512],
                              ot[:, xi * 512:(xi + 1) * 512]))(oc, g, xi, t0),
                          [(oEV, oEVb + gi + 1)], oDSs[xi])


def declare_consts(ctx, es, layer_set):
    nc = ctx.nc
    C = {}
    C["ones_bcol"] = sbuf(es, nc, "c_ones_b", [128, 1], BF16)
    C["ones_bfr"] = sbuf(es, nc, "c_ones_br", [1, 512], BF16)
    C["ident_bf"] = sbuf(es, nc, "c_id_bf", [128, 128], BF16)
    C["ident_f32"] = sbuf(es, nc, "c_id_f", [128, 128], F32)
    ident_bf = nc.dram_tensor("ident_bf", [128, 128], BF16, kind="ExternalInput").ap()
    ident_f = nc.dram_tensor("ident_f", [128, 128], F32, kind="ExternalInput").ap()
    dI = ctx.sem("c_dI")
    cM = ctx.sem("c_cM")
    dstep(ctx, "sync", lambda s: s.dma_start(C["ident_bf"][:, :], ident_bf[:, :]),
          [], dI)
    dstep(ctx, "sync", lambda s: s.dma_start(C["ident_f32"][:, :], ident_f[:, :]),
          [], dI)
    if "l0attn" in layer_set:
        C["maskT_dram"] = nc.dram_tensor("maskT", [4, 128, 512], BF16,
                                         kind="ExternalInput").ap()
    if "l1attn" in layer_set:
        C["rope_dram"] = nc.dram_tensor("rope", [2, 64, T], BF16,
                                        kind="ExternalInput").ap()
    cstep(ctx, "vector", lambda v: v.memset(C["ones_bcol"][:, :], 1.0),
          [(dI, dI.v)], cM)
    cstep(ctx, "vector", lambda v: v.memset(C["ones_bfr"][:, :], 1.0), [], cM)
    ctx.add("tensor", lambda t: t.nop(), [(cM, cM.v), (dI, dI.v)], [])
    ctx.add("scalar", lambda t: t.nop(), [(cM, cM.v), (dI, dI.v)], [])
    ctx.add("vector", lambda t: t.nop(), [(dI, dI.v)], [])
    ctx.add("gpsimd", lambda t: t.nop(), [(cM, cM.v), (dI, dI.v)], [])
    return C


def declare_weight_aps(nc, phases):
    W = {}
    if "l0attn" in phases:
        wqkv = nc.dram_tensor("a0_wqkv", [NH, 128, 3 * D], F32,
                              kind="ExternalInput").ap()
        wo = nc.dram_tensor("a0_wo", [NDC, 128, D], F32, kind="ExternalInput").ap()
        bqkv = nc.dram_tensor("a0_b", [NH, 1, 3 * 128], F32, kind="ExternalInput").ap()
        bo = nc.dram_tensor("a0_bo", [1, D], F32, kind="ExternalInput").ap()
        W["l0attn"] = dict(
            wqkv_slab=[wqkv[h] for h in range(NH)],
            wo_slab=[wo[dc] for dc in range(NDC)],
            bqkv=[bqkv[h] for h in range(NH)],
            bo=bo[0:1, :],
        )
    if "l0ffn" in phases:
        wr = nc.dram_tensor("f0_wr", [NDC, 128, 2 * D], F32, kind="ExternalInput").ap()
        W["l0ffn"] = dict(wr_slab=[wr[oc] for oc in range(NDC)])
    if "l1attn" in phases:
        wqkv = nc.dram_tensor("a1_wqkv", [NH, 128, 3 * D], F32,
                              kind="ExternalInput").ap()
        wo = nc.dram_tensor("a1_woT", [NDC, 128, D], F32, kind="ExternalInput").ap()
        W["l1attn"] = dict(
            wqkv_slab=[wqkv[h] for h in range(NH)],
            wo_slab=[wo[dc] for dc in range(NDC)],
        )
    if "l1ffn" in phases:
        gwT = nc.dram_tensor("f1_gwT", [128, NDC * 16], F32, kind="ExternalInput").ap()
        sgT = nc.dram_tensor("f1_sgT", [128, NDC], F32, kind="ExternalInput").ap()
        w13 = nc.dram_tensor("f1_w13gu", [E_MOE, NIC, 128, 2 * D], F32,
                             kind="ExternalInput").ap()
        w2T = nc.dram_tensor("f1_w2T", [E_MOE, IE, D], F32, kind="ExternalInput").ap()
        sgu2 = nc.dram_tensor("f1_sgugu", [4, NIC, 128, 2 * D], F32,
                              kind="ExternalInput").ap()
        sdwT = nc.dram_tensor("f1_sdwT", [ISH, D], F32, kind="ExternalInput").ap()
        W["l1ffn"] = dict(
            gwT=gwT[:, :],
            sgT=sgT[:, :],
            w13gu_sl=[[w13[e][p] for p in range(NIC)] for e in range(E_MOE)],
            w2_unit=[w2T[e].rearrange("(ic p) o -> p ic o", p=128)
                     for e in range(E_MOE)],
            sgugu_sl=[[sgu2[s][p] for p in range(NIC)] for s in range(4)],
            sdw_unit=[sdwT[s * IE:(s + 1) * IE, :].rearrange("(ic p) o -> p ic o", p=128)
                      for s in range(4)],
        )
    return W


PHASE_FNS = dict(
    l0attn=lambda ctx, C, xs, xd, W: emit_attn(ctx, C, 0, xs, xd, W),
    l0ffn=emit_l0ffn,
    l1attn=lambda ctx, C, xs, xd, W: emit_attn(ctx, C, 1, xs, xd, W),
    l1ffn=emit_l1ffn,
)


DEBUG_MIDS = True


def build_program(phases):
    nc = bass.Bass("TRN2", target_bir_lowering=False)
    x_in = nc.dram_tensor("x_in", [D, T], F32, kind="ExternalInput").ap()
    x_out = nc.dram_tensor("x_out", [D, T], F32, kind="ExternalOutput").ap()
    xr = [x_in]
    for i in range(len(phases) - 1):
        xr.append(nc.dram_tensor("x_mid%d" % i, [D, T], F32,
                                 kind="ExternalOutput" if DEBUG_MIDS else "Internal").ap())
    xr.append(x_out)
    W = declare_weight_aps(nc, phases)
    with contextlib.ExitStack() as es:
        ctx = Ctx(nc, es)
        C = declare_consts(ctx, es, set(phases))
        for i, ph in enumerate(phases):
            PHASE_FNS[ph](ctx, C, xr[i], xr[i + 1], W[ph])
            phase_join(ctx)
        ctx.emit()
    return nc


def host_consts():
    ident = np.eye(128)
    maskT = np.zeros((4, 128, 512), dtype=ml_dtypes.bfloat16)
    tkl = np.arange(128)[:, None]
    tql = np.arange(512)[None, :]
    for v in range(4):
        maskT[v] = (tkl <= tql - 128 * v).astype(ml_dtypes.bfloat16)
    inv_freq = 1.0 / (10000.0 ** (np.arange(0, 128, 2, dtype=np.float64) / 128.0))
    t = np.arange(T, dtype=np.float64)
    ang = np.outer(inv_freq, t)
    sin, cos = np.sin(ang), np.cos(ang)
    ropeA = np.concatenate([cos[0:32], sin[0:32]], axis=0)
    ropeB = np.concatenate([sin[32:64], cos[32:64]], axis=0)
    rope = np.stack([ropeA, ropeB]).astype(ml_dtypes.bfloat16)
    return dict(ident_bf=ident.astype(ml_dtypes.bfloat16),
                ident_f=ident.astype(np.float32),
                maskT=maskT, rope=rope)


def host_weight_maps(inputs, phases):
    m = {}

    def tile_in_out(w_out_in, n_out_grp):
        O, I = w_out_in.shape
        G = O // n_out_grp if False else None
        return None

    if "l0attn" in phases:
        wq = inputs["L0_attn_W_Q"]
        wk = inputs["L0_attn_W_K"]
        wv = inputs["L0_attn_W_V"]
        wo = inputs["L0_attn_W_O"]
        def qkv(w):
            return w.reshape(NH, NDC, 128, HD).transpose(0, 2, 1, 3).reshape(NH, 128, D)
        m["a0_wqkv"] = np.ascontiguousarray(
            np.concatenate([qkv(wq), qkv(wk), qkv(wv)], axis=2))
        m["a0_wo"] = np.ascontiguousarray(
            wo.reshape(NH, HD, NDC, 128).transpose(2, 1, 0, 3).reshape(NDC, 128, D))
        m["a0_b"] = np.ascontiguousarray(
            np.stack([inputs["L0_attn_b_Q"], inputs["L0_attn_b_K"],
                      inputs["L0_attn_b_V"]], axis=1).reshape(NH, 1, 3 * 128))
        m["a0_bo"] = np.ascontiguousarray(inputs["L0_attn_b_O"].reshape(1, D))
    if "l0ffn" in phases:
        def outin(w):
            return w.reshape(NDC, 128, NDC, 128).transpose(0, 3, 2, 1).reshape(NDC, 128, D)
        m["f0_wr"] = np.ascontiguousarray(
            np.concatenate([outin(inputs["L0_ffn_weight"]),
                            outin(inputs["L0_ffn_rram_variation"])], axis=2))
    if "l1attn" in phases:
        def heads(w):
            return w.reshape(NH, 128, NDC, 128).transpose(0, 3, 2, 1).reshape(NH, 128, D)
        m["a1_wqkv"] = np.ascontiguousarray(
            np.concatenate([heads(inputs["L1_attn_wq"]),
                            heads(inputs["L1_attn_wk"]),
                            heads(inputs["L1_attn_wv"])], axis=2))
        m["a1_woT"] = np.ascontiguousarray(
            inputs["L1_attn_wo"].reshape(NDC, 128, NH, 128)
            .transpose(0, 3, 2, 1).reshape(NDC, 128, D))
    if "l1ffn" in phases:
        gw = inputs["L1_ffn_gate_w"]
        m["f1_gwT"] = np.ascontiguousarray(
            gw.reshape(16, NDC, 128).transpose(2, 1, 0).reshape(128, NDC * 16))
        m["f1_sgT"] = np.ascontiguousarray(
            inputs["L1_ffn_sgate_w"].reshape(NDC, 128).T)
        w13 = inputs["L1_ffn_w13"]
        def uptile(w3):
            E = w3.shape[0]
            return w3.reshape(E, NIC, 128, NDC, 128).transpose(0, 1, 4, 3, 2) \
                .reshape(E, NIC, 128, D)
        m["f1_w13gu"] = np.ascontiguousarray(
            np.concatenate([uptile(w13[:, 0:IE, :]),
                            uptile(w13[:, IE:2 * IE, :])], axis=3))
        m["f1_w2T"] = np.ascontiguousarray(inputs["L1_ffn_w2"].transpose(0, 2, 1))
        sgu = inputs["L1_ffn_shared_gate_up_w"]
        def uptile1(w3):
            return w3.reshape(4, NIC, 128, NDC, 128).transpose(0, 1, 4, 3, 2) \
                .reshape(4, NIC, 128, D)
        m["f1_sgugu"] = np.ascontiguousarray(
            np.concatenate([uptile1(sgu[0:ISH]), uptile1(sgu[ISH:2 * ISH])], axis=3))
        m["f1_sdwT"] = np.ascontiguousarray(inputs["L1_ffn_shared_down_w"].T)
    return m


def const_maps(phases):
    hc = host_consts()
    m = dict(ident_bf=hc["ident_bf"], ident_f=hc["ident_f"])
    if "l0attn" in phases:
        m["maskT"] = hc["maskT"]
    if "l1attn" in phases:
        m["rope"] = hc["rope"]
    return m


def run_phases(x_percore, inputs, phases):
    nc = build_program(phases)
    wm = host_weight_maps(inputs, phases)
    cm = const_maps(phases)
    in_maps = []
    for c in range(8):
        d = dict(x_in=x_percore[c])
        d.update(wm)
        d.update(cm)
        in_maps.append(d)
    res = run_bass_kernel_spmd(nc, in_maps, core_ids=list(range(8)))
    return [res.results[c]["x_out"] for c in range(8)]


PHASE_ORDER = ["l0attn", "l0ffn", "l1attn", "l1ffn"]
FUSED = True


def kernel(**inputs):
    x = inputs["L0_attn_x"]
    x_percore = [np.ascontiguousarray(x[c].T) for c in range(8)]
    if FUSED:
        x_percore = run_phases(x_percore, inputs, PHASE_ORDER)
    else:
        for ph in PHASE_ORDER:
            x_percore = run_phases(x_percore, inputs, [ph])
    out = np.stack([np.ascontiguousarray(xc.T) for xc in x_percore])
    return np.ascontiguousarray(out.astype(np.float32))
